# version 02
# baseline (speedup 1.0000x reference)
import os
import numpy as np
import ml_dtypes
from contextlib import ExitStack

import concourse.bass as bass
import concourse.mybir as mybir
from concourse.bass_utils import run_bass_kernel_spmd

F32 = mybir.dt.float32
BF16 = mybir.dt.bfloat16
I32 = mybir.dt.int32
AF = mybir.ActivationFunctionType
ALU = mybir.AluOpType
AXX = mybir.AxisListType.X

D = 1024
TOK = 2048
T = 4096
NCH = D // 128
HID = 2816
NEXP = 8
NH = 8
DH = 64
EPS = 1e-5
BIG = 1.0e6
NEGINF = -1.0e30
SCALE0 = float(np.float32(1024.0 ** (-0.25)))
SCALE1 = float(1.0 / np.sqrt(96.0))

SUBLAYERS = ["moe0", "attn0", "moe1", "ffn1a", "attn1", "ffn1b"]


class Prog:
    def __init__(self, nc, ctx):
        self.nc = nc
        self.ctx = ctx
        self._sems = {}
        self._cnt = {}
        self._guard = {}

    def sem(self, name):
        h = self.ctx.enter_context(self.nc.semaphore(name))
        self._sems[name] = h
        self._cnt[name] = 0
        return h

    def inc(self, instr, name, n):
        self._cnt[name] += n
        instr.then_inc(self._sems[name], n)
        return self._cnt[name]

    def mile(self, eng, name):
        self._cnt[name] += 1
        eng.drain().then_inc(self._sems[name], 1)
        return self._cnt[name]

    def cnt(self, name):
        return self._cnt[name]

    def wait(self, eng, name, val=None):
        v = self._cnt[name] if val is None else val
        if v > 0:
            eng.wait_ge(self._sems[name], v)

    def fence(self, eng):
        for nm in ("sP", "sV", "sA", "dS", "dG"):
            self.wait(eng, nm)

    def guard_set(self, res, semname, val=None):
        v = self._cnt[semname] if val is None else val
        self._guard[res] = [(semname, v)]

    def guard_wait(self, eng, res):
        for semname, v in self._guard.pop(res, []):
            if v > 0:
                eng.wait_ge(self._sems[semname], v)



def emit_preamble(p, io, sb):
    nc = p.nc
    s = nc.sync
    g = nc.gpsimd
    p.inc(s.dma_start(out=sb["x"][:].rearrange("p (c t) -> p c t", c=NCH),
                      in_=io["x_in"].rearrange("(c p) t -> p c t", p=128)), "dS", 16)
    for nm in ("ones1", "ones_row", "eye_f", "iotaB", "sel8", "par", "nonces"):
        if nm in sb:
            p.inc(s.dma_start(out=sb[nm][:], in_=io[nm]), "dS", 16)
    p.inc(g.dma_start(out=sb["eye_b"][:], in_=io["eye_f"]), "dG", 16)


def emit_ln(p, sb, ps, to_dram=None, stats_only=False):

    nc = p.nc
    t, v, a = nc.tensor, nc.vector, nc.scalar
    x = sb["x"]
    p.wait(a, "dS")
    p.wait(t, "dS")
    p.wait(v, "dS")
    SLOC = [(ps["A"], 0), (ps["A"], 32), (ps["A"], 64), (ps["C"], 0)]
    SLOC2 = [(ps["B"], 0), (ps["B"], 32), (ps["B"], 64), (ps["F"], 0)]
    for nm in ("psA", "psB", "psC", "psF"):
        p.guard_wait(t, nm)
    for tg in range(4):
        bA, base = SLOC[tg]
        bB, base2 = SLOC2[tg]
        for c in range(NCH):
            xs = x[:, c * TOK + tg * 512: c * TOK + (tg + 1) * 512]
            slot = c % 2
            sqb = sb["tmpA"] if slot == 0 else sb["tmpB"]
            p.guard_wait(a, ("sq", slot))
            a.activation(sqb[:, 0:512], xs, AF.Square)
            p.mile(a, "sA")
            p.wait(t, "sA")
            t.matmul(bA[base:base + 1, :], sb["ones1"][:], xs,
                     start=(c == 0), stop=(c == NCH - 1))
            mi = t.matmul(bB[base2:base2 + 1, :], sb["ones1"][:],
                          sqb[:, 0:512],
                          start=(c == 0), stop=(c == NCH - 1))
            p.inc(mi, "sP", 1)
            p.guard_set(("sq", slot), "sP")
    p.wait(v, "sP")
    for tg in range(4):
        bA, base = SLOC[tg]
        bB, base2 = SLOC2[tg]
        sl = slice(tg * 512, (tg + 1) * 512)
        v.tensor_scalar_mul(sb["row_m"][0:1, sl], bA[base:base + 1, :], 1.0 / D)
        v.tensor_scalar_mul(sb["row_t"][0:1, sl], bB[base2:base2 + 1, :], 1.0 / D)
    p.mile(v, "sV")
    for nm in ("psA", "psB", "psC", "psF"):
        p.guard_set(nm, "sV")
    v.tensor_tensor(sb["row_r"][0:1, :], sb["row_m"][0:1, :], sb["row_m"][0:1, :], op=ALU.mult)
    v.drain()
    v.tensor_tensor(sb["row_t"][0:1, :], sb["row_t"][0:1, :], sb["row_r"][0:1, :], op=ALU.subtract)
    v.drain()
    v.tensor_scalar_add(sb["row_t"][0:1, :], sb["row_t"][0:1, :], EPS)
    p.mile(v, "sV")
    p.wait(a, "sV")
    a.activation(sb["row_mr"][0:1, :], sb["row_t"][0:1, :], AF.Ln)
    a.drain()
    a.activation(sb["row_r"][0:1, :], sb["row_mr"][0:1, :], AF.Exp, scale=-0.5)
    p.mile(a, "sA")
    p.wait(v, "sA")
    v.tensor_tensor(sb["row_mr"][0:1, :], sb["row_m"][0:1, :], sb["row_r"][0:1, :], op=ALU.mult)
    p.mile(v, "sV")
    p.wait(t, "sV")
    mrp = ps["C"]
    p.guard_wait(t, "psC")
    for tile in range(16):
        t.transpose(mrp[:, tile:tile + 1],
                    sb["row_m"][0:1, tile * 128:(tile + 1) * 128], sb["eye_f"][0:1, 0:1])
        ti = t.transpose(mrp[:, 16 + tile:17 + tile],
                         sb["row_r"][0:1, tile * 128:(tile + 1) * 128], sb["eye_f"][0:1, 0:1])
    p.inc(ti, "sP", 1)
    p.wait(v, "sP")
    v.tensor_copy(sb["mr_cols"][:], mrp[:, 0:32])
    p.mile(v, "sV")
    p.guard_set("psC", "sV")
    if stats_only:
        return
    rbb, mrb = ps["D"], ps["E"]
    s = nc.sync
    for tg in range(4):
        sl = slice(tg * 512, (tg + 1) * 512)
        p.guard_wait(t, "psD")
        p.guard_wait(t, "psE")
        t.matmul(rbb[:], sb["ones_row"][:], sb["row_r"][0:1, sl], start=True, stop=True)
        mi = t.matmul(mrb[:], sb["ones_row"][:], sb["row_mr"][0:1, sl], start=True, stop=True)
        p.inc(mi, "sP", 1)
        p.wait(v, "sP")
        p.guard_wait(v, ("sq", 0))
        if to_dram is not None:
            p.guard_wait(v, "nstage")
        for c in range(NCH):
            xs = x[:, c * TOK + tg * 512: c * TOK + (tg + 1) * 512]
            if to_dram is None:
                ns = sb["normed"][:, c * TOK + tg * 512: c * TOK + (tg + 1) * 512]
            else:
                ns = to_dram[1][:, c * 512:(c + 1) * 512]
            v.tensor_tensor(sb["tmpA"][:, 0:512], xs, rbb[:], op=ALU.mult)
            v.drain()
            v.tensor_tensor(ns, sb["tmpA"][:, 0:512], mrb[:], op=ALU.subtract)
            v.drain()
        p.mile(v, "sV")
        p.guard_set("psD", "sV")
        p.guard_set("psE", "sV")
        if to_dram is not None:
            loc_ap, stage = to_dram
            p.wait(s, "sV")
            dst = loc_ap.rearrange("(c p) (g t2) -> p c g t2", p=128, t2=512)
            di = s.dma_start(out=dst[:, :, tg, :],
                             in_=stage[:].rearrange("p (c t2) -> p c t2", c=NCH))
            p.inc(di, "dS", 16)
            p.guard_set("nstage", "dS")


def emit_moe_sublayer(p, sb, ps, io, tag):
    nc = p.nc
    with ExitStack() as st:
        def A(name, shape, dtype):
            t_ = st.enter_context(nc.sbuf_tensor(f"moe{tag}_{name}", shape, dtype))
            sb[name] = t_
            return t_
        A("normed", [128, NCH * TOK], BF16)
        A("gwT", [128, NCH * 8], F32)
        A("gsum_row", [1, 8], F32)
        A("gates", [8, TOK], F32)
        A("g_b", [128, 1024], F32)
        A("h", [128, 12 * 1024], BF16)
        sb["w1g"] = [st.enter_context(nc.sbuf_tensor(f"moe{tag}_w1g{i}", [128, NCH * 128], BF16))
                     for i in range(2)]
        sb["w3g"] = [st.enter_context(nc.sbuf_tensor(f"moe{tag}_w3g{i}", [128, NCH * 128], BF16))
                     for i in range(2)]
        sb["w2s"] = [st.enter_context(nc.sbuf_tensor(f"moe{tag}_w2s0", [128, 12 * 1024], BF16))]
        emit_ln(p, sb, ps)
        emit_moe(p, sb, ps, io)


def emit_moe(p, sb, ps, io):
    nc = p.nc
    t, v, a, g = nc.tensor, nc.vector, nc.scalar, nc.gpsimd
    s = nc.sync
    x = sb["x"]
    p.guard_wait(s, "gwT")
    p.inc(s.dma_start(out=sb["gwT"][:].rearrange("p (c e) -> p c e", c=NCH),
                      in_=io["gate_wT"].rearrange("(c p) e -> p c e", p=128)),
          "dS", 16)
    p.wait(t, "dS")
    bk = ps["A"]
    p.guard_wait(t, "psA")
    for c in range(NCH):
        mi = t.matmul(bk[0:1, 0:8], sb["ones1"][:], sb["gwT"][:, c * 8:(c + 1) * 8],
                      start=(c == 0), stop=(c == NCH - 1))
    p.inc(mi, "sP", 1)
    p.wait(v, "sP")
    v.tensor_copy(sb["gsum_row"][0:1, 0:8], bk[0:1, 0:8])
    v.tensor_scalar_mul(sb["row_t"][0:1, :], sb["row_m"][0:1, :], -1.0)
    p.mile(v, "sV")
    p.guard_set("psA", "sV")
    p.wait(t, "sV")
    scp = ps["C"]
    gtp = ps["F"]
    w = sb["tmpA"]
    for tile in range(16):
        p.guard_wait(t, "psC")
        if tile % 4 == 0:
            p.guard_wait(t, "psF")
        for c in range(NCH):
            xs = x[:, c * TOK + tile * 128:c * TOK + (tile + 1) * 128]
            t.matmul(scp[0:128, 0:8], xs, sb["gwT"][:, c * 8:(c + 1) * 8],
                     start=(c == 0), stop=False)
        mi = t.matmul(scp[0:128, 0:8], sb["row_t"][0:1, tile * 128:(tile + 1) * 128],
                      sb["gsum_row"][0:1, 0:8], start=False, stop=True)
        p.inc(mi, "sP", 1)
        p.wait(v, "sP")
        r_col = sb["mr_cols"][:, 16 + tile:17 + tile]
        v.tensor_scalar_mul(w[:, 0:8], scp[0:128, 0:8], r_col)
        p.mile(v, "sV")
        p.guard_set("psC", "sV")
        v.tensor_reduce(w[:, 16:17], w[:, 0:8], axis=AXX, op=ALU.max)
        v.drain()
        v.tensor_scalar_mul(w[:, 17:18], w[:, 16:17], -1.0)
        p.mile(v, "sV")
        p.wait(a, "sV")
        a.activation(w[:, 24:32], w[:, 0:8], AF.Exp, bias=w[:, 17:18])
        p.mile(a, "sA")
        p.wait(v, "sA")
        v.tensor_reduce(w[:, 18:19], w[:, 24:32], axis=AXX, op=ALU.add)
        v.drain()
        v.reciprocal(w[:, 19:20], w[:, 18:19])
        v.drain()
        v.tensor_scalar_mul(w[:, 32:40], w[:, 24:32], w[:, 19:20])
        v.drain()
        v.tensor_reduce(w[:, 16:17], w[:, 32:40], axis=AXX, op=ALU.max)
        v.drain()
        v.tensor_single_scalar(w[:, 8:16], w[:, 32:40], w[:, 16:17], op=ALU.is_equal)
        v.drain()
        v.tensor_tensor(w[:, 8:16], w[:, 8:16], sb["iotaB"][:], op=ALU.mult)
        v.drain()
        v.tensor_reduce(w[:, 16:17], w[:, 8:16], axis=AXX, op=ALU.min)
        v.drain()
        v.tensor_single_scalar(w[:, 40:48], sb["iotaB"][:], w[:, 16:17], op=ALU.is_equal)
        v.drain()
        v.scalar_tensor_tensor(w[:, 48:56], w[:, 40:48], -BIG, w[:, 32:40],
                               op0=ALU.mult, op1=ALU.add)
        v.drain()
        v.tensor_reduce(w[:, 16:17], w[:, 48:56], axis=AXX, op=ALU.max)
        v.drain()
        v.tensor_single_scalar(w[:, 8:16], w[:, 48:56], w[:, 16:17], op=ALU.is_equal)
        v.drain()
        v.tensor_tensor(w[:, 8:16], w[:, 8:16], sb["iotaB"][:], op=ALU.mult)
        v.drain()
        v.tensor_reduce(w[:, 16:17], w[:, 8:16], axis=AXX, op=ALU.min)
        v.drain()
        v.tensor_single_scalar(w[:, 8:16], sb["iotaB"][:], w[:, 16:17], op=ALU.is_equal)
        v.drain()
        v.tensor_tensor(w[:, 48:56], w[:, 40:48], w[:, 8:16], op=ALU.add)
        v.drain()
        v.tensor_tensor(w[:, 56:64], w[:, 48:56], w[:, 32:40], op=ALU.mult)
        if "dbg_scr" in sb:
            v.tensor_copy(sb["dbg_scr"][:, tile * 16:tile * 16 + 8], w[:, 0:8])
            v.tensor_copy(sb["dbg_scr"][:, tile * 16 + 8:tile * 16 + 16], w[:, 32:40])
            if tile == 0:
                v.tensor_copy(sb["dbg_scr"][:, 256:264], w[:, 24:32])
                v.tensor_copy(sb["dbg_scr"][:, 264:265], w[:, 16:17])
                v.tensor_copy(sb["dbg_scr"][:, 265:266], w[:, 17:18])
                v.tensor_copy(sb["dbg_scr"][:, 266:267], w[:, 18:19])
                v.tensor_copy(sb["dbg_scr"][:, 267:268], w[:, 19:20])

        p.mile(v, "sV")
        p.wait(t, "sV")
        ti = t.transpose(gtp[0:8, (tile % 4) * 128:(tile % 4 + 1) * 128],
                         w[:, 56:64], sb["eye_f"][:])
        p.inc(ti, "sP", 1)
        if tile % 4 == 3:
            p.wait(v, "sP")
            v.tensor_copy(sb["gates"][0:8, (tile // 4) * 512:(tile // 4 + 1) * 512],
                          gtp[0:8, 0:512])
            p.mile(v, "sV")
            p.guard_set("psF", "sV")
    p.guard_set("psC", "sV")

    zb = [ps["A"], ps["B"], ps["C"], ps["F"]]
    zbname = ["psA", "psB", "psC", "psF"]
    ob = [ps["D"], ps["E"]]
    obname = ["psD", "psE"]
    gates_ready = p.cnt("sV")
    HGRP = [list(range(0, 12)), list(range(12, 22))]
    HC0 = [0, 12]
    HCN = [12, 10]
    for hh in range(2):
        for e in range(NEXP):
            for m in range(2):
                tg = hh * 2 + m
                bank = ob[m]
                p.guard_wait(t, obname[m])
                if hh == 0 and e == 0 and m == 0:
                    p.wait(t, "sV", gates_ready)
                mi = t.matmul(bank[:], sb["sel8"][:, e * 128:(e + 1) * 128],
                              sb["gates"][0:8, tg * 512:(tg + 1) * 512],
                              start=True, stop=True)
                p.inc(mi, "sP", 1)
                p.wait(a, "sP")
                a.copy(sb["g_b"][:, m * 512:(m + 1) * 512], bank[:])
                p.mile(a, "sA")
                p.guard_set(obname[m], "sA")
            first_h_write = True
            for half in range(2):
                p.guard_wait(g, "w2s")
                w2b = sb["w2s"][0]
                w2_ready = p.inc(g.dma_start(
                    out=w2b[:, 0:HCN[half] * 1024].rearrange("p (gg f) -> p gg f", gg=HCN[half]),
                    in_=io["w2"][e].rearrange("(gg p) f -> p gg f", p=128)
                        [:, HC0[half]:HC0[half] + HCN[half], :]), "dG", 16)
                for wgi, hc in enumerate(HGRP[half]):
                    buf = wgi % 2
                    w1b = sb["w1g"][buf]
                    w3b = sb["w3g"][buf]
                    p.guard_wait(g, ("w1g", buf))
                    p.inc(g.dma_start(
                        out=w1b[:].rearrange("p (c h) -> p c h", c=NCH),
                        in_=io["w1"][e].rearrange("(c p) h -> p c h", p=128)
                            [:, :, hc * 128:(hc + 1) * 128]), "dG", 16)
                    dg_ready = p.inc(g.dma_start(
                        out=w3b[:].rearrange("p (c h) -> p c h", c=NCH),
                        in_=io["w3"][e].rearrange("(c p) h -> p c h", p=128)
                            [:, :, hc * 128:(hc + 1) * 128]), "dG", 16)
                    p.wait(t, "dG", dg_ready)
                    if True:
                        hcL = hc - HC0[half]
                        for m in range(2):
                            tg = hh * 2 + m
                            z1 = zb[m * 2]
                            z3 = zb[m * 2 + 1]
                            p.guard_wait(t, zbname[m * 2])
                            p.guard_wait(t, zbname[m * 2 + 1])
                            for c in range(NCH):
                                nss = sb["normed"][:, c * TOK + tg * 512:c * TOK + (tg + 1) * 512]
                                t.matmul(z1[:], w1b[:, c * 128:(c + 1) * 128],
                                         nss, start=(c == 0), stop=(c == NCH - 1))
                            for c in range(NCH):
                                nss = sb["normed"][:, c * TOK + tg * 512:c * TOK + (tg + 1) * 512]
                                mi = t.matmul(z3[:], w3b[:, c * 128:(c + 1) * 128],
                                              nss, start=(c == 0), stop=(c == NCH - 1))
                            p.inc(mi, "sP", 1)
                            p.wait(v, "sP")
                            v.tensor_tensor(sb["tmpA"][:, 0:512], z1[:],
                                            sb["g_b"][:, m * 512:(m + 1) * 512], op=ALU.mult)
                            p.mile(v, "sV")
                            p.guard_set(zbname[m * 2], "sV")
                            p.wait(a, "sV")
                            a.activation(sb["tmpB"][:, 0:512], sb["tmpA"][:, 0:512], AF.Silu)
                            p.mile(a, "sA")
                            p.wait(v, "sA")
                            if first_h_write:
                                p.guard_wait(v, "h")
                                first_h_write = False
                            v.tensor_tensor(
                                sb["h"][:, hcL * 1024 + m * 512:hcL * 1024 + (m + 1) * 512],
                                sb["tmpB"][:, 0:512], z3[:], op=ALU.mult)
                            p.mile(v, "sV")
                            p.guard_set(zbname[m * 2 + 1], "sV")
                    p.guard_set(("w1g", buf), "sP")
                h_ready = p.cnt("sV")
                p.wait(t, "dG", w2_ready)
                p.wait(t, "sV", h_ready)
                for oc in range(NCH):
                    for m in range(2):
                        tg = hh * 2 + m
                        bank = ob[m]
                        p.guard_wait(t, obname[m])
                        for gg in range(HCN[half]):
                            mi = t.matmul(bank[:],
                                          w2b[:, gg * 1024 + oc * 128:gg * 1024 + (oc + 1) * 128],
                                          sb["h"][:, gg * 1024 + m * 512:gg * 1024 + (m + 1) * 512],
                                          start=(gg == 0), stop=(gg == HCN[half] - 1))
                        p.inc(mi, "sP", 1)
                        p.wait(v, "sP")
                        xs = x[:, oc * TOK + tg * 512:oc * TOK + (tg + 1) * 512]
                        v.scalar_tensor_tensor(sb["tmpA"][:, 0:512], bank[:], 0.5,
                                               sb["g_b"][:, m * 512:(m + 1) * 512],
                                               op0=ALU.mult, op1=ALU.mult)
                        v.drain()
                        v.tensor_tensor(xs, xs, sb["tmpA"][:, 0:512], op=ALU.add)
                        p.mile(v, "sV")
                        p.guard_set(obname[m], "sV")
                p.guard_set("w2s", "sP")
                p.guard_set("h", "sP")
                first_h_write = True
            p.guard_set("g_b", "sV")
    p.guard_set("gwT", "sP")




def emit_exchange_flags(p, sb, io, exch, SLpar):
    nc = p.nc
    s = nc.sync
    ALUo = ALU
    p.wait(s, "dS")
    fl = io["sh_flags"].rearrange("(e sl) one -> e sl one", sl=2)
    di = s.dma_start(out=fl[exch, SLpar, :],
                     in_=sb["nonces"][exch:exch + 1, :])
    p.inc(di, "dS", 16)
    p.wait(s, "dS")
    nr = s.alloc_register(f"nr{exch}")
    s.load(nr, sb["nonces"][exch:exch + 1, 0:1])
    for slot in range(2):
        c = s.alloc_register(f"c{exch}_{slot}")
        f = s.alloc_register(f"f{exch}_{slot}")
        s.reg_mov(c, 1)
        with s.While(c):
            s.load(f, io["sh_flags"][exch * 2 + slot:exch * 2 + slot + 1, 0:1])
            s.reg_alu(c, f, nr, ALUo.not_equal)
    s.nop().then_inc(p._sems["sG"], 1)
    p._cnt["sG"] += 1


def emit_attn0(p, sb, ps, io, ctx):
    nc = p.nc
    t, v, a, g, s = nc.tensor, nc.vector, nc.scalar, nc.gpsimd, nc.sync
    x = sb["x"]
    NT = T // 512

    with ExitStack() as st:
        V_sb = st.enter_context(nc.sbuf_tensor("at_V", [128, 32 * 520], BF16))
        lncols = st.enter_context(nc.sbuf_tensor("at_lncols", [128, 4], F32))
        negU = st.enter_context(nc.sbuf_tensor("at_negU", [128, 8], F32))
        nrm8 = st.enter_context(nc.sbuf_tensor("at_nrm8", [8, 8], F32))
        bocols = st.enter_context(nc.sbuf_tensor("at_bo", [128, 8], F32))
        sb["selH"] = st.enter_context(nc.sbuf_tensor("at_selH", [128, 32], F32))
        sb["selT"] = st.enter_context(nc.sbuf_tensor("at_selT", [8, 4 * 128], F32))
        sb["trimask"] = st.enter_context(nc.sbuf_tensor("at_trimask", [128, 128], F32))
        p.fence(s)
        for nm in ("selH", "selT", "trimask"):
            p.inc(s.dma_start(out=sb[nm][:], in_=io[nm]), "dS", 16)
        lnv = {}
        for nm in ("qln_g", "qln_b", "kln_g", "kln_b"):
            lnv[nm] = st.enter_context(nc.sbuf_tensor("at_" + nm, [128, 1], F32))
            p.inc(s.dma_start(out=lnv[nm][:], in_=io[nm]), "dS", 16)
        p.inc(s.dma_start(out=bocols[:], in_=io["l0_bo_cols"]), "dS", 16)
        p.wait(s, "dS")
        preg_s = nc.values_load(sb["par"][0:1, 0:1], engines=[s.engine],
                                skip_runtime_bounds_check=True)
        SL = bass.ts(preg_s, 1)

        with ExitStack() as st2:
            stage = st2.enter_context(nc.sbuf_tensor("at_nstage", [128, NCH * 512], BF16))
            emit_ln(p, sb, ps, to_dram=(io["normed_loc"], stage))
            p.wait(s, "dS")
            shn = io["sh_normed0"].rearrange("r (hf t2) -> r hf t2", hf=2)
            di = s.dma_start(out=shn[:, SL, :],
                             in_=io["normed_loc"].rearrange("r (one t) -> r one t", one=1))
            p.inc(di, "dS", 16)
            emit_exchange_flags(p, sb, io, 0, SL)

        p.wait(v, "dS")
        v.tensor_scalar_mul(lncols[:, 0:1], lnv["qln_g"][:], SCALE0)
        v.tensor_scalar_mul(lncols[:, 1:2], lnv["qln_b"][:], SCALE0)
        v.tensor_scalar_mul(lncols[:, 2:3], lnv["kln_g"][:], SCALE0)
        v.tensor_scalar_mul(lncols[:, 3:4], lnv["kln_b"][:], SCALE0)
        v.memset(V_sb[:], 1.0)
        p.mile(v, "sV")

        with ExitStack() as st2:
            wq = st2.enter_context(nc.sbuf_tensor("at_wq", [128, NCH * 512], BF16))
            wk = st2.enter_context(nc.sbuf_tensor("at_wk", [128, NCH * 512], BF16))
            wv = st2.enter_context(nc.sbuf_tensor("at_wv", [128, NCH * 512], BF16))
            nstream = st2.enter_context(nc.sbuf_tensor("at_nst", [128, NCH * 512], BF16))
            stg = st2.enter_context(nc.sbuf_tensor("at_stg", [128, 4 * 512], BF16))
            qtmp = st2.enter_context(nc.sbuf_tensor("at_qtmp", [128, 4 * 512], F32))
            hrows = st2.enter_context(nc.sbuf_tensor("at_hrows", [8, 3 * 512], F32))
            p.wait(g, "dS")
            for nm, ionm in ((wq, "l0_wq"), (wk, "l0_wk"), (wv, "l0_wv")):
                p.inc(g.dma_start(out=nm[:].rearrange("p (c o) -> p c o", c=NCH),
                                  in_=io[ionm].rearrange("(c p) o -> p c o", p=128)),
                      "dG", 16)
            p.wait(g, "sG")
            nsh = io["sh_normed0"].rearrange("(c p) t -> p c t", p=128)
            pq = [ps["A"], ps["B"], ps["C"], ps["F"]]
            pqn = ["psA", "psB", "psC", "psF"]
            stats1, stats2 = ps["D"], ps["E"]
            for tg in range(NT):
                p.guard_wait(g, "nstream")
                di = g.dma_start(out=nstream[:].rearrange("p (c t2) -> p c t2", c=NCH),
                                 in_=nsh[:, :, tg * 512:(tg + 1) * 512])
                nst_ready = p.inc(di, "dG", 16)
                p.wait(t, "dG", nst_ready)
                for proj in range(2):
                    wgt = wq if proj == 0 else wk
                    gcol = lncols[:, 2 * proj:2 * proj + 1]
                    bcol = lncols[:, 2 * proj + 1:2 * proj + 2]
                    dr = io["q_dr"] if proj == 0 else io["k_dr"]
                    for oc in range(4):
                        p.guard_wait(t, pqn[oc])
                        for kc in range(NCH):
                            mi = t.matmul(pq[oc][:],
                                          wgt[:, kc * 512 + oc * 128:kc * 512 + (oc + 1) * 128],
                                          nstream[:, kc * 512:(kc + 1) * 512],
                                          start=(kc == 0), stop=(kc == NCH - 1))
                        p.inc(mi, "sP", 1)
                    p.wait(a, "sP")
                    for oc in range(4):
                        a.copy(qtmp[:, oc * 512:(oc + 1) * 512], pq[oc][:])
                    p.mile(a, "sA")
                    for oc in range(4):
                        p.guard_set(pqn[oc], "sA")
                    p.guard_wait(t, "psD")
                    p.guard_wait(t, "psE")
                    p.wait(t, "sA")
                    for oc in range(4):
                        sqb = sb["tmpB"] if oc % 2 == 0 else sb["tmpA"]
                        p.guard_wait(a, ("sqB", oc % 2))
                        a.activation(sqb[:, 0:512], qtmp[:, oc * 512:(oc + 1) * 512], AF.Square)
                        p.mile(a, "sA")
                        p.wait(t, "sA")
                        t.matmul(stats1[0:8, :], sb["selH"][:, oc * 8:(oc + 1) * 8],
                                 qtmp[:, oc * 512:(oc + 1) * 512],
                                 start=(oc == 0), stop=(oc == 3))
                        mi = t.matmul(stats2[0:8, :], sb["selH"][:, oc * 8:(oc + 1) * 8],
                                      sqb[:, 0:512], start=(oc == 0), stop=(oc == 3))
                        p.inc(mi, "sP", 1)
                        p.guard_set(("sqB", oc % 2), "sP")
                    p.wait(v, "sP")
                    v.tensor_scalar_mul(hrows[0:8, 0:512], stats1[0:8, :], 1.0 / DH)
                    v.tensor_scalar_mul(hrows[0:8, 512:1024], stats2[0:8, :], 1.0 / DH)
                    v.drain()
                    v.tensor_tensor(hrows[0:8, 1024:1536], hrows[0:8, 0:512],
                                    hrows[0:8, 0:512], op=ALU.mult)
                    v.drain()
                    v.tensor_tensor(hrows[0:8, 512:1024], hrows[0:8, 512:1024],
                                    hrows[0:8, 1024:1536], op=ALU.subtract)
                    v.drain()
                    v.tensor_scalar_add(hrows[0:8, 512:1024], hrows[0:8, 512:1024], EPS)
                    p.mile(v, "sV")
                    p.guard_set("psD", "sV")
                    p.guard_set("psE", "sV")
                    p.wait(a, "sV")
                    a.activation(hrows[0:8, 1024:1536], hrows[0:8, 512:1024], AF.Ln)
                    a.drain()
                    a.activation(hrows[0:8, 512:1024], hrows[0:8, 1024:1536], AF.Exp,
                                 scale=-0.5)
                    p.mile(a, "sA")
                    p.wait(t, "sA")
                    for oc in range(4):
                        p.guard_wait(t, "psD")
                        p.guard_wait(t, "psE")
                        t.matmul(stats1[:], sb["selT"][0:8, oc * 128:(oc + 1) * 128],
                                 hrows[0:8, 0:512], start=True, stop=True)
                        mi = t.matmul(stats2[:], sb["selT"][0:8, oc * 128:(oc + 1) * 128],
                                      hrows[0:8, 512:1024], start=True, stop=True)
                        p.inc(mi, "sP", 1)
                        p.wait(v, "sP")
                        v.tensor_tensor(sb["tmpA"][:, 0:512], qtmp[:, oc * 512:(oc + 1) * 512],
                                        stats1[:], op=ALU.subtract)
                        v.drain()
                        v.tensor_tensor(sb["tmpB"][:, 0:512], sb["tmpA"][:, 0:512],
                                        stats2[:], op=ALU.mult)
                        v.drain()
                        v.tensor_scalar(stg[:, oc * 512:(oc + 1) * 512], sb["tmpB"][:, 0:512],
                                        gcol, bcol, op0=ALU.mult, op1=ALU.add)
                        p.mile(v, "sV")
                        p.guard_set("psD", "sV")
                        p.guard_set("psE", "sV")
                    p.guard_wait(t, "psD")
                    p.wait(t, "sV")
                    p.wait(a, "sV")
                    for oc in range(4):
                        p.guard_wait(a, ("sqB", 0))
                        a.activation(sb["tmpB"][:, 0:512], stg[:, oc * 512:(oc + 1) * 512],
                                     AF.Square)
                        p.mile(a, "sA")
                        p.wait(t, "sA")
                        mi = t.matmul(stats1[0:8, :], sb["selH"][:, oc * 8:(oc + 1) * 8],
                                      sb["tmpB"][:, 0:512], start=(oc == 0), stop=(oc == 3))
                        p.inc(mi, "sP", 1)
                        p.guard_set(("sqB", 0), "sP")
                    p.wait(v, "sP")
                    v.tensor_reduce(sb["tmpA"][0:8, 0:1], stats1[0:8, :], axis=AXX, op=ALU.max)
                    v.drain()
                    if tg == 0:
                        v.tensor_copy(nrm8[0:8, proj:proj + 1], sb["tmpA"][0:8, 0:1])
                    else:
                        v.tensor_tensor(nrm8[0:8, proj:proj + 1], nrm8[0:8, proj:proj + 1],
                                        sb["tmpA"][0:8, 0:1], op=ALU.max)
                    p.mile(v, "sV")
                    p.guard_set("psD", "sV")
                    p.wait(s, "sV")
                    p.guard_wait(s, "stg")
                    di = s.dma_start(out=dr.rearrange("(oc p2) t -> p2 oc t", p2=128)
                                     [:, :, tg * 512:(tg + 1) * 512],
                                     in_=stg[:].rearrange("p (oc t2) -> p oc t2", oc=4))
                    p.inc(di, "dS", 16)
                    p.guard_set("stg", "dS")
                for ttl in range(4):
                    tt = tg * 4 + ttl
                    bank = pq[ttl % 2]
                    p.guard_wait(t, pqn[ttl % 2])
                    for kc in range(NCH):
                        mi = t.matmul(bank[:],
                                      nstream[:, kc * 512 + ttl * 128:kc * 512 + (ttl + 1) * 128],
                                      wv[:, kc * 512:(kc + 1) * 512],
                                      start=(kc == 0), stop=(kc == NCH - 1))
                    p.inc(mi, "sP", 1)
                    p.wait(v, "sP")
                    vv = V_sb[:].rearrange("p (tt hb d) -> p tt hb d", tt=32, hb=8)
                    v.tensor_copy(vv[:, tt, :, 0:64],
                                  bank[:].rearrange("p (hb d) -> p hb d", hb=8))
                    p.mile(v, "sV")
                    p.guard_set(pqn[ttl % 2], "sV")
                p.guard_set("nstream", "sP")

        v.tensor_tensor(nrm8[0:8, 2:3], nrm8[0:8, 0:1], nrm8[0:8, 1:2], op=ALU.mult)
        v.drain()
        p.mile(v, "sV")
        p.wait(a, "sV")
        a.activation(nrm8[0:8, 3:4], nrm8[0:8, 2:3], AF.Ln)
        a.drain()
        a.activation(nrm8[0:8, 4:5], nrm8[0:8, 3:4], AF.Exp, scale=0.5)
        p.mile(a, "sA")
        p.wait(t, "sA")
        p.guard_wait(t, "psD")
        ti = t.transpose(ps["D"][0:1, 0:8], nrm8[0:8, 4:5], sb["eye_f"][0:8, 0:8])
        p.inc(ti, "sP", 1)
        p.wait(v, "sP")
        v.tensor_copy(sb["tmpA"][0:1, 0:8], ps["D"][0:1, 0:8])
        p.mile(v, "sV")
        p.wait(t, "sV")
        ti = t.matmul(ps["D"][:, 0:8], sb["ones_row"][:], sb["tmpA"][0:1, 0:8],
                      start=True, stop=True)
        p.inc(ti, "sP", 1)
        p.wait(a, "sP")
        a.activation(negU[:], ps["D"][:, 0:8], AF.Copy, scale=-1.0)
        p.mile(a, "sA")
        p.guard_set("psD", "sA")

        with ExitStack() as st2:
            kst = [st2.enter_context(nc.sbuf_tensor(f"at_kst{i}", [64, T], BF16))
                   for i in range(2)]
            qst = [st2.enter_context(nc.sbuf_tensor(f"at_qst{i}", [64, 512], BF16))
                   for i in range(2)]
            Pt = [st2.enter_context(nc.sbuf_tensor(f"at_pt{i}", [128, 512], BF16))
                  for i in range(2)]
            arow = st2.enter_context(nc.sbuf_tensor("at_arow", [64, 512], BF16))
            sS = [ps["A"], ps["B"]]
            sSn = ["psA", "psB"]
            pO, pOn = ps["C"], "psC"
            pB, pBn = ps["E"], "psE"
            units = [(h, qg) for h in range(NH) for qg in range(NT)]
            p.fence(g)
            for ui, (h, qg) in enumerate(units):
                ktn = qg * 4 + 4
                bufi = ui % 2
                p.guard_wait(g, ("kst", bufi))
                p.inc(g.dma_start(out=kst[bufi][:, 0:ktn * 128],
                                  in_=io["k_dr"][h * 64:(h + 1) * 64, 0:ktn * 128]), "dG", 16)
                p.guard_wait(g, ("qst", bufi))
                dg_ready = p.inc(g.dma_start(out=qst[bufi][:],
                                             in_=io["q_dr"][h * 64:(h + 1) * 64,
                                                            qg * 512:(qg + 1) * 512]),
                                 "dG", 16)
                p.guard_wait(t, pOn)
                p.wait(t, "dG", dg_ready)
                for kt in range(ktn):
                    dead = max(0, kt * 128 - qg * 512)
                    nvalid = 512 - dead
                    sbank = sS[kt % 2]
                    p.guard_wait(t, sSn[kt % 2])
                    mi = t.matmul(sbank[:, 0:nvalid], kst[bufi][:, kt * 128:(kt + 1) * 128],
                                  qst[bufi][:, dead:512], start=True, stop=True)
                    p.inc(mi, "sP", 1)
                    if kt >= qg * 4:
                        p.wait(v, "sP")
                        v.tensor_tensor(sbank[:, 0:128], sbank[:, 0:128], sb["trimask"][:],
                                        op=ALU.add)
                        p.mile(v, "sV")
                        p.wait(a, "sV")
                    p.guard_wait(v, ("Pt", kt % 2))
                    if dead > 0:
                        v.memset(Pt[kt % 2][:, 0:dead], 0.0)
                        p.mile(v, "sV")
                        p.wait(a, "sV")
                    p.wait(a, "sP")
                    a.activation(Pt[kt % 2][:, dead:512], sbank[:, 0:nvalid], AF.Exp,
                                 bias=negU[:, h:h + 1])
                    p.mile(a, "sA")
                    p.guard_set(sSn[kt % 2], "sA")
                    p.wait(t, "sA")
                    mi = t.matmul(pO[0:65, :],
                                  V_sb[:].rearrange("p (tt c) -> p tt c", tt=32)
                                  [:, kt, h * 65:(h + 1) * 65],
                                  Pt[kt % 2][:], start=(kt == 0), stop=(kt == ktn - 1))
                    p.inc(mi, "sP", 1)
                    p.guard_set(("Pt", kt % 2), "sP")
                p.guard_set(("kst", bufi), "sP")
                p.guard_set(("qst", bufi), "sP")
                p.wait(v, "sP")
                p.guard_wait(v, ("lrow",))
                v.tensor_copy(sb["tmpB"][0:1, 0:512], pO[64:65, :])
                v.drain()
                v.reciprocal(sb["tmpB"][0:1, 0:512], sb["tmpB"][0:1, 0:512])
                p.mile(v, "sV")
                p.wait(t, "sV")
                p.guard_wait(t, pBn)
                mi = t.matmul(pB[:], sb["ones_row"][:], sb["tmpB"][0:1, 0:512],
                              start=True, stop=True)
                p.inc(mi, "sP", 1)
                p.guard_set(("lrow",), "sP")
                p.wait(a, "sP")
                p.guard_wait(a, ("rlb",))
                a.copy(sb["tmpA"][0:64, 0:512], pB[0:64, :])
                p.mile(a, "sA")
                p.wait(v, "sA")
                v.tensor_tensor(arow[:], pO[0:64, :], sb["tmpA"][0:64, 0:512], op=ALU.mult)
                p.mile(v, "sV")
                p.guard_set(pOn, "sV")
                p.guard_set(pBn, "sV")
                p.guard_set(("rlb",), "sV")
                p.wait(s, "sV")
                p.guard_wait(s, "arow")
                di = s.dma_start(out=io["attn_dr"][h * 64:(h + 1) * 64,
                                                   qg * 512:(qg + 1) * 512],
                                 in_=arow[:])
                p.inc(di, "dS", 16)
                p.guard_set("arow", "dS")

        with ExitStack() as st2:
            wo = st2.enter_context(nc.sbuf_tensor("at_wo", [128, 4 * 1024], BF16))
            astream = st2.enter_context(nc.sbuf_tensor("at_ast", [128, 4 * 512], BF16))
            stg2 = st2.enter_context(nc.sbuf_tensor("at_stg2", [128, 2 * 512], BF16))
            p.fence(g)
            p.inc(g.dma_start(out=wo[:].rearrange("p (c o) -> p c o", c=4),
                              in_=io["l0_wo"].rearrange("(c p) o -> p c o", p=128)), "dG", 16)
            ash = io["attn_dr"].rearrange("(c p) t -> p c t", p=128)
            p.wait(a, "dS")
            pr = [ps["A"], ps["B"]]
            prn = ["psA", "psB"]
            psh = io["partial_loc"].rearrange("(c p) t -> p c t", p=128)
            for tg in range(NT):
                p.guard_wait(g, "astream")
                di = g.dma_start(out=astream[:].rearrange("p (c t2) -> p c t2", c=4),
                                 in_=ash[:, :, tg * 512:(tg + 1) * 512])
                nst_ready = p.inc(di, "dG", 16)
                p.wait(t, "dG", nst_ready)
                for oc in range(NCH):
                    bank = pr[oc % 2]
                    p.guard_wait(t, prn[oc % 2])
                    for kc in range(4):
                        mi = t.matmul(bank[:],
                                      wo[:, kc * 1024 + oc * 128:kc * 1024 + (oc + 1) * 128],
                                      astream[:, kc * 512:(kc + 1) * 512],
                                      start=(kc == 0), stop=(kc == 3))
                    p.inc(mi, "sP", 1)
                    p.wait(a, "sP")
                    p.guard_wait(a, ("pstg", oc % 2))
                    a.copy(stg2[:, (oc % 2) * 512:(oc % 2 + 1) * 512], bank[:])
                    p.mile(a, "sA")
                    p.guard_set(prn[oc % 2], "sA")
                    p.wait(s, "sA")
                    di = s.dma_start(out=psh[:, oc:oc + 1, tg * 512:(tg + 1) * 512],
                                     in_=stg2[:, (oc % 2) * 512:(oc % 2 + 1) * 512]
                                     .rearrange("p (one t) -> p one t", one=1))
                    p.inc(di, "dS", 16)
                    p.guard_set(("pstg", oc % 2), "dS")
                p.guard_set("astream", "sP")
        p.wait(s, "dS")
        shp = io["sh_partial0"].rearrange("(sl r) t -> sl r t", sl=2)
        di = s.dma_start(out=shp[SL, :, :],
                         in_=io["partial_loc"].rearrange("(one r) t -> one r t", one=1))
        p.inc(di, "dS", 16)
        emit_exchange_flags(p, sb, io, 1, SL)
        with ExitStack() as st2:
            rd = st2.enter_context(nc.sbuf_tensor("at_rd", [128, NCH * 2048], BF16))
            psh = io["sh_partial0"].rearrange("(sl c p) (hf t2) -> sl p c hf t2",
                                              sl=2, p=128, hf=2)
            p.fence(s)
            for slot in range(2):
                p.wait(s, "sG")
                p.guard_wait(s, "rd")
                di = s.dma_start(out=rd[:].rearrange("p (c one t) -> p c one t",
                                                     c=NCH, one=1),
                                 in_=psh[slot][:, :, SL, :])
                rd_ready = p.inc(di, "dS", 16)
                p.wait(v, "dS", rd_ready)
                for c in range(NCH):
                    for tg in range(4):
                        xs = x[:, c * TOK + tg * 512:c * TOK + (tg + 1) * 512]
                        rds = rd[:, c * 2048 + tg * 512:c * 2048 + (tg + 1) * 512]
                        if slot == 0:
                            v.tensor_tensor(sb["tmpA"][:, 0:512], xs, rds, op=ALU.add)
                            v.drain()
                            v.tensor_scalar_add(xs, sb["tmpA"][:, 0:512],
                                                bocols[:, c:c + 1])
                        else:
                            v.tensor_tensor(xs, xs, rds, op=ALU.add)
                p.mile(v, "sV")
                p.guard_set("rd", "sV")




def emit_quant_tile(p, q, out_bf, in_f32, scr):
    nc = p.nc
    v, a = nc.vector, nc.scalar
    bm3 = scr["bm"][:].rearrange("p (b one) -> p b one", one=1)
    in3 = in_f32.rearrange("p (b w) -> p b w", w=32)
    v.tensor_reduce(bm3, in3, axis=AXX, op=ALU.max, apply_absolute_value=True)
    v.drain()
    v.tensor_scalar(scr["sc"][:], scr["bm"][:], 1.0 / 7.5, 1e-30, op0=ALU.mult,
                    op1=ALU.max)
    v.drain()
    v.reciprocal(scr["rs"][:], scr["sc"][:])
    v.drain()
    for b in range(32):
        v.tensor_scalar_mul(scr["vq"][:, b * 32:(b + 1) * 32],
                            in_f32[:, b * 32:(b + 1) * 32], scr["rs"][:, b:b + 1])
    p.mile(v, "sV")
    p.wait(a, "sV")
    a.activation(scr["av"][:], scr["vq"][:], AF.Abs)
    p.mile(a, "sA")
    p.wait(v, "sA")
    v.tensor_single_scalar(scr["g2"][:], scr["av"][:], 2.0, op=ALU.is_ge)
    v.tensor_single_scalar(scr["g4"][:], scr["av"][:], 4.0, op=ALU.is_ge)
    v.drain()
    v.tensor_scalar(scr["rr"][:], scr["g2"][:], -4.0, 8.0, op0=ALU.mult, op1=ALU.add)
    v.tensor_scalar(scr["ss"][:], scr["g2"][:], 0.125, 0.125, op0=ALU.mult, op1=ALU.add)
    v.drain()
    v.scalar_tensor_tensor(scr["rr"][:], scr["g4"][:], -2.0, scr["rr"][:],
                           op0=ALU.mult, op1=ALU.add)
    v.scalar_tensor_tensor(scr["ss"][:], scr["g4"][:], 0.25, scr["ss"][:],
                           op0=ALU.mult, op1=ALU.add)
    v.drain()
    v.tensor_tensor(scr["yy"][:], scr["vq"][:], scr["rr"][:], op=ALU.mult)
    v.drain()
    v.tensor_single_scalar(scr["ff"][:], scr["yy"][:], 12582912.0, op=ALU.add)
    v.drain()
    v.tensor_single_scalar(scr["ff"][:], scr["ff"][:], -12582912.0, op=ALU.add)
    v.drain()
    v.tensor_tensor(scr["g2"][:], scr["ff"][:], scr["yy"][:], op=ALU.subtract)
    v.drain()
    v.tensor_single_scalar(scr["g2"][:], scr["g2"][:], 0.5, op=ALU.is_equal)
    v.drain()
    v.tensor_tensor(scr["ff"][:], scr["ff"][:], scr["g2"][:], op=ALU.subtract)
    v.drain()
    v.tensor_tensor(scr["yy"][:], scr["ff"][:], scr["ss"][:], op=ALU.mult)
    v.drain()
    for b in range(32):
        v.tensor_scalar_mul(out_bf[:, b * 32:(b + 1) * 32],
                            scr["yy"][:, b * 32:(b + 1) * 32], scr["sc"][:, b:b + 1])
    p.mile(v, "sV")


def emit_ffn1(p, sb, ps, io, tag):
    nc = p.nc
    t, v, a, g, s = nc.tensor, nc.vector, nc.scalar, nc.gpsimd, nc.sync
    x = sb["x"]
    emit_ln(p, sb, ps, stats_only=True)
    with ExitStack() as st:
        wd = st.enter_context(nc.sbuf_tensor(f"{tag}_wd", [128, NCH * 1024], BF16))
        xd = st.enter_context(nc.sbuf_tensor(f"{tag}_xd", [128, NCH * TOK], BF16))
        scr = {}
        for nm in ("bm", "sc", "rs"):
            scr[nm] = st.enter_context(nc.sbuf_tensor(f"{tag}_" + nm, [128, 32], F32))
        for nm in ("vq", "av", "g2", "g4", "rr", "ss", "yy", "ff", "wtmp"):
            scr[nm] = st.enter_context(nc.sbuf_tensor(f"{tag}_" + nm, [128, 1024], F32))
        qtile = st.enter_context(nc.sbuf_tensor(f"{tag}_qt", [128, 1024], BF16))
        hbias = st.enter_context(nc.sbuf_tensor(f"{tag}_hb", [128, 8], F32))
        bcols = st.enter_context(nc.sbuf_tensor(f"{tag}_bc", [128, 8], F32))
        p.fence(s)
        p.inc(s.dma_start(out=bcols[:], in_=io["l1fb_cols"]), "dS", 16)
        p.wait(v, "dS")
        v.tensor_scalar_mul(hbias[:], bcols[:], 0.5)
        p.mile(v, "sV")
        pT = ps["H"]
        for rc in range(NCH):
            p.guard_wait(s, "wtmp")
            di = s.dma_start(out=scr["wtmp"][:],
                             in_=io["l1fw"][rc * 128:(rc + 1) * 128, :])
            p.inc(di, "dS", 16)
            p.wait(v, "dS")
            emit_quant_tile(p, None, qtile[:], scr["wtmp"][:], scr)
            p.guard_set("wtmp", "sV")
            p.wait(t, "sV")
            p.guard_wait(t, "psH")
            for cc in range(NCH):
                ti = t.transpose(pT[:, cc * 128:(cc + 1) * 128],
                                 qtile[:, cc * 128:(cc + 1) * 128], sb["eye_b"][:])
            p.inc(ti, "sP", 1)
            p.wait(v, "sP")
            for cc in range(NCH):
                v.tensor_copy(wd[:, cc * 1024 + rc * 128:cc * 1024 + (rc + 1) * 128],
                              pT[:, cc * 128:(cc + 1) * 128])
            p.mile(v, "sV")
            p.guard_set("psH", "sV")
        pF = ps["A"]
        for tile in range(16):
            m_col = sb["mr_cols"][:, tile:tile + 1]
            r_col = sb["mr_cols"][:, 16 + tile:17 + tile]
            for half in range(2):
                p.wait(t, "sV")
                p.guard_wait(t, "psA")
                for cl in range(4):
                    c = half * 4 + cl
                    ti = t.transpose(pF[:, cl * 128:(cl + 1) * 128],
                                     x[:, c * TOK + tile * 128:c * TOK + (tile + 1) * 128],
                                     sb["eye_f"][:])
                p.inc(ti, "sP", 1)
                p.wait(v, "sP")
                if half == 0:
                    p.guard_wait(v, "wtmp")
                v.tensor_scalar(scr["wtmp"][:, half * 512:(half + 1) * 512], pF[:],
                                m_col, r_col, op0=ALU.subtract, op1=ALU.mult)
                p.mile(v, "sV")
                p.guard_set("psA", "sV")
            emit_quant_tile(p, None, qtile[:], scr["wtmp"][:], scr)
            p.guard_set("wtmp", "sV")
            p.wait(t, "sV")
            p.guard_wait(t, "psH")
            for c in range(NCH):
                ti = t.transpose(pT[:, c * 128:(c + 1) * 128],
                                 qtile[:, c * 128:(c + 1) * 128], sb["eye_b"][:])
            p.inc(ti, "sP", 1)
            p.wait(v, "sP")
            for c in range(NCH):
                v.tensor_copy(xd[:, c * TOK + tile * 128:c * TOK + (tile + 1) * 128],
                              pT[:, c * 128:(c + 1) * 128])
            p.mile(v, "sV")
            p.guard_set("psH", "sV")
        xd_ready = p.cnt("sV")
        p.wait(t, "sV", xd_ready)
        pr = [ps["B"], ps["C"]]
        prn = ["psB", "psC"]
        for oc in range(NCH):
            for tg in range(4):
                bank = pr[(oc * 4 + tg) % 2]
                bname = prn[(oc * 4 + tg) % 2]
                p.guard_wait(t, bname)
                for kc in range(NCH):
                    mi = t.matmul(bank[:],
                                  wd[:, kc * 1024 + oc * 128:kc * 1024 + (oc + 1) * 128],
                                  xd[:, kc * TOK + tg * 512:kc * TOK + (tg + 1) * 512],
                                  start=(kc == 0), stop=(kc == NCH - 1))
                p.inc(mi, "sP", 1)
                p.wait(v, "sP")
                p.guard_wait(v, ("fstg", 0))
                v.tensor_scalar(sb["tmpB"][:, 0:512], bank[:], 0.5, hbias[:, oc:oc + 1],
                                op0=ALU.mult, op1=ALU.add)
                v.drain()
                p.mile(v, "sV")
                p.guard_set(bname, "sV")
                xs = x[:, oc * TOK + tg * 512:oc * TOK + (tg + 1) * 512]
                v.tensor_tensor(xs, xs, sb["tmpB"][:, 0:512], op=ALU.add)
                p.mile(v, "sV")
                p.guard_set(("fstg", 0), "sV")




def emit_attn1(p, sb, ps, io, ctx):
    nc = p.nc
    t, v, a, g, s = nc.tensor, nc.vector, nc.scalar, nc.gpsimd, nc.sync
    x = sb["x"]
    NT = T // 512

    with ExitStack() as st:
        V_sb = st.enter_context(nc.sbuf_tensor("a1_V", [128, 32 * 520], BF16))
        krs = st.enter_context(nc.sbuf_tensor("a1_krs", [32, T], BF16))
        negU = st.enter_context(nc.sbuf_tensor("a1_negU", [128, 8], F32))
        nrm8 = st.enter_context(nc.sbuf_tensor("a1_nrm8", [8, 8], F32))
        crow = st.enter_context(nc.sbuf_tensor("a1_crow", [1, 8], F32))
        sb["selH"] = st.enter_context(nc.sbuf_tensor("a1_selH", [128, 32], F32))
        sb["selHr"] = st.enter_context(nc.sbuf_tensor("a1_selHr", [128, 16], F32))
        sb["trimask"] = st.enter_context(nc.sbuf_tensor("a1_trimask", [128, 128], F32))
        p.fence(s)
        for nm in ("selH", "selHr", "trimask"):
            p.inc(s.dma_start(out=sb[nm][:], in_=io[nm]), "dS", 16)
        p.inc(s.dma_start(out=crow[:], in_=io["lobo"]), "dS", 16)
        p.wait(s, "dS")
        preg_s = nc.values_load(sb["par"][0:1, 0:1], engines=[s.engine],
                                skip_runtime_bounds_check=True)
        SL = bass.ts(preg_s, 1)

        p.wait(a, "dS")
        a.activation(crow[:], crow[:], AF.Exp)
        p.mile(a, "sA")

        with ExitStack() as st2:
            stage = st2.enter_context(nc.sbuf_tensor("a1_nstage", [128, NCH * 512], BF16))
            emit_ln(p, sb, ps, to_dram=(io["normed_loc"], stage))
            p.wait(s, "dS")
            shn = io["sh_normed1"].rearrange("r (hf t2) -> r hf t2", hf=2)
            di = s.dma_start(out=shn[:, SL, :],
                             in_=io["normed_loc"].rearrange("r (one t) -> r one t", one=1))
            p.inc(di, "dS", 16)
            emit_exchange_flags(p, sb, io, 2, SL)

        v.memset(V_sb[:], 1.0)
        p.mile(v, "sV")

        with ExitStack() as st2:
            wq = st2.enter_context(nc.sbuf_tensor("a1_wq", [128, NCH * 512], BF16))
            wqr = st2.enter_context(nc.sbuf_tensor("a1_wqr", [128, NCH * 512], BF16))
            wkvd = st2.enter_context(nc.sbuf_tensor("a1_wkvd", [128, NCH * 256], BF16))
            wkr = st2.enter_context(nc.sbuf_tensor("a1_wkr", [128, NCH * 64], BF16))
            wku = st2.enter_context(nc.sbuf_tensor("a1_wku", [128, 2 * 512], BF16))
            wvu = st2.enter_context(nc.sbuf_tensor("a1_wvu", [128, 2 * 512], BF16))
            nstream = st2.enter_context(nc.sbuf_tensor("a1_nst", [128, NCH * 512], BF16))
            stg = st2.enter_context(nc.sbuf_tensor("a1_stg", [128, 4 * 512], BF16))
            lat = st2.enter_context(nc.sbuf_tensor("a1_lat", [128, 2 * 512], BF16))
            rope = [st2.enter_context(nc.sbuf_tensor(f"a1_rope{i}", [128, 2 * 512], F32))
                    for i in range(2)]
            p.fence(g)
            for nm, ionm in ((wq, "l1_wq"), (wqr, "l1_wqr_cat"), (wkvd, "l1_wkvd"),
                             (wkr, "l1_wkr_cat"), (wku, "l1_wku"), (wvu, "l1_wvu")):
                ncol = nm.shape[1] // NCH if nm not in (wku, wvu) else 512
                p.inc(g.dma_start(out=nm[:].rearrange("p (c o) -> p c o",
                                                      c=NCH if nm not in (wku, wvu) else 2),
                                  in_=io[ionm].rearrange("(c p) o -> p c o", p=128)),
                      "dG", 16)
            p.wait(g, "sG")
            nsh = io["sh_normed1"].rearrange("(c p) t -> p c t", p=128)
            pq = [ps["A"], ps["B"], ps["C"], ps["F"]]
            pqn = ["psA", "psB", "psC", "psF"]
            stats1, stats2 = ps["D"], ps["E"]
            for tg in range(NT):
                p.guard_wait(g, "nstream")
                di = g.dma_start(out=nstream[:].rearrange("p (c t2) -> p c t2", c=NCH),
                                 in_=nsh[:, :, tg * 512:(tg + 1) * 512])
                nst_ready = p.inc(di, "dG", 16)
                rb = rope[tg % 2]
                p.guard_wait(g, ("rope", tg % 2))
                p.inc(g.dma_start(out=rb[:, 0:512],
                                  in_=io["ropecos"][:, tg * 512:(tg + 1) * 512]), "dG", 16)
                rope_ready = p.inc(g.dma_start(out=rb[:, 512:1024],
                                               in_=io["ropesin"][:, tg * 512:(tg + 1) * 512]),
                                   "dG", 16)
                p.wait(t, "dG", nst_ready)
                for oc in range(4):
                    p.guard_wait(t, pqn[oc])
                    for kc in range(NCH):
                        mi = t.matmul(pq[oc][:],
                                      wq[:, kc * 512 + oc * 128:kc * 512 + (oc + 1) * 128],
                                      nstream[:, kc * 512:(kc + 1) * 512],
                                      start=(kc == 0), stop=(kc == NCH - 1))
                    p.inc(mi, "sP", 1)
                p.wait(v, "sP")
                for oc in range(4):
                    v.tensor_copy(stg[:, oc * 512:(oc + 1) * 512], pq[oc][:])
                p.mile(v, "sV")
                for oc in range(4):
                    p.guard_set(pqn[oc], "sV")
                emit_norm_update(p, sb, ps, stg, nrm8, 0, tg, 4, "selH")
                p.wait(s, "sV")
                p.guard_wait(s, "stg")
                di = s.dma_start(out=io["q_dr"].rearrange("(oc p2) t -> p2 oc t", p2=128)
                                 [:, :, tg * 512:(tg + 1) * 512],
                                 in_=stg[:].rearrange("p (oc t2) -> p oc t2", oc=4))
                p.inc(di, "dS", 16)
                p.guard_set("stg", "dS")
                for oc in range(4):
                    p.guard_wait(t, pqn[oc])
                    for kc in range(NCH):
                        mi = t.matmul(pq[oc][:],
                                      wqr[:, kc * 512 + oc * 128:kc * 512 + (oc + 1) * 128],
                                      nstream[:, kc * 512:(kc + 1) * 512],
                                      start=(kc == 0), stop=(kc == NCH - 1))
                    p.inc(mi, "sP", 1)
                p.wait(v, "sP")
                p.wait(v, "dG", rope_ready)
                p.guard_wait(v, "stg")
                for c in range(2):
                    v.tensor_tensor(sb["tmpA"][:, 0:512], pq[c][:], rb[:, 0:512], op=ALU.mult)
                    v.drain()
                    v.tensor_tensor(sb["tmpB"][:, 0:512], pq[2 + c][:], rb[:, 512:1024],
                                    op=ALU.mult)
                    v.drain()
                    v.tensor_tensor(stg[:, c * 512:(c + 1) * 512], sb["tmpA"][:, 0:512],
                                    sb["tmpB"][:, 0:512], op=ALU.add)
                    v.drain()
                p.mile(v, "sV")
                for oc in range(4):
                    p.guard_set(pqn[oc], "sV")
                emit_norm_update(p, sb, ps, stg, nrm8, 1, tg, 2, "selHr")
                p.wait(s, "sV")
                di = s.dma_start(out=io["qr_dr"].rearrange("(oc p2) t -> p2 oc t", p2=128)
                                 [:, :, tg * 512:(tg + 1) * 512],
                                 in_=stg[:, 0:1024].rearrange("p (oc t2) -> p oc t2", oc=2))
                p.inc(di, "dS", 16)
                p.guard_set("stg", "dS")
                for oc in range(2):
                    p.guard_wait(t, pqn[oc])
                    for kc in range(NCH):
                        mi = t.matmul(pq[oc][:],
                                      wkvd[:, kc * 256 + oc * 128:kc * 256 + (oc + 1) * 128],
                                      nstream[:, kc * 512:(kc + 1) * 512],
                                      start=(kc == 0), stop=(kc == NCH - 1))
                    p.inc(mi, "sP", 1)
                p.wait(v, "sP")
                p.guard_wait(v, "lat")
                for oc in range(2):
                    v.tensor_copy(lat[:, oc * 512:(oc + 1) * 512], pq[oc][:])
                p.mile(v, "sV")
                for oc in range(2):
                    p.guard_set(pqn[oc], "sV")
                p.guard_wait(t, pqn[2])
                for kc in range(NCH):
                    mi = t.matmul(pq[2][0:64, :], wkr[:, kc * 64:(kc + 1) * 64],
                                  nstream[:, kc * 512:(kc + 1) * 512],
                                  start=(kc == 0), stop=(kc == NCH - 1))
                p.inc(mi, "sP", 1)
                p.wait(v, "sP")
                v.tensor_tensor(sb["tmpA"][0:32, 0:512], pq[2][0:32, :], rb[0:32, 0:512],
                                op=ALU.mult)
                v.drain()
                v.tensor_tensor(sb["tmpB"][0:32, 0:512], pq[2][32:64, :], rb[0:32, 512:1024],
                                op=ALU.mult)
                v.drain()
                v.tensor_tensor(krs[:, tg * 512:(tg + 1) * 512], sb["tmpA"][0:32, 0:512],
                                sb["tmpB"][0:32, 0:512], op=ALU.add)
                p.mile(v, "sV")
                p.guard_set(pqn[2], "sV")
                p.guard_set(("rope", tg % 2), "sV")
                p.wait(a, "sV")
                p.guard_wait(a, ("sqB", 0))
                a.activation(sb["tmpB"][0:32, 0:512], krs[:, tg * 512:(tg + 1) * 512],
                             AF.Square)
                p.mile(a, "sA")
                p.wait(t, "sA")
                p.guard_wait(t, "psD")
                mi = t.matmul(stats1[0:1, :], sb["ones1"][0:32, :], sb["tmpB"][0:32, 0:512],
                              start=True, stop=True)
                p.inc(mi, "sP", 1)
                p.guard_set(("sqB", 0), "sP")
                p.wait(v, "sP")
                v.tensor_reduce(sb["tmpA"][0:1, 0:1], stats1[0:1, :], axis=AXX, op=ALU.max)
                v.drain()
                if tg == 0:
                    v.tensor_copy(nrm8[0:1, 3:4], sb["tmpA"][0:1, 0:1])
                else:
                    v.tensor_tensor(nrm8[0:1, 3:4], nrm8[0:1, 3:4], sb["tmpA"][0:1, 0:1],
                                    op=ALU.max)
                p.mile(v, "sV")
                p.guard_set("psD", "sV")
                p.wait(t, "sV")
                for oc in range(4):
                    p.guard_wait(t, pqn[oc])
                    for kc in range(2):
                        mi = t.matmul(pq[oc][:],
                                      wku[:, kc * 512 + oc * 128:kc * 512 + (oc + 1) * 128],
                                      lat[:, kc * 512:(kc + 1) * 512],
                                      start=(kc == 0), stop=(kc == 1))
                    p.inc(mi, "sP", 1)
                p.wait(v, "sP")
                p.guard_wait(v, "stg")
                for oc in range(4):
                    v.tensor_copy(stg[:, oc * 512:(oc + 1) * 512], pq[oc][:])
                p.mile(v, "sV")
                for oc in range(4):
                    p.guard_set(pqn[oc], "sV")
                emit_norm_update(p, sb, ps, stg, nrm8, 2, tg, 4, "selH")
                p.wait(s, "sV")
                di = s.dma_start(out=io["k_dr"].rearrange("(oc p2) t -> p2 oc t", p2=128)
                                 [:, :, tg * 512:(tg + 1) * 512],
                                 in_=stg[:].rearrange("p (oc t2) -> p oc t2", oc=4))
                p.inc(di, "dS", 16)
                p.guard_set("stg", "dS")
                for ttl in range(4):
                    tt = tg * 4 + ttl
                    bank = pq[ttl % 2]
                    p.guard_wait(t, pqn[ttl % 2])
                    for kc in range(2):
                        mi = t.matmul(bank[:],
                                      lat[:, kc * 512 + ttl * 128:kc * 512 + (ttl + 1) * 128],
                                      wvu[:, kc * 512:(kc + 1) * 512],
                                      start=(kc == 0), stop=(kc == 1))
                    p.inc(mi, "sP", 1)
                    p.wait(v, "sP")
                    vv = V_sb[:].rearrange("p (tt hb d) -> p tt hb d", tt=32, hb=8)
                    v.tensor_copy(vv[:, tt, :, 0:64],
                                  bank[:].rearrange("p (hb d) -> p hb d", hb=8))
                    p.mile(v, "sV")
                    p.guard_set(pqn[ttl % 2], "sV")
                p.guard_set("lat", "sP")
                p.guard_set("nstream", "sP")

        v.tensor_tensor(nrm8[0:8, 4:5], nrm8[0:8, 0:1], nrm8[0:8, 1:2], op=ALU.add)
        v.drain()
        p.mile(v, "sV")
        p.wait(t, "sV")
        p.guard_wait(t, "psD")
        ti = t.matmul(ps["D"][0:8, 0:1], sb["ones_row"][0:1, 0:8], nrm8[0:1, 3:4],
                      start=True, stop=True)
        p.inc(ti, "sP", 1)
        p.wait(v, "sP")
        v.tensor_tensor(nrm8[0:8, 5:6], nrm8[0:8, 2:3], ps["D"][0:8, 0:1], op=ALU.add)
        v.drain()
        v.tensor_tensor(nrm8[0:8, 6:7], nrm8[0:8, 4:5], nrm8[0:8, 5:6], op=ALU.mult)
        p.mile(v, "sV")
        p.guard_set("psD", "sV")
        p.wait(a, "sV")
        a.activation(nrm8[0:8, 7:8], nrm8[0:8, 6:7], AF.Ln)
        a.drain()
        a.activation(nrm8[0:8, 6:7], nrm8[0:8, 7:8], AF.Exp, scale=0.5)
        p.mile(a, "sA")
        p.wait(t, "sA")
        p.guard_wait(t, "psD")
        ti = t.transpose(ps["D"][0:1, 0:8], nrm8[0:8, 6:7], sb["eye_f"][0:8, 0:8])
        p.inc(ti, "sP", 1)
        p.wait(v, "sP")
        v.tensor_copy(sb["tmpA"][0:1, 0:8], ps["D"][0:1, 0:8])
        p.mile(v, "sV")
        p.wait(t, "sV")
        ti = t.matmul(ps["D"][:, 0:8], sb["ones_row"][:], sb["tmpA"][0:1, 0:8],
                      start=True, stop=True)
        p.inc(ti, "sP", 1)
        p.wait(a, "sP")
        a.activation(negU[:], ps["D"][:, 0:8], AF.Copy, scale=-SCALE1)
        p.mile(a, "sA")
        p.guard_set("psD", "sA")
        p.wait(a, "sA")
        a.activation(nrm8[0:8, 7:8], nrm8[0:8, 6:7], AF.Exp, scale=-SCALE1)
        p.mile(a, "sA")
        p.wait(t, "sA")
        p.guard_wait(t, "psE")
        ti = t.transpose(ps["E"][0:8, 0:1], crow[0:1, 0:8], sb["eye_f"][0:1, 0:1])
        p.inc(ti, "sP", 1)
        p.wait(v, "sP")
        v.tensor_tensor(nrm8[0:8, 4:5], ps["E"][0:8, 0:1], nrm8[0:8, 7:8], op=ALU.mult)
        v.drain()
        p.mile(v, "sV")
        p.wait(t, "sV")
        ti = t.transpose(ps["E"][0:1, 0:8], nrm8[0:8, 4:5], sb["eye_f"][0:8, 0:8])
        p.inc(ti, "sP", 1)
        p.wait(v, "sP")
        v.tensor_copy(crow[0:1, 0:8], ps["E"][0:1, 0:8])
        p.mile(v, "sV")
        p.guard_set("psE", "sV")

        with ExitStack() as st2:
            kst = [st2.enter_context(nc.sbuf_tensor(f"a1_kst{i}", [64, T], BF16))
                   for i in range(2)]
            qst = [st2.enter_context(nc.sbuf_tensor(f"a1_qst{i}", [64, 512], BF16))
                   for i in range(2)]
            qrst = [st2.enter_context(nc.sbuf_tensor(f"a1_qrst{i}", [32, 512], BF16))
                    for i in range(2)]
            Pt = [st2.enter_context(nc.sbuf_tensor(f"a1_pt{i}", [128, 512], BF16))
                  for i in range(2)]
            arow = st2.enter_context(nc.sbuf_tensor("a1_arow", [64, 512], BF16))
            sS = [ps["A"], ps["B"]]
            sSn = ["psA", "psB"]
            pO, pOn = ps["C"], "psC"
            pB, pBn = ps["E"], "psE"
            units = [(h, qg) for h in range(NH) for qg in range(NT)]
            p.fence(g)
            for ui, (h, qg) in enumerate(units):
                ktn = qg * 4 + 4
                bufi = ui % 2
                p.guard_wait(g, ("kst", bufi))
                p.inc(g.dma_start(out=kst[bufi][:, 0:ktn * 128],
                                  in_=io["k_dr"][h * 64:(h + 1) * 64, 0:ktn * 128]), "dG", 16)
                p.guard_wait(g, ("qst", bufi))
                p.inc(g.dma_start(out=qst[bufi][:],
                                  in_=io["q_dr"][h * 64:(h + 1) * 64,
                                                 qg * 512:(qg + 1) * 512]), "dG", 16)
                dg_ready = p.inc(g.dma_start(out=qrst[bufi][:],
                                             in_=io["qr_dr"][h * 32:(h + 1) * 32,
                                                             qg * 512:(qg + 1) * 512]),
                                 "dG", 16)
                p.guard_wait(t, pOn)
                p.wait(t, "dG", dg_ready)
                for kt in range(ktn):
                    dead = max(0, kt * 128 - qg * 512)
                    nvalid = 512 - dead
                    sbank = sS[kt % 2]
                    p.guard_wait(t, sSn[kt % 2])
                    t.matmul(sbank[:, 0:nvalid], kst[bufi][:, kt * 128:(kt + 1) * 128],
                             qst[bufi][:, dead:512], start=True, stop=False)
                    mi = t.matmul(sbank[:, 0:nvalid], krs[:, kt * 128:(kt + 1) * 128],
                                  qrst[bufi][:, dead:512], start=False, stop=True)
                    p.inc(mi, "sP", 1)
                    if kt >= qg * 4:
                        p.wait(v, "sP")
                        v.tensor_tensor(sbank[:, 0:128], sbank[:, 0:128], sb["trimask"][:],
                                        op=ALU.add)
                        p.mile(v, "sV")
                        p.wait(a, "sV")
                    p.guard_wait(v, ("Pt", kt % 2))
                    if dead > 0:
                        v.memset(Pt[kt % 2][:, 0:dead], 0.0)
                        p.mile(v, "sV")
                        p.wait(a, "sV")
                    p.wait(a, "sP")
                    a.activation(Pt[kt % 2][:, dead:512], sbank[:, 0:nvalid], AF.Exp,
                                 scale=SCALE1, bias=negU[:, h:h + 1])
                    p.mile(a, "sA")
                    p.guard_set(sSn[kt % 2], "sA")
                    p.wait(t, "sA")
                    mi = t.matmul(pO[0:65, :],
                                  V_sb[:].rearrange("p (tt c) -> p tt c", tt=32)
                                  [:, kt, h * 65:(h + 1) * 65],
                                  Pt[kt % 2][:], start=(kt == 0), stop=(kt == ktn - 1))
                    p.inc(mi, "sP", 1)
                    p.guard_set(("Pt", kt % 2), "sP")
                p.guard_set(("kst", bufi), "sP")
                p.guard_set(("qst", bufi), "sP")
                p.wait(v, "sP")
                p.guard_wait(v, ("lrow",))
                v.tensor_copy(sb["tmpB"][0:1, 0:512], pO[64:65, :])
                v.drain()
                v.tensor_single_scalar(sb["tmpB"][0:1, 0:512], sb["tmpB"][0:1, 0:512],
                                       crow[0:1, h:h + 1], op=ALU.add)
                v.drain()
                v.reciprocal(sb["tmpB"][0:1, 0:512], sb["tmpB"][0:1, 0:512])
                p.mile(v, "sV")
                p.wait(t, "sV")
                p.guard_wait(t, pBn)
                mi = t.matmul(pB[:], sb["ones_row"][:], sb["tmpB"][0:1, 0:512],
                              start=True, stop=True)
                p.inc(mi, "sP", 1)
                p.guard_set(("lrow",), "sP")
                p.wait(a, "sP")
                p.guard_wait(a, ("rlb",))
                a.copy(sb["tmpA"][0:64, 0:512], pB[0:64, :])
                p.mile(a, "sA")
                p.wait(v, "sA")
                v.tensor_tensor(arow[:], pO[0:64, :], sb["tmpA"][0:64, 0:512], op=ALU.mult)
                p.mile(v, "sV")
                p.guard_set(pOn, "sV")
                p.guard_set(pBn, "sV")
                p.guard_set(("rlb",), "sV")
                p.wait(s, "sV")
                p.guard_wait(s, "arow")
                di = s.dma_start(out=io["attn_dr"][h * 64:(h + 1) * 64,
                                                   qg * 512:(qg + 1) * 512],
                                 in_=arow[:])
                p.inc(di, "dS", 16)
                p.guard_set("arow", "dS")

        with ExitStack() as st2:
            wo = st2.enter_context(nc.sbuf_tensor("a1_wo", [128, 4 * 1024], BF16))
            astream = st2.enter_context(nc.sbuf_tensor("a1_ast", [128, 4 * 512], BF16))
            stg2 = st2.enter_context(nc.sbuf_tensor("a1_stg2", [128, 2 * 512], BF16))
            p.fence(g)
            p.inc(g.dma_start(out=wo[:].rearrange("p (c o) -> p c o", c=4),
                              in_=io["l1_wo"].rearrange("(c p) o -> p c o", p=128)), "dG", 16)
            ash = io["attn_dr"].rearrange("(c p) t -> p c t", p=128)
            p.wait(a, "dS")
            pr = [ps["A"], ps["B"]]
            prn = ["psA", "psB"]
            psh = io["partial_loc"].rearrange("(c p) t -> p c t", p=128)
            for tg in range(NT):
                p.guard_wait(g, "astream")
                di = g.dma_start(out=astream[:].rearrange("p (c t2) -> p c t2", c=4),
                                 in_=ash[:, :, tg * 512:(tg + 1) * 512])
                nst_ready = p.inc(di, "dG", 16)
                p.wait(t, "dG", nst_ready)
                for oc in range(NCH):
                    bank = pr[oc % 2]
                    p.guard_wait(t, prn[oc % 2])
                    for kc in range(4):
                        mi = t.matmul(bank[:],
                                      wo[:, kc * 1024 + oc * 128:kc * 1024 + (oc + 1) * 128],
                                      astream[:, kc * 512:(kc + 1) * 512],
                                      start=(kc == 0), stop=(kc == 3))
                    p.inc(mi, "sP", 1)
                    p.wait(a, "sP")
                    p.guard_wait(a, ("pstg", oc % 2))
                    a.copy(stg2[:, (oc % 2) * 512:(oc % 2 + 1) * 512], bank[:])
                    p.mile(a, "sA")
                    p.guard_set(prn[oc % 2], "sA")
                    p.wait(s, "sA")
                    di = s.dma_start(out=psh[:, oc:oc + 1, tg * 512:(tg + 1) * 512],
                                     in_=stg2[:, (oc % 2) * 512:(oc % 2 + 1) * 512]
                                     .rearrange("p (one t) -> p one t", one=1))
                    p.inc(di, "dS", 16)
                    p.guard_set(("pstg", oc % 2), "dS")
                p.guard_set("astream", "sP")
        p.wait(s, "dS")
        shp = io["sh_partial1"].rearrange("(sl r) t -> sl r t", sl=2)
        di = s.dma_start(out=shp[SL, :, :],
                         in_=io["partial_loc"].rearrange("(one r) t -> one r t", one=1))
        p.inc(di, "dS", 16)
        emit_exchange_flags(p, sb, io, 3, SL)
        with ExitStack() as st2:
            rd = st2.enter_context(nc.sbuf_tensor("a1_rd", [128, NCH * 2048], BF16))
            psh = io["sh_partial1"].rearrange("(sl c p) (hf t2) -> sl p c hf t2",
                                              sl=2, p=128, hf=2)
            p.fence(s)
            for slot in range(2):
                p.wait(s, "sG")
                p.guard_wait(s, "rd")
                di = s.dma_start(out=rd[:].rearrange("p (c one t) -> p c one t",
                                                     c=NCH, one=1),
                                 in_=psh[slot][:, :, SL, :])
                rd_ready = p.inc(di, "dS", 16)
                p.wait(v, "dS", rd_ready)
                for c in range(NCH):
                    for tg in range(4):
                        xs = x[:, c * TOK + tg * 512:c * TOK + (tg + 1) * 512]
                        rds = rd[:, c * 2048 + tg * 512:c * 2048 + (tg + 1) * 512]
                        v.tensor_tensor(xs, xs, rds, op=ALU.add)
                p.mile(v, "sV")
                p.guard_set("rd", "sV")


def emit_norm_update(p, sb, ps, stg, nrm8, col, tg, nchunks, selname):
    nc = p.nc
    t, v, a = nc.tensor, nc.vector, nc.scalar
    stats1 = ps["D"]
    p.wait(a, "sV")
    p.guard_wait(t, "psD")
    for oc in range(nchunks):
        p.guard_wait(a, ("sqB", 0))
        a.activation(sb["tmpB"][:, 0:512], stg[:, oc * 512:(oc + 1) * 512], AF.Square)
        p.mile(a, "sA")
        p.wait(t, "sA")
        mi = t.matmul(stats1[0:8, :], sb[selname][:, oc * 8:(oc + 1) * 8],
                      sb["tmpB"][:, 0:512], start=(oc == 0), stop=(oc == nchunks - 1))
        p.inc(mi, "sP", 1)
        p.guard_set(("sqB", 0), "sP")
    p.wait(v, "sP")
    v.tensor_reduce(sb["tmpA"][0:8, 0:1], stats1[0:8, :], axis=AXX, op=ALU.max)
    v.drain()
    if tg == 0:
        v.tensor_copy(nrm8[0:8, col:col + 1], sb["tmpA"][0:8, 0:1])
    else:
        v.tensor_tensor(nrm8[0:8, col:col + 1], nrm8[0:8, col:col + 1],
                        sb["tmpA"][0:8, 0:1], op=ALU.max)
    p.mile(v, "sV")
    p.guard_set("psD", "sV")


def emit_postamble(p, io, sb):
    nc = p.nc
    s = nc.sync
    p.wait(s, "sV")
    p.wait(s, "sA")
    p.wait(s, "sP")
    di = s.dma_start(out=io["x_out"].rearrange("(c p) t -> p c t", p=128),
                     in_=sb["x"][:].rearrange("p (c t) -> p c t", c=NCH))
    p.inc(di, "dS", 16)
    if "dbg_rows" in io:
        for i, nm in enumerate(("row_m", "row_r", "row_mr", "row_t")):
            p.inc(s.dma_start(out=io["dbg_rows"][i:i + 1, :], in_=sb[nm][0:1, :]), "dS", 16)
        p.inc(s.dma_start(out=io["dbg_gates"], in_=sb["gates"][0:8, :]), "dS", 16)
        p.inc(s.dma_start(out=io["dbg_normed"].rearrange("(c p) t -> p c t", p=128),
                          in_=sb["normed"][:].rearrange("p (c t) -> p c t", c=NCH)), "dS", 16)
        if "dbg_scr" in sb:
            p.inc(s.dma_start(out=io["dbg_scr"], in_=sb["dbg_scr"][:]), "dS", 16)
    p.wait(s, "dS")



def build_program(plan):
    nc = bass.Bass("TRN2", target_bir_lowering=False)
    io = {}
    io["x_in"] = nc.dram_tensor("x_in", [D, TOK], F32, kind="ExternalInput").ap()
    io["x_out"] = nc.dram_tensor("x_out", [D, TOK], F32, kind="ExternalOutput").ap()
    io["par"] = nc.dram_tensor("par", [1, 1], I32, kind="ExternalInput").ap()
    io["ones1"] = nc.dram_tensor("ones1", [128, 1], F32, kind="ExternalInput").ap()
    io["ones_row"] = nc.dram_tensor("ones_row", [1, 128], F32, kind="ExternalInput").ap()
    io["eye_f"] = nc.dram_tensor("eye_f", [128, 128], F32, kind="ExternalInput").ap()
    io["iotaB"] = nc.dram_tensor("iotaB", [128, 8], F32, kind="ExternalInput").ap()
    io["sel8"] = nc.dram_tensor("sel8", [8, 8 * 128], F32, kind="ExternalInput").ap()
    io["selH"] = nc.dram_tensor("selH", [128, 32], F32, kind="ExternalInput").ap()
    io["selT"] = nc.dram_tensor("selT", [8, 4 * 128], F32, kind="ExternalInput").ap()
    io["selHr"] = nc.dram_tensor("selHr", [128, 16], F32, kind="ExternalInput").ap()
    io["trimask"] = nc.dram_tensor("trimask", [128, 128], F32, kind="ExternalInput").ap()
    io["nonces"] = nc.dram_tensor("nonces", [8, 1], I32, kind="ExternalInput").ap()
    if os.environ.get("MK_DEBUG"):
        io["dbg_rows"] = nc.dram_tensor("dbg_rows", [4, TOK], F32, kind="ExternalOutput").ap()
        io["dbg_gates"] = nc.dram_tensor("dbg_gates", [8, TOK], F32, kind="ExternalOutput").ap()
        io["dbg_normed"] = nc.dram_tensor("dbg_normed", [D, TOK], BF16, kind="ExternalOutput").ap()
        io["dbg_scr"] = nc.dram_tensor("dbg_scr", [128, 320], F32, kind="ExternalOutput").ap()
    has_moe = any(sl in plan for sl in ("moe0", "moe1"))
    if "attn0" in plan:
        io["l0_wq"] = nc.dram_tensor("l0_wq", [D, 512], F32, kind="ExternalInput").ap()
        io["l0_wk"] = nc.dram_tensor("l0_wk", [D, 512], F32, kind="ExternalInput").ap()
        io["l0_wv"] = nc.dram_tensor("l0_wv", [D, 512], F32, kind="ExternalInput").ap()
        io["l0_wo"] = nc.dram_tensor("l0_wo", [512, D], F32, kind="ExternalInput").ap()
        io["qln_g"] = nc.dram_tensor("qln_g", [128, 1], F32, kind="ExternalInput").ap()
        io["qln_b"] = nc.dram_tensor("qln_b", [128, 1], F32, kind="ExternalInput").ap()
        io["kln_g"] = nc.dram_tensor("kln_g", [128, 1], F32, kind="ExternalInput").ap()
        io["kln_b"] = nc.dram_tensor("kln_b", [128, 1], F32, kind="ExternalInput").ap()
        io["l0_bo_cols"] = nc.dram_tensor("l0_bo_cols", [128, 8], F32, kind="ExternalInput").ap()
        io["sh_normed0"] = nc.dram_tensor("sh_normed0", [D, T], BF16,
                                          kind="Internal", addr_space="Shared").ap()
        io["sh_partial0"] = nc.dram_tensor("sh_partial0", [2 * D, T], BF16,
                                           kind="Internal", addr_space="Shared").ap()
        io["q_dr"] = nc.dram_tensor("q_dr", [512, T], BF16, kind="Internal").ap()
        io["normed_loc"] = nc.dram_tensor("normed_loc", [D, TOK], BF16, kind="Internal").ap()
        io["partial_loc"] = nc.dram_tensor("partial_loc", [D, T], BF16, kind="Internal").ap()
        io["k_dr"] = nc.dram_tensor("k_dr", [512, T], BF16, kind="Internal").ap()
        io["attn_dr"] = nc.dram_tensor("attn_dr", [512, T], BF16, kind="Internal").ap()
    if any(sl in plan for sl in ("ffn1a", "ffn1b")):
        io["l1fw"] = nc.dram_tensor("l1fw", [D, D], F32, kind="ExternalInput").ap()
        io["l1fb_cols"] = nc.dram_tensor("l1fb_cols", [128, 8], F32, kind="ExternalInput").ap()
    if "attn1" in plan:
        io["l1_wq"] = nc.dram_tensor("l1_wq", [D, 512], F32, kind="ExternalInput").ap()
        io["l1_wqr_cat"] = nc.dram_tensor("l1_wqr_cat", [D, 512], F32, kind="ExternalInput").ap()
        io["l1_wkvd"] = nc.dram_tensor("l1_wkvd", [D, 256], F32, kind="ExternalInput").ap()
        io["l1_wkr_cat"] = nc.dram_tensor("l1_wkr_cat", [D, 64], F32, kind="ExternalInput").ap()
        io["l1_wku"] = nc.dram_tensor("l1_wku", [256, 512], F32, kind="ExternalInput").ap()
        io["l1_wvu"] = nc.dram_tensor("l1_wvu", [256, 512], F32, kind="ExternalInput").ap()
        io["l1_wo"] = nc.dram_tensor("l1_wo", [512, D], F32, kind="ExternalInput").ap()
        io["lobo"] = nc.dram_tensor("lobo", [1, 8], F32, kind="ExternalInput").ap()
        io["ropecos"] = nc.dram_tensor("ropecos", [128, T], F32, kind="ExternalInput").ap()
        io["ropesin"] = nc.dram_tensor("ropesin", [128, T], F32, kind="ExternalInput").ap()
        io["sh_normed1"] = nc.dram_tensor("sh_normed1", [D, T], BF16,
                                          kind="Internal", addr_space="Shared").ap()
        io["sh_partial1"] = nc.dram_tensor("sh_partial1", [2 * D, T], BF16,
                                           kind="Internal", addr_space="Shared").ap()
        if "attn0" not in plan:
            io["q_dr"] = nc.dram_tensor("q_dr", [512, T], BF16, kind="Internal").ap()
            io["k_dr"] = nc.dram_tensor("k_dr", [512, T], BF16, kind="Internal").ap()
            io["attn_dr"] = nc.dram_tensor("attn_dr", [512, T], BF16, kind="Internal").ap()
            io["normed_loc"] = nc.dram_tensor("normed_loc", [D, TOK], BF16, kind="Internal").ap()
            io["partial_loc"] = nc.dram_tensor("partial_loc", [D, T], BF16, kind="Internal").ap()
        io["qr_dr"] = nc.dram_tensor("qr_dr", [256, T], BF16, kind="Internal").ap()
    if any(sl.startswith("attn") for sl in plan):
        io["sh_flags"] = nc.dram_tensor("sh_flags", [16, 1], I32,
                                        kind="Internal", addr_space="Shared").ap()
    if has_moe:
        io["gate_wT"] = nc.dram_tensor("gate_wT", [D, 8], F32, kind="ExternalInput").ap()
        io["w1"] = nc.dram_tensor("w1", [NEXP, D, HID], F32, kind="ExternalInput").ap()
        io["w2"] = nc.dram_tensor("w2", [NEXP, HID, D], F32, kind="ExternalInput").ap()
        io["w3"] = nc.dram_tensor("w3", [NEXP, D, HID], F32, kind="ExternalInput").ap()

    with ExitStack() as ctx:
        p = Prog(nc, ctx)
        for name in ("dS", "dG", "sP", "sV", "sA", "sG"):
            p.sem(name)
        sb = {}

        def alloc(name, shape, dtype):
            sb[name] = ctx.enter_context(nc.sbuf_tensor("sb_" + name, shape, dtype))
            return sb[name]

        alloc("x", [128, NCH * TOK], F32)
        alloc("ones1", [128, 1], F32)
        alloc("ones_row", [1, 128], F32)
        alloc("eye_f", [128, 128], F32)
        alloc("eye_b", [128, 128], BF16)
        alloc("iotaB", [128, 8], F32)
        alloc("sel8", [8, 8 * 128], F32)
        has_attn = any(sl.startswith("attn") for sl in plan)
        alloc("par", [1, 1], I32)
        alloc("nonces", [8, 1], I32)
        alloc("row_m", [1, TOK], F32)
        alloc("row_r", [1, TOK], F32)
        alloc("row_mr", [1, TOK], F32)
        alloc("row_t", [1, TOK], F32)
        alloc("mr_cols", [128, 32], F32)
        alloc("tmpA", [128, 512], F32)
        alloc("tmpB", [128, 512], F32)
        if os.environ.get("MK_DEBUG"):
            alloc("dbg_scr", [128, 320], F32)

        ps = {}
        for nm in ("A", "B", "C", "D", "E", "F"):
            ps[nm] = ctx.enter_context(nc.psum_tensor(f"ps{nm}", [128, 512], F32))
        ps["H"] = ctx.enter_context(nc.psum_tensor("psH", [128, 1024], BF16))

        with nc.Block() as block:
            emit_preamble(p, io, sb)
            for name in plan:
                for eng in (nc.tensor, nc.vector, nc.scalar, nc.sync, nc.gpsimd):
                    p.fence(eng)
                if name in ("moe0", "moe1"):
                    emit_moe_sublayer(p, sb, ps, io, name)
                elif name == "attn0":
                    emit_attn0(p, sb, ps, io, ctx)
                elif name in ("ffn1a", "ffn1b"):
                    emit_ffn1(p, sb, ps, io, name)
                elif name == "attn1":
                    emit_attn1(p, sb, ps, io, ctx)
                else:
                    raise NotImplementedError(name)
            emit_postamble(p, io, sb)
    return nc, io



def _host_constants(core):
    iota = np.arange(8, dtype=np.float32)
    sel8 = np.zeros((8, 8 * 128), np.float32)
    for e in range(8):
        sel8[e, e * 128:(e + 1) * 128] = 1.0
    selH = np.zeros((128, 32), np.float32)
    selT = np.zeros((8, 4 * 128), np.float32)
    for c in range(4):
        for pp in range(128):
            h = 2 * c + pp // 64
            selH[pp, c * 8 + h] = 1.0
            selT[h, c * 128 + pp] = 1.0
    selHr = np.zeros((128, 16), np.float32)
    for c in range(2):
        for pp in range(128):
            h = 4 * c + pp // 32
            selHr[pp, c * 8 + h] = 1.0
    qp, kp = np.meshgrid(np.arange(128), np.arange(128), indexing="xy")
    trimask = np.where(kp <= qp, 0.0, NEGINF).astype(np.float32)
    return {
        "par": np.array([[core % 2]], np.int32),
        "ones1": np.ones((128, 1), np.float32),
        "ones_row": np.ones((1, 128), np.float32),
        "eye_f": np.eye(128, dtype=np.float32),
        "iotaB": np.broadcast_to((iota - BIG)[None, :], (128, 8)).copy(),
        "sel8": sel8,
        "selH": selH,
        "selT": selT,
        "selHr": selHr,
        "trimask": trimask,
    }


def kernel(**inputs):
    plan_env = os.environ.get("MK_PLAN", "fused")
    if plan_env == "unfused":
        plans = [[sl] for sl in SUBLAYERS]
    elif plan_env == "fused":
        plans = [SUBLAYERS]
    elif plan_env.startswith("pfx:"):
        k = int(plan_env.split(":")[1])
        plans = [[sl] for sl in SUBLAYERS[:k]]
    else:
        plans = [[sl] for sl in plan_env.split(",")]

    x = np.asarray(inputs["L0_attn_x"])
    B = x.shape[0]
    ncores = 8
    xs = [np.ascontiguousarray(x[c // 2, (c % 2) * TOK:(c % 2 + 1) * TOK, :].T)
          for c in range(ncores)]
    consts = [_host_constants(c) for c in range(ncores)]
    moe_w = {
        "gate_wT": np.ascontiguousarray(np.asarray(inputs["L0_ffn_gate_w"]).T),
        "w1": np.ascontiguousarray(np.asarray(inputs["L0_ffn_w1"])),
        "w2": np.ascontiguousarray(np.asarray(inputs["L0_ffn_w2"])),
        "w3": np.ascontiguousarray(np.asarray(inputs["L0_ffn_w3"])),
    }
    attn0_w = []
    for c in range(ncores):
        h0 = (c % 2) * 8
        rs = slice(h0 * 64, (h0 + 8) * 64)
        attn0_w.append({
            "l0_wq": np.ascontiguousarray(np.asarray(inputs["L0_attn_Wq"])[rs, :].T),
            "l0_wk": np.ascontiguousarray(np.asarray(inputs["L0_attn_Wk"])[rs, :].T),
            "l0_wv": np.ascontiguousarray(np.asarray(inputs["L0_attn_Wv"])[rs, :].T),
            "l0_wo": np.ascontiguousarray(np.asarray(inputs["L0_attn_Wo"])[:, rs].T),
            "qln_g": np.ascontiguousarray(np.tile(np.asarray(inputs["L0_attn_qln_g"]), 2)[:, None]),
            "qln_b": np.ascontiguousarray(np.tile(np.asarray(inputs["L0_attn_qln_b"]), 2)[:, None]),
            "kln_g": np.ascontiguousarray(np.tile(np.asarray(inputs["L0_attn_kln_g"]), 2)[:, None]),
            "kln_b": np.ascontiguousarray(np.tile(np.asarray(inputs["L0_attn_kln_b"]), 2)[:, None]),
            "l0_bo_cols": np.ascontiguousarray(np.asarray(inputs["L0_attn_bo"]).reshape(8, 128).T),
        })

    def _rope_tables():
        half = 16
        inv = (1.0 / (np.float32(10000.0) ** (np.arange(half, dtype=np.float32)
                                              / np.float32(half)))).astype(np.float32)
        pos = np.arange(T, dtype=np.float32)
        ang = (pos[:, None] * inv[None, :]).astype(np.float32)
        cos = np.cos(ang).astype(np.float32)
        sin = np.sin(ang).astype(np.float32)
        cos128 = np.empty((128, T), np.float32)
        sin128 = np.empty((128, T), np.float32)
        for j in range(128):
            b = j % 32
            if b < half:
                cos128[j] = cos[:, b]
                sin128[j] = -sin[:, b]
            else:
                cos128[j] = cos[:, b - half]
                sin128[j] = sin[:, b - half]
        return cos128, sin128

    ropecos, ropesin = _rope_tables()

    def _swap_qr(w, dh):
        w = w.reshape(w.shape[0], -1, dh)
        return np.concatenate([w[:, :, dh // 2:], w[:, :, :dh // 2]], axis=2).reshape(
            w.shape[0], -1)

    attn1_w = []
    for c in range(ncores):
        h0 = (c % 2) * 8
        wq = np.asarray(inputs["L1_attn_Wq"])[:, h0 * 64:(h0 + 8) * 64]
        wqr_all = np.asarray(inputs["L1_attn_Wqr"])
        wqr = wqr_all[:, h0 * 32:(h0 + 8) * 32]
        wqr_cat = np.concatenate([wqr, _swap_qr(wqr, 32)], axis=1)
        wkr = np.asarray(inputs["L1_attn_Wkr"])
        wkr_cat = np.concatenate([wkr, _swap_qr(wkr, 32)], axis=1)
        attn1_w.append({
            "l1_wq": np.ascontiguousarray(wq),
            "l1_wqr_cat": np.ascontiguousarray(wqr_cat),
            "l1_wkvd": np.ascontiguousarray(np.asarray(inputs["L1_attn_Wkvd"])),
            "l1_wkr_cat": np.ascontiguousarray(wkr_cat),
            "l1_wku": np.ascontiguousarray(np.asarray(inputs["L1_attn_Wku"])[:, h0 * 64:(h0 + 8) * 64]),
            "l1_wvu": np.ascontiguousarray(np.asarray(inputs["L1_attn_Wvu"])[:, h0 * 64:(h0 + 8) * 64]),
            "l1_wo": np.ascontiguousarray(np.asarray(inputs["L1_attn_Wo"])[h0 * 64:(h0 + 8) * 64, :]),
            "lobo": np.ascontiguousarray(np.asarray(inputs["L1_attn_lobo_log"])[h0:h0 + 8][None, :]),
            "ropecos": ropecos,
            "ropesin": ropesin,
        })

    ffn1_w = {
        "l1fw": np.ascontiguousarray(np.asarray(inputs["L1_ffn_weight"])),
        "l1fb_cols": np.ascontiguousarray(np.asarray(inputs["L1_ffn_bias"]).reshape(8, 128).T),
    }
    rng = np.random.RandomState()
    for plan in plans:
        nc, _io = build_program(plan)
        nonces = rng.randint(1, 2 ** 30, size=(8, 1)).astype(np.int32)
        in_maps = []
        for c in range(ncores):
            m = {"x_in": xs[c], "nonces": nonces}
            m.update(consts[c])
            if any(sl in plan for sl in ("moe0", "moe1")):
                m.update(moe_w)
            if "attn0" in plan:
                m.update(attn0_w[c])
            if "ffn1a" in plan or "ffn1b" in plan:
                m.update(ffn1_w)
            if "attn1" in plan:
                m.update(attn1_w[c])
            in_maps.append(m)
        res = run_bass_kernel_spmd(nc, in_maps, core_ids=list(range(ncores)))
        xs = [np.asarray(res.results[c]["x_out"]) for c in range(ncores)]

    out = np.empty((B, T, D), np.float32)
    for c in range(ncores):
        out[c // 2, (c % 2) * TOK:(c % 2 + 1) * TOK, :] = xs[c].T
    return out


# version 03
# speedup vs baseline: 1.6200x; 1.6200x over previous
import os
import numpy as np
import ml_dtypes
from contextlib import ExitStack

import concourse.bass as bass
import concourse.mybir as mybir
from concourse.bass_utils import run_bass_kernel_spmd

F32 = mybir.dt.float32
BF16 = mybir.dt.bfloat16
I32 = mybir.dt.int32
AF = mybir.ActivationFunctionType
ALU = mybir.AluOpType
AXX = mybir.AxisListType.X

D = 1024
TOK = 2048
T = 4096
NCH = D // 128
HID = 2816
NEXP = 8
NH = 8
DH = 64
EPS = 1e-5
BIG = 1.0e6
NEGINF = -1.0e30
SCALE0 = float(np.float32(1024.0 ** (-0.25)))
SCALE1 = float(1.0 / np.sqrt(96.0))

SUBLAYERS = ["moe0", "attn0", "moe1", "ffn1a", "attn1", "ffn1b"]


class Prog:
    def __init__(self, nc, ctx):
        self.nc = nc
        self.ctx = ctx
        self._sems = {}
        self._cnt = {}
        self._guard = {}

    def sem(self, name):
        h = self.ctx.enter_context(self.nc.semaphore(name))
        self._sems[name] = h
        self._cnt[name] = 0
        return h

    def inc(self, instr, name, n):
        self._cnt[name] += n
        instr.then_inc(self._sems[name], n)
        return self._cnt[name]

    def mile(self, eng, name):
        self._cnt[name] += 1
        eng.drain().then_inc(self._sems[name], 1)
        return self._cnt[name]

    def cnt(self, name):
        return self._cnt[name]

    def wait(self, eng, name, val=None):
        v = self._cnt[name] if val is None else val
        if v > 0:
            eng.wait_ge(self._sems[name], v)

    def fence(self, eng):
        for nm in ("sP", "sV", "sA", "dS", "dG"):
            self.wait(eng, nm)

    def guard_set(self, res, semname, val=None):
        v = self._cnt[semname] if val is None else val
        self._guard[res] = [(semname, v)]

    def guard_wait(self, eng, res):
        for semname, v in self._guard.pop(res, []):
            if v > 0:
                eng.wait_ge(self._sems[semname], v)



def emit_preamble(p, io, sb):
    nc = p.nc
    s = nc.sync
    g = nc.gpsimd
    p.inc(s.dma_start(out=sb["x"][:].rearrange("p (c t) -> p c t", c=NCH),
                      in_=io["x_in"].rearrange("(c p) t -> p c t", p=128)), "dS", 16)
    for nm in ("ones1", "ones_row", "eye_f", "iotaB", "sel8", "par", "nonces"):
        if nm in sb:
            p.inc(s.dma_start(out=sb[nm][:], in_=io[nm]), "dS", 16)
    p.inc(g.dma_start(out=sb["eye_b"][:], in_=io["eye_f"]), "dG", 16)


def emit_ln(p, sb, ps, to_dram=None, stats_only=False):

    nc = p.nc
    t, v, a = nc.tensor, nc.vector, nc.scalar
    x = sb["x"]
    p.wait(a, "dS")
    p.wait(t, "dS")
    p.wait(v, "dS")
    SLOC = [(ps["A"], 0), (ps["A"], 32), (ps["A"], 64), (ps["C"], 0)]
    SLOC2 = [(ps["B"], 0), (ps["B"], 32), (ps["B"], 64), (ps["F"], 0)]
    for nm in ("psA", "psB", "psC", "psF"):
        p.guard_wait(t, nm)
    for tg in range(4):
        bA, base = SLOC[tg]
        bB, base2 = SLOC2[tg]
        for c in range(NCH):
            xs = x[:, c * TOK + tg * 512: c * TOK + (tg + 1) * 512]
            slot = c % 2
            sqb = sb["tmpA"] if slot == 0 else sb["tmpB"]
            p.guard_wait(a, ("sq", slot))
            a.activation(sqb[:, 0:512], xs, AF.Square)
            p.mile(a, "sA")
            p.wait(t, "sA")
            t.matmul(bA[base:base + 1, :], sb["ones1"][:], xs,
                     start=(c == 0), stop=(c == NCH - 1))
            mi = t.matmul(bB[base2:base2 + 1, :], sb["ones1"][:],
                          sqb[:, 0:512],
                          start=(c == 0), stop=(c == NCH - 1))
            p.inc(mi, "sP", 1)
            p.guard_set(("sq", slot), "sP")
    p.wait(v, "sP")
    for tg in range(4):
        bA, base = SLOC[tg]
        bB, base2 = SLOC2[tg]
        sl = slice(tg * 512, (tg + 1) * 512)
        v.tensor_scalar_mul(sb["row_m"][0:1, sl], bA[base:base + 1, :], 1.0 / D)
        v.tensor_scalar_mul(sb["row_t"][0:1, sl], bB[base2:base2 + 1, :], 1.0 / D)
    p.mile(v, "sV")
    for nm in ("psA", "psB", "psC", "psF"):
        p.guard_set(nm, "sV")
    v.tensor_tensor(sb["row_r"][0:1, :], sb["row_m"][0:1, :], sb["row_m"][0:1, :], op=ALU.mult)
    v.drain()
    v.tensor_tensor(sb["row_t"][0:1, :], sb["row_t"][0:1, :], sb["row_r"][0:1, :], op=ALU.subtract)
    v.drain()
    v.tensor_scalar_add(sb["row_t"][0:1, :], sb["row_t"][0:1, :], EPS)
    p.mile(v, "sV")
    p.wait(a, "sV")
    a.activation(sb["row_mr"][0:1, :], sb["row_t"][0:1, :], AF.Ln)
    a.drain()
    a.activation(sb["row_r"][0:1, :], sb["row_mr"][0:1, :], AF.Exp, scale=-0.5)
    p.mile(a, "sA")
    p.wait(v, "sA")
    v.tensor_tensor(sb["row_mr"][0:1, :], sb["row_m"][0:1, :], sb["row_r"][0:1, :], op=ALU.mult)
    p.mile(v, "sV")
    p.wait(t, "sV")
    mrp = ps["C"]
    p.guard_wait(t, "psC")
    for tile in range(16):
        t.transpose(mrp[:, tile:tile + 1],
                    sb["row_m"][0:1, tile * 128:(tile + 1) * 128], sb["eye_f"][0:1, 0:1])
        ti = t.transpose(mrp[:, 16 + tile:17 + tile],
                         sb["row_r"][0:1, tile * 128:(tile + 1) * 128], sb["eye_f"][0:1, 0:1])
    p.inc(ti, "sP", 1)
    p.wait(v, "sP")
    v.tensor_copy(sb["mr_cols"][:], mrp[:, 0:32])
    p.mile(v, "sV")
    p.guard_set("psC", "sV")
    if stats_only:
        return
    rbb, mrb = ps["D"], ps["E"]
    s = nc.sync
    for tg in range(4):
        sl = slice(tg * 512, (tg + 1) * 512)
        p.guard_wait(t, "psD")
        p.guard_wait(t, "psE")
        t.matmul(rbb[:], sb["ones_row"][:], sb["row_r"][0:1, sl], start=True, stop=True)
        mi = t.matmul(mrb[:], sb["ones_row"][:], sb["row_mr"][0:1, sl], start=True, stop=True)
        p.inc(mi, "sP", 1)
        p.wait(v, "sP")
        p.guard_wait(v, ("sq", 0))
        if to_dram is not None:
            p.guard_wait(v, "nstage")
        for c in range(NCH):
            xs = x[:, c * TOK + tg * 512: c * TOK + (tg + 1) * 512]
            if to_dram is None:
                ns = sb["normed"][:, c * TOK + tg * 512: c * TOK + (tg + 1) * 512]
            else:
                ns = to_dram[1][:, c * 512:(c + 1) * 512]
            v.tensor_tensor(sb["tmpA"][:, 0:512], xs, rbb[:], op=ALU.mult)
            v.drain()
            v.tensor_tensor(ns, sb["tmpA"][:, 0:512], mrb[:], op=ALU.subtract)
            v.drain()
        p.mile(v, "sV")
        p.guard_set("psD", "sV")
        p.guard_set("psE", "sV")
        if to_dram is not None:
            loc_ap, stage = to_dram
            p.wait(s, "sV")
            dst = loc_ap.rearrange("(c p) (g t2) -> p c g t2", p=128, t2=512)
            di = s.dma_start(out=dst[:, :, tg, :],
                             in_=stage[:].rearrange("p (c t2) -> p c t2", c=NCH))
            p.inc(di, "dS", 16)
            p.guard_set("nstage", "dS")


def emit_moe_sublayer(p, sb, ps, io, tag):
    nc = p.nc
    with ExitStack() as st:
        def A(name, shape, dtype):
            t_ = st.enter_context(nc.sbuf_tensor(f"moe{tag}_{name}", shape, dtype))
            sb[name] = t_
            return t_
        A("normed", [128, NCH * TOK], BF16)
        A("gwT", [128, NCH * 8], F32)
        A("gsum_row", [1, 8], F32)
        A("gates", [8, TOK], F32)
        A("g_b", [128, 1024], F32)
        A("h", [128, 12 * 1024], BF16)
        sb["w1g"] = [st.enter_context(nc.sbuf_tensor(f"moe{tag}_w1g{i}", [128, NCH * 128], BF16))
                     for i in range(2)]
        sb["w3g"] = [st.enter_context(nc.sbuf_tensor(f"moe{tag}_w3g{i}", [128, NCH * 128], BF16))
                     for i in range(2)]
        sb["w2s"] = [st.enter_context(nc.sbuf_tensor(f"moe{tag}_w2s0", [128, 12 * 1024], BF16))]
        emit_ln(p, sb, ps)
        emit_moe(p, sb, ps, io)


def emit_moe(p, sb, ps, io):
    nc = p.nc
    t, v, a, g = nc.tensor, nc.vector, nc.scalar, nc.gpsimd
    s = nc.sync
    x = sb["x"]
    p.guard_wait(s, "gwT")
    p.inc(s.dma_start(out=sb["gwT"][:].rearrange("p (c e) -> p c e", c=NCH),
                      in_=io["gate_wT"].rearrange("(c p) e -> p c e", p=128)),
          "dS", 16)
    p.wait(t, "dS")
    bk = ps["A"]
    p.guard_wait(t, "psA")
    for c in range(NCH):
        mi = t.matmul(bk[0:1, 0:8], sb["ones1"][:], sb["gwT"][:, c * 8:(c + 1) * 8],
                      start=(c == 0), stop=(c == NCH - 1))
    p.inc(mi, "sP", 1)
    p.wait(v, "sP")
    v.tensor_copy(sb["gsum_row"][0:1, 0:8], bk[0:1, 0:8])
    v.tensor_scalar_mul(sb["row_t"][0:1, :], sb["row_m"][0:1, :], -1.0)
    p.mile(v, "sV")
    p.guard_set("psA", "sV")
    p.wait(t, "sV")
    scp = ps["C"]
    gtp = ps["F"]
    w = sb["tmpA"]
    for tile in range(16):
        p.guard_wait(t, "psC")
        if tile % 4 == 0:
            p.guard_wait(t, "psF")
        for c in range(NCH):
            xs = x[:, c * TOK + tile * 128:c * TOK + (tile + 1) * 128]
            t.matmul(scp[0:128, 0:8], xs, sb["gwT"][:, c * 8:(c + 1) * 8],
                     start=(c == 0), stop=False)
        mi = t.matmul(scp[0:128, 0:8], sb["row_t"][0:1, tile * 128:(tile + 1) * 128],
                      sb["gsum_row"][0:1, 0:8], start=False, stop=True)
        p.inc(mi, "sP", 1)
        p.wait(v, "sP")
        r_col = sb["mr_cols"][:, 16 + tile:17 + tile]
        v.tensor_scalar_mul(w[:, 0:8], scp[0:128, 0:8], r_col)
        p.mile(v, "sV")
        p.guard_set("psC", "sV")
        v.tensor_reduce(w[:, 16:17], w[:, 0:8], axis=AXX, op=ALU.max)
        v.drain()
        v.tensor_scalar_mul(w[:, 17:18], w[:, 16:17], -1.0)
        p.mile(v, "sV")
        p.wait(a, "sV")
        a.activation(w[:, 24:32], w[:, 0:8], AF.Exp, bias=w[:, 17:18])
        p.mile(a, "sA")
        p.wait(v, "sA")
        v.tensor_reduce(w[:, 18:19], w[:, 24:32], axis=AXX, op=ALU.add)
        v.drain()
        v.reciprocal(w[:, 19:20], w[:, 18:19])
        v.drain()
        v.tensor_scalar_mul(w[:, 32:40], w[:, 24:32], w[:, 19:20])
        v.drain()
        v.tensor_reduce(w[:, 16:17], w[:, 32:40], axis=AXX, op=ALU.max)
        v.drain()
        v.tensor_single_scalar(w[:, 8:16], w[:, 32:40], w[:, 16:17], op=ALU.is_equal)
        v.drain()
        v.tensor_tensor(w[:, 8:16], w[:, 8:16], sb["iotaB"][:], op=ALU.mult)
        v.drain()
        v.tensor_reduce(w[:, 16:17], w[:, 8:16], axis=AXX, op=ALU.min)
        v.drain()
        v.tensor_single_scalar(w[:, 40:48], sb["iotaB"][:], w[:, 16:17], op=ALU.is_equal)
        v.drain()
        v.scalar_tensor_tensor(w[:, 48:56], w[:, 40:48], -BIG, w[:, 32:40],
                               op0=ALU.mult, op1=ALU.add)
        v.drain()
        v.tensor_reduce(w[:, 16:17], w[:, 48:56], axis=AXX, op=ALU.max)
        v.drain()
        v.tensor_single_scalar(w[:, 8:16], w[:, 48:56], w[:, 16:17], op=ALU.is_equal)
        v.drain()
        v.tensor_tensor(w[:, 8:16], w[:, 8:16], sb["iotaB"][:], op=ALU.mult)
        v.drain()
        v.tensor_reduce(w[:, 16:17], w[:, 8:16], axis=AXX, op=ALU.min)
        v.drain()
        v.tensor_single_scalar(w[:, 8:16], sb["iotaB"][:], w[:, 16:17], op=ALU.is_equal)
        v.drain()
        v.tensor_tensor(w[:, 48:56], w[:, 40:48], w[:, 8:16], op=ALU.add)
        v.drain()
        v.tensor_tensor(w[:, 56:64], w[:, 48:56], w[:, 32:40], op=ALU.mult)
        if "dbg_scr" in sb:
            v.tensor_copy(sb["dbg_scr"][:, tile * 16:tile * 16 + 8], w[:, 0:8])
            v.tensor_copy(sb["dbg_scr"][:, tile * 16 + 8:tile * 16 + 16], w[:, 32:40])
            if tile == 0:
                v.tensor_copy(sb["dbg_scr"][:, 256:264], w[:, 24:32])
                v.tensor_copy(sb["dbg_scr"][:, 264:265], w[:, 16:17])
                v.tensor_copy(sb["dbg_scr"][:, 265:266], w[:, 17:18])
                v.tensor_copy(sb["dbg_scr"][:, 266:267], w[:, 18:19])
                v.tensor_copy(sb["dbg_scr"][:, 267:268], w[:, 19:20])

        p.mile(v, "sV")
        p.wait(t, "sV")
        ti = t.transpose(gtp[0:8, (tile % 4) * 128:(tile % 4 + 1) * 128],
                         w[:, 56:64], sb["eye_f"][:])
        p.inc(ti, "sP", 1)
        if tile % 4 == 3:
            p.wait(v, "sP")
            v.tensor_copy(sb["gates"][0:8, (tile // 4) * 512:(tile // 4 + 1) * 512],
                          gtp[0:8, 0:512])
            p.mile(v, "sV")
            p.guard_set("psF", "sV")
    p.guard_set("psC", "sV")

    zb = [ps["A"], ps["B"], ps["C"], ps["F"]]
    zbname = ["psA", "psB", "psC", "psF"]
    ob = [ps["D"], ps["E"]]
    obname = ["psD", "psE"]
    gates_ready = p.cnt("sV")
    HGRP = [list(range(0, 12)), list(range(12, 22))]
    HC0 = [0, 12]
    HCN = [12, 10]
    for hh in range(2):
        for e in range(NEXP):
            for m in range(2):
                tg = hh * 2 + m
                bank = ob[m]
                p.guard_wait(t, obname[m])
                if hh == 0 and e == 0 and m == 0:
                    p.wait(t, "sV", gates_ready)
                mi = t.matmul(bank[:], sb["sel8"][:, e * 128:(e + 1) * 128],
                              sb["gates"][0:8, tg * 512:(tg + 1) * 512],
                              start=True, stop=True)
                p.inc(mi, "sP", 1)
                p.wait(a, "sP")
                a.copy(sb["g_b"][:, m * 512:(m + 1) * 512], bank[:])
                p.mile(a, "sA")
                p.guard_set(obname[m], "sA")
            first_h_write = True
            for half in range(2):
                p.guard_wait(g, "w2s")
                w2b = sb["w2s"][0]
                w2_ready = p.inc(g.dma_start(
                    out=w2b[:, 0:HCN[half] * 1024].rearrange("p (gg f) -> p gg f", gg=HCN[half]),
                    in_=io["w2"][e].rearrange("(gg p) f -> p gg f", p=128)
                        [:, HC0[half]:HC0[half] + HCN[half], :]), "dG", 16)
                for wgi, hc in enumerate(HGRP[half]):
                    buf = wgi % 2
                    w1b = sb["w1g"][buf]
                    w3b = sb["w3g"][buf]
                    p.guard_wait(g, ("w1g", buf))
                    p.inc(g.dma_start(
                        out=w1b[:].rearrange("p (c h) -> p c h", c=NCH),
                        in_=io["w1"][e].rearrange("(c p) h -> p c h", p=128)
                            [:, :, hc * 128:(hc + 1) * 128]), "dG", 16)
                    dg_ready = p.inc(g.dma_start(
                        out=w3b[:].rearrange("p (c h) -> p c h", c=NCH),
                        in_=io["w3"][e].rearrange("(c p) h -> p c h", p=128)
                            [:, :, hc * 128:(hc + 1) * 128]), "dG", 16)
                    p.wait(t, "dG", dg_ready)
                    if True:
                        hcL = hc - HC0[half]
                        for m in range(2):
                            p.guard_wait(t, zbname[m * 2])
                            p.guard_wait(t, zbname[m * 2 + 1])
                        for c in range(NCH):
                            for m in range(2):
                                tg = hh * 2 + m
                                nss = sb["normed"][:, c * TOK + tg * 512:c * TOK + (tg + 1) * 512]
                                t.matmul(zb[m * 2][:], w1b[:, c * 128:(c + 1) * 128],
                                         nss, start=(c == 0), stop=(c == NCH - 1))
                        for c in range(NCH):
                            for m in range(2):
                                tg = hh * 2 + m
                                nss = sb["normed"][:, c * TOK + tg * 512:c * TOK + (tg + 1) * 512]
                                mi = t.matmul(zb[m * 2 + 1][:], w3b[:, c * 128:(c + 1) * 128],
                                              nss, start=(c == 0), stop=(c == NCH - 1))
                        p.inc(mi, "sP", 1)
                        p.wait(v, "sP")
                        v.tensor_tensor(sb["tmpA"][:, 0:512], zb[0][:],
                                        sb["g_b"][:, 0:512], op=ALU.mult)
                        v.tensor_tensor(sb["tmpA"][:, 512:1024], zb[2][:],
                                        sb["g_b"][:, 512:1024], op=ALU.mult)
                        p.mile(v, "sV")
                        p.guard_set(zbname[0], "sV")
                        p.guard_set(zbname[2], "sV")
                        p.wait(a, "sV")
                        if first_h_write:
                            p.guard_wait(a, "h")
                            first_h_write = False
                        a.activation(sb["h"][:, hcL * 1024:hcL * 1024 + 512],
                                     sb["tmpA"][:, 0:512], AF.Silu)
                        a.activation(sb["h"][:, hcL * 1024 + 512:hcL * 1024 + 1024],
                                     sb["tmpA"][:, 512:1024], AF.Silu)
                        p.mile(a, "sA")
                        p.wait(v, "sA")
                        v.tensor_tensor(sb["h"][:, hcL * 1024:hcL * 1024 + 512],
                                        sb["h"][:, hcL * 1024:hcL * 1024 + 512],
                                        zb[1][:], op=ALU.mult)
                        v.tensor_tensor(sb["h"][:, hcL * 1024 + 512:hcL * 1024 + 1024],
                                        sb["h"][:, hcL * 1024 + 512:hcL * 1024 + 1024],
                                        zb[3][:], op=ALU.mult)
                        p.mile(v, "sV")
                        p.guard_set(zbname[1], "sV")
                        p.guard_set(zbname[3], "sV")
                    p.guard_set(("w1g", buf), "sP")
                h_ready = p.cnt("sV")
                p.wait(t, "dG", w2_ready)
                p.wait(t, "sV", h_ready)
                for oc in range(NCH):
                    for m in range(2):
                        p.guard_wait(t, obname[m])
                    for gg in range(HCN[half]):
                        for m in range(2):
                            mi = t.matmul(ob[m][:],
                                          w2b[:, gg * 1024 + oc * 128:gg * 1024 + (oc + 1) * 128],
                                          sb["h"][:, gg * 1024 + m * 512:gg * 1024 + (m + 1) * 512],
                                          start=(gg == 0), stop=(gg == HCN[half] - 1))
                    p.inc(mi, "sP", 1)
                    p.wait(v, "sP")
                    v.scalar_tensor_tensor(sb["tmpA"][:, 0:512], ob[0][:], 0.5,
                                           sb["g_b"][:, 0:512], op0=ALU.mult, op1=ALU.mult)
                    v.scalar_tensor_tensor(sb["tmpA"][:, 512:1024], ob[1][:], 0.5,
                                           sb["g_b"][:, 512:1024], op0=ALU.mult, op1=ALU.mult)
                    v.drain()
                    for m in range(2):
                        tg = hh * 2 + m
                        xs = x[:, oc * TOK + tg * 512:oc * TOK + (tg + 1) * 512]
                        v.tensor_tensor(xs, xs, sb["tmpA"][:, m * 512:(m + 1) * 512],
                                        op=ALU.add)
                    p.mile(v, "sV")
                    for m in range(2):
                        p.guard_set(obname[m], "sV")
                p.guard_set("w2s", "sP")
                p.guard_set("h", "sP")
                first_h_write = True
            p.guard_set("g_b", "sV")
    p.guard_set("gwT", "sP")




def emit_exchange_flags(p, sb, io, exch, SLpar):
    nc = p.nc
    s = nc.sync
    ALUo = ALU
    p.wait(s, "dS")
    fl = io["sh_flags"].rearrange("(e sl) one -> e sl one", sl=2)
    di = s.dma_start(out=fl[exch, SLpar, :],
                     in_=sb["nonces"][exch:exch + 1, :])
    p.inc(di, "dS", 16)
    p.wait(s, "dS")
    nr = s.alloc_register(f"nr{exch}")
    s.load(nr, sb["nonces"][exch:exch + 1, 0:1])
    for slot in range(2):
        c = s.alloc_register(f"c{exch}_{slot}")
        f = s.alloc_register(f"f{exch}_{slot}")
        s.reg_mov(c, 1)
        with s.While(c):
            s.load(f, io["sh_flags"][exch * 2 + slot:exch * 2 + slot + 1, 0:1])
            s.reg_alu(c, f, nr, ALUo.not_equal)
    s.nop().then_inc(p._sems["sG"], 1)
    p._cnt["sG"] += 1


def emit_attn0(p, sb, ps, io, ctx):
    nc = p.nc
    t, v, a, g, s = nc.tensor, nc.vector, nc.scalar, nc.gpsimd, nc.sync
    x = sb["x"]
    NT = T // 512

    with ExitStack() as st:
        V_sb = st.enter_context(nc.sbuf_tensor("at_V", [128, 32 * 520], BF16))
        lncols = st.enter_context(nc.sbuf_tensor("at_lncols", [128, 4], F32))
        negU = st.enter_context(nc.sbuf_tensor("at_negU", [128, 8], F32))
        nrm8 = st.enter_context(nc.sbuf_tensor("at_nrm8", [8, 8], F32))
        bocols = st.enter_context(nc.sbuf_tensor("at_bo", [128, 8], F32))
        sb["selH"] = st.enter_context(nc.sbuf_tensor("at_selH", [128, 32], F32))
        sb["selT"] = st.enter_context(nc.sbuf_tensor("at_selT", [8, 4 * 128], F32))
        sb["trimask"] = st.enter_context(nc.sbuf_tensor("at_trimask", [128, 128], F32))
        p.fence(s)
        for nm in ("selH", "selT", "trimask"):
            p.inc(s.dma_start(out=sb[nm][:], in_=io[nm]), "dS", 16)
        lnv = {}
        for nm in ("qln_g", "qln_b", "kln_g", "kln_b"):
            lnv[nm] = st.enter_context(nc.sbuf_tensor("at_" + nm, [128, 1], F32))
            p.inc(s.dma_start(out=lnv[nm][:], in_=io[nm]), "dS", 16)
        p.inc(s.dma_start(out=bocols[:], in_=io["l0_bo_cols"]), "dS", 16)
        p.wait(s, "dS")
        preg_s = nc.values_load(sb["par"][0:1, 0:1], engines=[s.engine],
                                skip_runtime_bounds_check=True)
        SL = bass.ts(preg_s, 1)

        with ExitStack() as st2:
            stage = st2.enter_context(nc.sbuf_tensor("at_nstage", [128, NCH * 512], BF16))
            emit_ln(p, sb, ps, to_dram=(io["normed_loc"], stage))
            p.wait(s, "dS")
            shn = io["sh_normed0"].rearrange("r (hf t2) -> r hf t2", hf=2)
            di = s.dma_start(out=shn[:, SL, :],
                             in_=io["normed_loc"].rearrange("r (one t) -> r one t", one=1))
            p.inc(di, "dS", 16)
            emit_exchange_flags(p, sb, io, 0, SL)

        p.wait(v, "dS")
        v.tensor_scalar_mul(lncols[:, 0:1], lnv["qln_g"][:], SCALE0)
        v.tensor_scalar_mul(lncols[:, 1:2], lnv["qln_b"][:], SCALE0)
        v.tensor_scalar_mul(lncols[:, 2:3], lnv["kln_g"][:], SCALE0)
        v.tensor_scalar_mul(lncols[:, 3:4], lnv["kln_b"][:], SCALE0)
        v.memset(V_sb[:], 1.0)
        p.mile(v, "sV")

        with ExitStack() as st2:
            wq = st2.enter_context(nc.sbuf_tensor("at_wq", [128, NCH * 512], BF16))
            wk = st2.enter_context(nc.sbuf_tensor("at_wk", [128, NCH * 512], BF16))
            wv = st2.enter_context(nc.sbuf_tensor("at_wv", [128, NCH * 512], BF16))
            nstream = st2.enter_context(nc.sbuf_tensor("at_nst", [128, NCH * 512], BF16))
            stg = st2.enter_context(nc.sbuf_tensor("at_stg", [128, 4 * 512], BF16))
            qtmp = st2.enter_context(nc.sbuf_tensor("at_qtmp", [128, 4 * 512], F32))
            hrows = st2.enter_context(nc.sbuf_tensor("at_hrows", [8, 3 * 512], F32))
            p.wait(g, "dS")
            for nm, ionm in ((wq, "l0_wq"), (wk, "l0_wk"), (wv, "l0_wv")):
                p.inc(g.dma_start(out=nm[:].rearrange("p (c o) -> p c o", c=NCH),
                                  in_=io[ionm].rearrange("(c p) o -> p c o", p=128)),
                      "dG", 16)
            p.wait(g, "sG")
            nsh = io["sh_normed0"].rearrange("(c p) t -> p c t", p=128)
            pq = [ps["A"], ps["B"], ps["C"], ps["F"]]
            pqn = ["psA", "psB", "psC", "psF"]
            stats1, stats2 = ps["D"], ps["E"]
            for tg in range(NT):
                p.guard_wait(g, "nstream")
                di = g.dma_start(out=nstream[:].rearrange("p (c t2) -> p c t2", c=NCH),
                                 in_=nsh[:, :, tg * 512:(tg + 1) * 512])
                nst_ready = p.inc(di, "dG", 16)
                p.wait(t, "dG", nst_ready)
                for proj in range(2):
                    wgt = wq if proj == 0 else wk
                    gcol = lncols[:, 2 * proj:2 * proj + 1]
                    bcol = lncols[:, 2 * proj + 1:2 * proj + 2]
                    dr = io["q_dr"] if proj == 0 else io["k_dr"]
                    for oc in range(4):
                        p.guard_wait(t, pqn[oc])
                        for kc in range(NCH):
                            mi = t.matmul(pq[oc][:],
                                          wgt[:, kc * 512 + oc * 128:kc * 512 + (oc + 1) * 128],
                                          nstream[:, kc * 512:(kc + 1) * 512],
                                          start=(kc == 0), stop=(kc == NCH - 1))
                        p.inc(mi, "sP", 1)
                    p.wait(a, "sP")
                    for oc in range(4):
                        a.copy(qtmp[:, oc * 512:(oc + 1) * 512], pq[oc][:])
                    p.mile(a, "sA")
                    for oc in range(4):
                        p.guard_set(pqn[oc], "sA")
                    p.guard_wait(t, "psD")
                    p.guard_wait(t, "psE")
                    p.wait(t, "sA")
                    for oc in range(4):
                        sqb = sb["tmpB"] if oc % 2 == 0 else sb["tmpA"]
                        p.guard_wait(a, ("sqB", oc % 2))
                        a.activation(sqb[:, 0:512], qtmp[:, oc * 512:(oc + 1) * 512], AF.Square)
                        p.mile(a, "sA")
                        p.wait(t, "sA")
                        t.matmul(stats1[0:8, :], sb["selH"][:, oc * 8:(oc + 1) * 8],
                                 qtmp[:, oc * 512:(oc + 1) * 512],
                                 start=(oc == 0), stop=(oc == 3))
                        mi = t.matmul(stats2[0:8, :], sb["selH"][:, oc * 8:(oc + 1) * 8],
                                      sqb[:, 0:512], start=(oc == 0), stop=(oc == 3))
                        p.inc(mi, "sP", 1)
                        p.guard_set(("sqB", oc % 2), "sP")
                    p.wait(v, "sP")
                    v.tensor_scalar_mul(hrows[0:8, 0:512], stats1[0:8, :], 1.0 / DH)
                    v.tensor_scalar_mul(hrows[0:8, 512:1024], stats2[0:8, :], 1.0 / DH)
                    v.drain()
                    v.tensor_tensor(hrows[0:8, 1024:1536], hrows[0:8, 0:512],
                                    hrows[0:8, 0:512], op=ALU.mult)
                    v.drain()
                    v.tensor_tensor(hrows[0:8, 512:1024], hrows[0:8, 512:1024],
                                    hrows[0:8, 1024:1536], op=ALU.subtract)
                    v.drain()
                    v.tensor_scalar_add(hrows[0:8, 512:1024], hrows[0:8, 512:1024], EPS)
                    p.mile(v, "sV")
                    p.guard_set("psD", "sV")
                    p.guard_set("psE", "sV")
                    p.wait(a, "sV")
                    a.activation(hrows[0:8, 1024:1536], hrows[0:8, 512:1024], AF.Ln)
                    a.drain()
                    a.activation(hrows[0:8, 512:1024], hrows[0:8, 1024:1536], AF.Exp,
                                 scale=-0.5)
                    p.mile(a, "sA")
                    p.wait(t, "sA")
                    for oc in range(4):
                        p.guard_wait(t, "psD")
                        p.guard_wait(t, "psE")
                        t.matmul(stats1[:], sb["selT"][0:8, oc * 128:(oc + 1) * 128],
                                 hrows[0:8, 0:512], start=True, stop=True)
                        mi = t.matmul(stats2[:], sb["selT"][0:8, oc * 128:(oc + 1) * 128],
                                      hrows[0:8, 512:1024], start=True, stop=True)
                        p.inc(mi, "sP", 1)
                        p.wait(v, "sP")
                        v.tensor_tensor(sb["tmpA"][:, 0:512], qtmp[:, oc * 512:(oc + 1) * 512],
                                        stats1[:], op=ALU.subtract)
                        v.drain()
                        v.tensor_tensor(sb["tmpB"][:, 0:512], sb["tmpA"][:, 0:512],
                                        stats2[:], op=ALU.mult)
                        v.drain()
                        v.tensor_scalar(stg[:, oc * 512:(oc + 1) * 512], sb["tmpB"][:, 0:512],
                                        gcol, bcol, op0=ALU.mult, op1=ALU.add)
                        p.mile(v, "sV")
                        p.guard_set("psD", "sV")
                        p.guard_set("psE", "sV")
                    p.guard_wait(t, "psD")
                    p.wait(t, "sV")
                    p.wait(a, "sV")
                    for oc in range(4):
                        p.guard_wait(a, ("sqB", 0))
                        a.activation(sb["tmpB"][:, 0:512], stg[:, oc * 512:(oc + 1) * 512],
                                     AF.Square)
                        p.mile(a, "sA")
                        p.wait(t, "sA")
                        mi = t.matmul(stats1[0:8, :], sb["selH"][:, oc * 8:(oc + 1) * 8],
                                      sb["tmpB"][:, 0:512], start=(oc == 0), stop=(oc == 3))
                        p.inc(mi, "sP", 1)
                        p.guard_set(("sqB", 0), "sP")
                    p.wait(v, "sP")
                    v.tensor_reduce(sb["tmpA"][0:8, 0:1], stats1[0:8, :], axis=AXX, op=ALU.max)
                    v.drain()
                    if tg == 0:
                        v.tensor_copy(nrm8[0:8, proj:proj + 1], sb["tmpA"][0:8, 0:1])
                    else:
                        v.tensor_tensor(nrm8[0:8, proj:proj + 1], nrm8[0:8, proj:proj + 1],
                                        sb["tmpA"][0:8, 0:1], op=ALU.max)
                    p.mile(v, "sV")
                    p.guard_set("psD", "sV")
                    p.wait(s, "sV")
                    p.guard_wait(s, "stg")
                    di = s.dma_start(out=dr.rearrange("(oc p2) t -> p2 oc t", p2=128)
                                     [:, :, tg * 512:(tg + 1) * 512],
                                     in_=stg[:].rearrange("p (oc t2) -> p oc t2", oc=4))
                    p.inc(di, "dS", 16)
                    p.guard_set("stg", "dS")
                for ttl in range(4):
                    tt = tg * 4 + ttl
                    bank = pq[ttl % 2]
                    p.guard_wait(t, pqn[ttl % 2])
                    for kc in range(NCH):
                        mi = t.matmul(bank[:],
                                      nstream[:, kc * 512 + ttl * 128:kc * 512 + (ttl + 1) * 128],
                                      wv[:, kc * 512:(kc + 1) * 512],
                                      start=(kc == 0), stop=(kc == NCH - 1))
                    p.inc(mi, "sP", 1)
                    p.wait(v, "sP")
                    vv = V_sb[:].rearrange("p (tt hb d) -> p tt hb d", tt=32, hb=8)
                    v.tensor_copy(vv[:, tt, :, 0:64],
                                  bank[:].rearrange("p (hb d) -> p hb d", hb=8))
                    p.mile(v, "sV")
                    p.guard_set(pqn[ttl % 2], "sV")
                p.guard_set("nstream", "sP")

        v.tensor_tensor(nrm8[0:8, 2:3], nrm8[0:8, 0:1], nrm8[0:8, 1:2], op=ALU.mult)
        v.drain()
        p.mile(v, "sV")
        p.wait(a, "sV")
        a.activation(nrm8[0:8, 3:4], nrm8[0:8, 2:3], AF.Ln)
        a.drain()
        a.activation(nrm8[0:8, 4:5], nrm8[0:8, 3:4], AF.Exp, scale=0.5)
        p.mile(a, "sA")
        p.wait(t, "sA")
        p.guard_wait(t, "psD")
        ti = t.transpose(ps["D"][0:1, 0:8], nrm8[0:8, 4:5], sb["eye_f"][0:8, 0:8])
        p.inc(ti, "sP", 1)
        p.wait(v, "sP")
        v.tensor_copy(sb["tmpA"][0:1, 0:8], ps["D"][0:1, 0:8])
        p.mile(v, "sV")
        p.wait(t, "sV")
        ti = t.matmul(ps["D"][:, 0:8], sb["ones_row"][:], sb["tmpA"][0:1, 0:8],
                      start=True, stop=True)
        p.inc(ti, "sP", 1)
        p.wait(a, "sP")
        a.activation(negU[:], ps["D"][:, 0:8], AF.Copy, scale=-1.0)
        p.mile(a, "sA")
        p.guard_set("psD", "sA")

        with ExitStack() as st2:
            kst = [st2.enter_context(nc.sbuf_tensor(f"at_kst{i}", [64, T], BF16))
                   for i in range(2)]
            qst = [st2.enter_context(nc.sbuf_tensor(f"at_qst{i}", [64, 512], BF16))
                   for i in range(2)]
            Pt = [st2.enter_context(nc.sbuf_tensor(f"at_pt{i}", [128, 512], BF16))
                  for i in range(2)]
            arow = st2.enter_context(nc.sbuf_tensor("at_arow", [64, 512], BF16))
            sS = [ps["A"], ps["B"]]
            sSn = ["psA", "psB"]
            pO, pOn = ps["C"], "psC"
            pB, pBn = ps["E"], "psE"
            units = [(h, qg) for h in range(NH) for qg in range(NT)]
            p.fence(g)
            for ui, (h, qg) in enumerate(units):
                ktn = qg * 4 + 4
                bufi = ui % 2
                p.guard_wait(g, ("kst", bufi))
                p.inc(g.dma_start(out=kst[bufi][:, 0:ktn * 128],
                                  in_=io["k_dr"][h * 64:(h + 1) * 64, 0:ktn * 128]), "dG", 16)
                p.guard_wait(g, ("qst", bufi))
                dg_ready = p.inc(g.dma_start(out=qst[bufi][:],
                                             in_=io["q_dr"][h * 64:(h + 1) * 64,
                                                            qg * 512:(qg + 1) * 512]),
                                 "dG", 16)
                p.guard_wait(t, pOn)
                p.wait(t, "dG", dg_ready)
                for kt in range(ktn):
                    dead = max(0, kt * 128 - qg * 512)
                    nvalid = 512 - dead
                    sbank = sS[kt % 2]
                    p.guard_wait(t, sSn[kt % 2])
                    mi = t.matmul(sbank[:, 0:nvalid], kst[bufi][:, kt * 128:(kt + 1) * 128],
                                  qst[bufi][:, dead:512], start=True, stop=True)
                    p.inc(mi, "sP", 1)
                    if kt >= qg * 4:
                        p.wait(v, "sP")
                        v.tensor_tensor(sbank[:, 0:128], sbank[:, 0:128], sb["trimask"][:],
                                        op=ALU.add)
                        p.mile(v, "sV")
                        p.wait(a, "sV")
                    p.guard_wait(v, ("Pt", kt % 2))
                    if dead > 0:
                        v.memset(Pt[kt % 2][:, 0:dead], 0.0)
                        p.mile(v, "sV")
                        p.wait(a, "sV")
                    p.wait(a, "sP")
                    a.activation(Pt[kt % 2][:, dead:512], sbank[:, 0:nvalid], AF.Exp,
                                 bias=negU[:, h:h + 1])
                    p.mile(a, "sA")
                    p.guard_set(sSn[kt % 2], "sA")
                    p.wait(t, "sA")
                    mi = t.matmul(pO[0:65, :],
                                  V_sb[:].rearrange("p (tt c) -> p tt c", tt=32)
                                  [:, kt, h * 65:(h + 1) * 65],
                                  Pt[kt % 2][:], start=(kt == 0), stop=(kt == ktn - 1))
                    p.inc(mi, "sP", 1)
                    p.guard_set(("Pt", kt % 2), "sP")
                p.guard_set(("kst", bufi), "sP")
                p.guard_set(("qst", bufi), "sP")
                p.wait(v, "sP")
                p.guard_wait(v, ("lrow",))
                v.tensor_copy(sb["tmpB"][0:1, 0:512], pO[64:65, :])
                v.drain()
                v.reciprocal(sb["tmpB"][0:1, 0:512], sb["tmpB"][0:1, 0:512])
                p.mile(v, "sV")
                p.wait(t, "sV")
                p.guard_wait(t, pBn)
                mi = t.matmul(pB[:], sb["ones_row"][:], sb["tmpB"][0:1, 0:512],
                              start=True, stop=True)
                p.inc(mi, "sP", 1)
                p.guard_set(("lrow",), "sP")
                p.wait(a, "sP")
                p.guard_wait(a, ("rlb",))
                a.copy(sb["tmpA"][0:64, 0:512], pB[0:64, :])
                p.mile(a, "sA")
                p.wait(v, "sA")
                v.tensor_tensor(arow[:], pO[0:64, :], sb["tmpA"][0:64, 0:512], op=ALU.mult)
                p.mile(v, "sV")
                p.guard_set(pOn, "sV")
                p.guard_set(pBn, "sV")
                p.guard_set(("rlb",), "sV")
                p.wait(s, "sV")
                p.guard_wait(s, "arow")
                di = s.dma_start(out=io["attn_dr"][h * 64:(h + 1) * 64,
                                                   qg * 512:(qg + 1) * 512],
                                 in_=arow[:])
                p.inc(di, "dS", 16)
                p.guard_set("arow", "dS")

        with ExitStack() as st2:
            wo = st2.enter_context(nc.sbuf_tensor("at_wo", [128, 4 * 1024], BF16))
            astream = st2.enter_context(nc.sbuf_tensor("at_ast", [128, 4 * 512], BF16))
            stg2 = st2.enter_context(nc.sbuf_tensor("at_stg2", [128, 2 * 512], BF16))
            p.fence(g)
            p.inc(g.dma_start(out=wo[:].rearrange("p (c o) -> p c o", c=4),
                              in_=io["l0_wo"].rearrange("(c p) o -> p c o", p=128)), "dG", 16)
            ash = io["attn_dr"].rearrange("(c p) t -> p c t", p=128)
            p.wait(a, "dS")
            pr = [ps["A"], ps["B"]]
            prn = ["psA", "psB"]
            psh = io["partial_loc"].rearrange("(c p) t -> p c t", p=128)
            for tg in range(NT):
                p.guard_wait(g, "astream")
                di = g.dma_start(out=astream[:].rearrange("p (c t2) -> p c t2", c=4),
                                 in_=ash[:, :, tg * 512:(tg + 1) * 512])
                nst_ready = p.inc(di, "dG", 16)
                p.wait(t, "dG", nst_ready)
                for oc in range(NCH):
                    bank = pr[oc % 2]
                    p.guard_wait(t, prn[oc % 2])
                    for kc in range(4):
                        mi = t.matmul(bank[:],
                                      wo[:, kc * 1024 + oc * 128:kc * 1024 + (oc + 1) * 128],
                                      astream[:, kc * 512:(kc + 1) * 512],
                                      start=(kc == 0), stop=(kc == 3))
                    p.inc(mi, "sP", 1)
                    p.wait(a, "sP")
                    p.guard_wait(a, ("pstg", oc % 2))
                    a.copy(stg2[:, (oc % 2) * 512:(oc % 2 + 1) * 512], bank[:])
                    p.mile(a, "sA")
                    p.guard_set(prn[oc % 2], "sA")
                    p.wait(s, "sA")
                    di = s.dma_start(out=psh[:, oc:oc + 1, tg * 512:(tg + 1) * 512],
                                     in_=stg2[:, (oc % 2) * 512:(oc % 2 + 1) * 512]
                                     .rearrange("p (one t) -> p one t", one=1))
                    p.inc(di, "dS", 16)
                    p.guard_set(("pstg", oc % 2), "dS")
                p.guard_set("astream", "sP")
        p.wait(s, "dS")
        shp = io["sh_partial0"].rearrange("(sl r) t -> sl r t", sl=2)
        di = s.dma_start(out=shp[SL, :, :],
                         in_=io["partial_loc"].rearrange("(one r) t -> one r t", one=1))
        p.inc(di, "dS", 16)
        emit_exchange_flags(p, sb, io, 1, SL)
        with ExitStack() as st2:
            rd = st2.enter_context(nc.sbuf_tensor("at_rd", [128, NCH * 2048], BF16))
            psh = io["sh_partial0"].rearrange("(sl c p) (hf t2) -> sl p c hf t2",
                                              sl=2, p=128, hf=2)
            p.fence(s)
            for slot in range(2):
                p.wait(s, "sG")
                p.guard_wait(s, "rd")
                di = s.dma_start(out=rd[:].rearrange("p (c one t) -> p c one t",
                                                     c=NCH, one=1),
                                 in_=psh[slot][:, :, SL, :])
                rd_ready = p.inc(di, "dS", 16)
                p.wait(v, "dS", rd_ready)
                for c in range(NCH):
                    for tg in range(4):
                        xs = x[:, c * TOK + tg * 512:c * TOK + (tg + 1) * 512]
                        rds = rd[:, c * 2048 + tg * 512:c * 2048 + (tg + 1) * 512]
                        if slot == 0:
                            v.tensor_tensor(sb["tmpA"][:, 0:512], xs, rds, op=ALU.add)
                            v.drain()
                            v.tensor_scalar_add(xs, sb["tmpA"][:, 0:512],
                                                bocols[:, c:c + 1])
                        else:
                            v.tensor_tensor(xs, xs, rds, op=ALU.add)
                p.mile(v, "sV")
                p.guard_set("rd", "sV")




def emit_quant_tile(p, q, out_bf, in_f32, scr):
    nc = p.nc
    v, a = nc.vector, nc.scalar
    bm3 = scr["bm"][:].rearrange("p (b one) -> p b one", one=1)
    in3 = in_f32.rearrange("p (b w) -> p b w", w=32)
    v.tensor_reduce(bm3, in3, axis=AXX, op=ALU.max, apply_absolute_value=True)
    v.drain()
    v.tensor_scalar(scr["sc"][:], scr["bm"][:], 1.0 / 7.5, 1e-30, op0=ALU.mult,
                    op1=ALU.max)
    v.drain()
    v.reciprocal(scr["rs"][:], scr["sc"][:])
    v.drain()
    for b in range(32):
        v.tensor_scalar_mul(scr["vq"][:, b * 32:(b + 1) * 32],
                            in_f32[:, b * 32:(b + 1) * 32], scr["rs"][:, b:b + 1])
    p.mile(v, "sV")
    p.wait(a, "sV")
    a.activation(scr["av"][:], scr["vq"][:], AF.Abs)
    p.mile(a, "sA")
    p.wait(v, "sA")
    v.tensor_single_scalar(scr["g2"][:], scr["av"][:], 2.0, op=ALU.is_ge)
    v.tensor_single_scalar(scr["g4"][:], scr["av"][:], 4.0, op=ALU.is_ge)
    v.drain()
    v.tensor_scalar(scr["rr"][:], scr["g2"][:], -4.0, 8.0, op0=ALU.mult, op1=ALU.add)
    v.tensor_scalar(scr["ss"][:], scr["g2"][:], 0.125, 0.125, op0=ALU.mult, op1=ALU.add)
    v.drain()
    v.scalar_tensor_tensor(scr["rr"][:], scr["g4"][:], -2.0, scr["rr"][:],
                           op0=ALU.mult, op1=ALU.add)
    v.scalar_tensor_tensor(scr["ss"][:], scr["g4"][:], 0.25, scr["ss"][:],
                           op0=ALU.mult, op1=ALU.add)
    v.drain()
    v.tensor_tensor(scr["yy"][:], scr["vq"][:], scr["rr"][:], op=ALU.mult)
    v.drain()
    v.tensor_single_scalar(scr["ff"][:], scr["yy"][:], 12582912.0, op=ALU.add)
    v.drain()
    v.tensor_single_scalar(scr["ff"][:], scr["ff"][:], -12582912.0, op=ALU.add)
    v.drain()
    v.tensor_tensor(scr["g2"][:], scr["ff"][:], scr["yy"][:], op=ALU.subtract)
    v.drain()
    v.tensor_single_scalar(scr["g2"][:], scr["g2"][:], 0.5, op=ALU.is_equal)
    v.drain()
    v.tensor_tensor(scr["ff"][:], scr["ff"][:], scr["g2"][:], op=ALU.subtract)
    v.drain()
    v.tensor_tensor(scr["yy"][:], scr["ff"][:], scr["ss"][:], op=ALU.mult)
    v.drain()
    for b in range(32):
        v.tensor_scalar_mul(out_bf[:, b * 32:(b + 1) * 32],
                            scr["yy"][:, b * 32:(b + 1) * 32], scr["sc"][:, b:b + 1])
    p.mile(v, "sV")


def emit_ffn1(p, sb, ps, io, tag):
    nc = p.nc
    t, v, a, g, s = nc.tensor, nc.vector, nc.scalar, nc.gpsimd, nc.sync
    x = sb["x"]
    emit_ln(p, sb, ps, stats_only=True)
    with ExitStack() as st:
        wd = st.enter_context(nc.sbuf_tensor(f"{tag}_wd", [128, NCH * 1024], BF16))
        xd = st.enter_context(nc.sbuf_tensor(f"{tag}_xd", [128, NCH * TOK], BF16))
        scr = {}
        for nm in ("bm", "sc", "rs"):
            scr[nm] = st.enter_context(nc.sbuf_tensor(f"{tag}_" + nm, [128, 32], F32))
        for nm in ("vq", "av", "g2", "g4", "rr", "ss", "yy", "ff", "wtmp"):
            scr[nm] = st.enter_context(nc.sbuf_tensor(f"{tag}_" + nm, [128, 1024], F32))
        qtile = st.enter_context(nc.sbuf_tensor(f"{tag}_qt", [128, 1024], BF16))
        hbias = st.enter_context(nc.sbuf_tensor(f"{tag}_hb", [128, 8], F32))
        bcols = st.enter_context(nc.sbuf_tensor(f"{tag}_bc", [128, 8], F32))
        p.fence(s)
        p.inc(s.dma_start(out=bcols[:], in_=io["l1fb_cols"]), "dS", 16)
        p.wait(v, "dS")
        v.tensor_scalar_mul(hbias[:], bcols[:], 0.5)
        p.mile(v, "sV")
        pT = ps["H"]
        for rc in range(NCH):
            p.guard_wait(s, "wtmp")
            di = s.dma_start(out=scr["wtmp"][:],
                             in_=io["l1fw"][rc * 128:(rc + 1) * 128, :])
            p.inc(di, "dS", 16)
            p.wait(v, "dS")
            emit_quant_tile(p, None, qtile[:], scr["wtmp"][:], scr)
            p.guard_set("wtmp", "sV")
            p.wait(t, "sV")
            p.guard_wait(t, "psH")
            for cc in range(NCH):
                ti = t.transpose(pT[:, cc * 128:(cc + 1) * 128],
                                 qtile[:, cc * 128:(cc + 1) * 128], sb["eye_b"][:])
            p.inc(ti, "sP", 1)
            p.wait(v, "sP")
            for cc in range(NCH):
                v.tensor_copy(wd[:, cc * 1024 + rc * 128:cc * 1024 + (rc + 1) * 128],
                              pT[:, cc * 128:(cc + 1) * 128])
            p.mile(v, "sV")
            p.guard_set("psH", "sV")
        pF = ps["A"]
        for tile in range(16):
            m_col = sb["mr_cols"][:, tile:tile + 1]
            r_col = sb["mr_cols"][:, 16 + tile:17 + tile]
            for half in range(2):
                p.wait(t, "sV")
                p.guard_wait(t, "psA")
                for cl in range(4):
                    c = half * 4 + cl
                    ti = t.transpose(pF[:, cl * 128:(cl + 1) * 128],
                                     x[:, c * TOK + tile * 128:c * TOK + (tile + 1) * 128],
                                     sb["eye_f"][:])
                p.inc(ti, "sP", 1)
                p.wait(v, "sP")
                if half == 0:
                    p.guard_wait(v, "wtmp")
                v.tensor_scalar(scr["wtmp"][:, half * 512:(half + 1) * 512], pF[:],
                                m_col, r_col, op0=ALU.subtract, op1=ALU.mult)
                p.mile(v, "sV")
                p.guard_set("psA", "sV")
            emit_quant_tile(p, None, qtile[:], scr["wtmp"][:], scr)
            p.guard_set("wtmp", "sV")
            p.wait(t, "sV")
            p.guard_wait(t, "psH")
            for c in range(NCH):
                ti = t.transpose(pT[:, c * 128:(c + 1) * 128],
                                 qtile[:, c * 128:(c + 1) * 128], sb["eye_b"][:])
            p.inc(ti, "sP", 1)
            p.wait(v, "sP")
            for c in range(NCH):
                v.tensor_copy(xd[:, c * TOK + tile * 128:c * TOK + (tile + 1) * 128],
                              pT[:, c * 128:(c + 1) * 128])
            p.mile(v, "sV")
            p.guard_set("psH", "sV")
        xd_ready = p.cnt("sV")
        p.wait(t, "sV", xd_ready)
        pr = [ps["B"], ps["C"]]
        prn = ["psB", "psC"]
        for oc in range(NCH):
            for tg in range(4):
                bank = pr[(oc * 4 + tg) % 2]
                bname = prn[(oc * 4 + tg) % 2]
                p.guard_wait(t, bname)
                for kc in range(NCH):
                    mi = t.matmul(bank[:],
                                  wd[:, kc * 1024 + oc * 128:kc * 1024 + (oc + 1) * 128],
                                  xd[:, kc * TOK + tg * 512:kc * TOK + (tg + 1) * 512],
                                  start=(kc == 0), stop=(kc == NCH - 1))
                p.inc(mi, "sP", 1)
                p.wait(v, "sP")
                p.guard_wait(v, ("fstg", 0))
                v.tensor_scalar(sb["tmpB"][:, 0:512], bank[:], 0.5, hbias[:, oc:oc + 1],
                                op0=ALU.mult, op1=ALU.add)
                v.drain()
                p.mile(v, "sV")
                p.guard_set(bname, "sV")
                xs = x[:, oc * TOK + tg * 512:oc * TOK + (tg + 1) * 512]
                v.tensor_tensor(xs, xs, sb["tmpB"][:, 0:512], op=ALU.add)
                p.mile(v, "sV")
                p.guard_set(("fstg", 0), "sV")




def emit_attn1(p, sb, ps, io, ctx):
    nc = p.nc
    t, v, a, g, s = nc.tensor, nc.vector, nc.scalar, nc.gpsimd, nc.sync
    x = sb["x"]
    NT = T // 512

    with ExitStack() as st:
        V_sb = st.enter_context(nc.sbuf_tensor("a1_V", [128, 32 * 520], BF16))
        krs = st.enter_context(nc.sbuf_tensor("a1_krs", [32, T], BF16))
        negU = st.enter_context(nc.sbuf_tensor("a1_negU", [128, 8], F32))
        nrm8 = st.enter_context(nc.sbuf_tensor("a1_nrm8", [8, 8], F32))
        crow = st.enter_context(nc.sbuf_tensor("a1_crow", [1, 8], F32))
        sb["selH"] = st.enter_context(nc.sbuf_tensor("a1_selH", [128, 32], F32))
        sb["selHr"] = st.enter_context(nc.sbuf_tensor("a1_selHr", [128, 16], F32))
        sb["trimask"] = st.enter_context(nc.sbuf_tensor("a1_trimask", [128, 128], F32))
        p.fence(s)
        for nm in ("selH", "selHr", "trimask"):
            p.inc(s.dma_start(out=sb[nm][:], in_=io[nm]), "dS", 16)
        p.inc(s.dma_start(out=crow[:], in_=io["lobo"]), "dS", 16)
        p.wait(s, "dS")
        preg_s = nc.values_load(sb["par"][0:1, 0:1], engines=[s.engine],
                                skip_runtime_bounds_check=True)
        SL = bass.ts(preg_s, 1)

        p.wait(a, "dS")
        a.activation(crow[:], crow[:], AF.Exp)
        p.mile(a, "sA")

        with ExitStack() as st2:
            stage = st2.enter_context(nc.sbuf_tensor("a1_nstage", [128, NCH * 512], BF16))
            emit_ln(p, sb, ps, to_dram=(io["normed_loc"], stage))
            p.wait(s, "dS")
            shn = io["sh_normed1"].rearrange("r (hf t2) -> r hf t2", hf=2)
            di = s.dma_start(out=shn[:, SL, :],
                             in_=io["normed_loc"].rearrange("r (one t) -> r one t", one=1))
            p.inc(di, "dS", 16)
            emit_exchange_flags(p, sb, io, 2, SL)

        v.memset(V_sb[:], 1.0)
        p.mile(v, "sV")

        with ExitStack() as st2:
            wq = st2.enter_context(nc.sbuf_tensor("a1_wq", [128, NCH * 512], BF16))
            wqr = st2.enter_context(nc.sbuf_tensor("a1_wqr", [128, NCH * 512], BF16))
            wkvd = st2.enter_context(nc.sbuf_tensor("a1_wkvd", [128, NCH * 256], BF16))
            wkr = st2.enter_context(nc.sbuf_tensor("a1_wkr", [128, NCH * 64], BF16))
            wku = st2.enter_context(nc.sbuf_tensor("a1_wku", [128, 2 * 512], BF16))
            wvu = st2.enter_context(nc.sbuf_tensor("a1_wvu", [128, 2 * 512], BF16))
            nstream = st2.enter_context(nc.sbuf_tensor("a1_nst", [128, NCH * 512], BF16))
            stg = st2.enter_context(nc.sbuf_tensor("a1_stg", [128, 4 * 512], BF16))
            lat = st2.enter_context(nc.sbuf_tensor("a1_lat", [128, 2 * 512], BF16))
            rope = [st2.enter_context(nc.sbuf_tensor(f"a1_rope{i}", [128, 2 * 512], F32))
                    for i in range(2)]
            p.fence(g)
            for nm, ionm in ((wq, "l1_wq"), (wqr, "l1_wqr_cat"), (wkvd, "l1_wkvd"),
                             (wkr, "l1_wkr_cat"), (wku, "l1_wku"), (wvu, "l1_wvu")):
                ncol = nm.shape[1] // NCH if nm not in (wku, wvu) else 512
                p.inc(g.dma_start(out=nm[:].rearrange("p (c o) -> p c o",
                                                      c=NCH if nm not in (wku, wvu) else 2),
                                  in_=io[ionm].rearrange("(c p) o -> p c o", p=128)),
                      "dG", 16)
            p.wait(g, "sG")
            nsh = io["sh_normed1"].rearrange("(c p) t -> p c t", p=128)
            pq = [ps["A"], ps["B"], ps["C"], ps["F"]]
            pqn = ["psA", "psB", "psC", "psF"]
            stats1, stats2 = ps["D"], ps["E"]
            for tg in range(NT):
                p.guard_wait(g, "nstream")
                di = g.dma_start(out=nstream[:].rearrange("p (c t2) -> p c t2", c=NCH),
                                 in_=nsh[:, :, tg * 512:(tg + 1) * 512])
                nst_ready = p.inc(di, "dG", 16)
                rb = rope[tg % 2]
                p.guard_wait(g, ("rope", tg % 2))
                p.inc(g.dma_start(out=rb[:, 0:512],
                                  in_=io["ropecos"][:, tg * 512:(tg + 1) * 512]), "dG", 16)
                rope_ready = p.inc(g.dma_start(out=rb[:, 512:1024],
                                               in_=io["ropesin"][:, tg * 512:(tg + 1) * 512]),
                                   "dG", 16)
                p.wait(t, "dG", nst_ready)
                for oc in range(4):
                    p.guard_wait(t, pqn[oc])
                    for kc in range(NCH):
                        mi = t.matmul(pq[oc][:],
                                      wq[:, kc * 512 + oc * 128:kc * 512 + (oc + 1) * 128],
                                      nstream[:, kc * 512:(kc + 1) * 512],
                                      start=(kc == 0), stop=(kc == NCH - 1))
                    p.inc(mi, "sP", 1)
                p.wait(v, "sP")
                for oc in range(4):
                    v.tensor_copy(stg[:, oc * 512:(oc + 1) * 512], pq[oc][:])
                p.mile(v, "sV")
                for oc in range(4):
                    p.guard_set(pqn[oc], "sV")
                emit_norm_update(p, sb, ps, stg, nrm8, 0, tg, 4, "selH")
                p.wait(s, "sV")
                p.guard_wait(s, "stg")
                di = s.dma_start(out=io["q_dr"].rearrange("(oc p2) t -> p2 oc t", p2=128)
                                 [:, :, tg * 512:(tg + 1) * 512],
                                 in_=stg[:].rearrange("p (oc t2) -> p oc t2", oc=4))
                p.inc(di, "dS", 16)
                p.guard_set("stg", "dS")
                for oc in range(4):
                    p.guard_wait(t, pqn[oc])
                    for kc in range(NCH):
                        mi = t.matmul(pq[oc][:],
                                      wqr[:, kc * 512 + oc * 128:kc * 512 + (oc + 1) * 128],
                                      nstream[:, kc * 512:(kc + 1) * 512],
                                      start=(kc == 0), stop=(kc == NCH - 1))
                    p.inc(mi, "sP", 1)
                p.wait(v, "sP")
                p.wait(v, "dG", rope_ready)
                p.guard_wait(v, "stg")
                for c in range(2):
                    v.tensor_tensor(sb["tmpA"][:, 0:512], pq[c][:], rb[:, 0:512], op=ALU.mult)
                    v.drain()
                    v.tensor_tensor(sb["tmpB"][:, 0:512], pq[2 + c][:], rb[:, 512:1024],
                                    op=ALU.mult)
                    v.drain()
                    v.tensor_tensor(stg[:, c * 512:(c + 1) * 512], sb["tmpA"][:, 0:512],
                                    sb["tmpB"][:, 0:512], op=ALU.add)
                    v.drain()
                p.mile(v, "sV")
                for oc in range(4):
                    p.guard_set(pqn[oc], "sV")
                emit_norm_update(p, sb, ps, stg, nrm8, 1, tg, 2, "selHr")
                p.wait(s, "sV")
                di = s.dma_start(out=io["qr_dr"].rearrange("(oc p2) t -> p2 oc t", p2=128)
                                 [:, :, tg * 512:(tg + 1) * 512],
                                 in_=stg[:, 0:1024].rearrange("p (oc t2) -> p oc t2", oc=2))
                p.inc(di, "dS", 16)
                p.guard_set("stg", "dS")
                for oc in range(2):
                    p.guard_wait(t, pqn[oc])
                    for kc in range(NCH):
                        mi = t.matmul(pq[oc][:],
                                      wkvd[:, kc * 256 + oc * 128:kc * 256 + (oc + 1) * 128],
                                      nstream[:, kc * 512:(kc + 1) * 512],
                                      start=(kc == 0), stop=(kc == NCH - 1))
                    p.inc(mi, "sP", 1)
                p.wait(v, "sP")
                p.guard_wait(v, "lat")
                for oc in range(2):
                    v.tensor_copy(lat[:, oc * 512:(oc + 1) * 512], pq[oc][:])
                p.mile(v, "sV")
                for oc in range(2):
                    p.guard_set(pqn[oc], "sV")
                p.guard_wait(t, pqn[2])
                for kc in range(NCH):
                    mi = t.matmul(pq[2][0:64, :], wkr[:, kc * 64:(kc + 1) * 64],
                                  nstream[:, kc * 512:(kc + 1) * 512],
                                  start=(kc == 0), stop=(kc == NCH - 1))
                p.inc(mi, "sP", 1)
                p.wait(v, "sP")
                v.tensor_tensor(sb["tmpA"][0:32, 0:512], pq[2][0:32, :], rb[0:32, 0:512],
                                op=ALU.mult)
                v.drain()
                v.tensor_tensor(sb["tmpB"][0:32, 0:512], pq[2][32:64, :], rb[0:32, 512:1024],
                                op=ALU.mult)
                v.drain()
                v.tensor_tensor(krs[:, tg * 512:(tg + 1) * 512], sb["tmpA"][0:32, 0:512],
                                sb["tmpB"][0:32, 0:512], op=ALU.add)
                p.mile(v, "sV")
                p.guard_set(pqn[2], "sV")
                p.guard_set(("rope", tg % 2), "sV")
                p.wait(a, "sV")
                p.guard_wait(a, ("sqB", 0))
                a.activation(sb["tmpB"][0:32, 0:512], krs[:, tg * 512:(tg + 1) * 512],
                             AF.Square)
                p.mile(a, "sA")
                p.wait(t, "sA")
                p.guard_wait(t, "psD")
                mi = t.matmul(stats1[0:1, :], sb["ones1"][0:32, :], sb["tmpB"][0:32, 0:512],
                              start=True, stop=True)
                p.inc(mi, "sP", 1)
                p.guard_set(("sqB", 0), "sP")
                p.wait(v, "sP")
                v.tensor_reduce(sb["tmpA"][0:1, 0:1], stats1[0:1, :], axis=AXX, op=ALU.max)
                v.drain()
                if tg == 0:
                    v.tensor_copy(nrm8[0:1, 3:4], sb["tmpA"][0:1, 0:1])
                else:
                    v.tensor_tensor(nrm8[0:1, 3:4], nrm8[0:1, 3:4], sb["tmpA"][0:1, 0:1],
                                    op=ALU.max)
                p.mile(v, "sV")
                p.guard_set("psD", "sV")
                p.wait(t, "sV")
                for oc in range(4):
                    p.guard_wait(t, pqn[oc])
                    for kc in range(2):
                        mi = t.matmul(pq[oc][:],
                                      wku[:, kc * 512 + oc * 128:kc * 512 + (oc + 1) * 128],
                                      lat[:, kc * 512:(kc + 1) * 512],
                                      start=(kc == 0), stop=(kc == 1))
                    p.inc(mi, "sP", 1)
                p.wait(v, "sP")
                p.guard_wait(v, "stg")
                for oc in range(4):
                    v.tensor_copy(stg[:, oc * 512:(oc + 1) * 512], pq[oc][:])
                p.mile(v, "sV")
                for oc in range(4):
                    p.guard_set(pqn[oc], "sV")
                emit_norm_update(p, sb, ps, stg, nrm8, 2, tg, 4, "selH")
                p.wait(s, "sV")
                di = s.dma_start(out=io["k_dr"].rearrange("(oc p2) t -> p2 oc t", p2=128)
                                 [:, :, tg * 512:(tg + 1) * 512],
                                 in_=stg[:].rearrange("p (oc t2) -> p oc t2", oc=4))
                p.inc(di, "dS", 16)
                p.guard_set("stg", "dS")
                for ttl in range(4):
                    tt = tg * 4 + ttl
                    bank = pq[ttl % 2]
                    p.guard_wait(t, pqn[ttl % 2])
                    for kc in range(2):
                        mi = t.matmul(bank[:],
                                      lat[:, kc * 512 + ttl * 128:kc * 512 + (ttl + 1) * 128],
                                      wvu[:, kc * 512:(kc + 1) * 512],
                                      start=(kc == 0), stop=(kc == 1))
                    p.inc(mi, "sP", 1)
                    p.wait(v, "sP")
                    vv = V_sb[:].rearrange("p (tt hb d) -> p tt hb d", tt=32, hb=8)
                    v.tensor_copy(vv[:, tt, :, 0:64],
                                  bank[:].rearrange("p (hb d) -> p hb d", hb=8))
                    p.mile(v, "sV")
                    p.guard_set(pqn[ttl % 2], "sV")
                p.guard_set("lat", "sP")
                p.guard_set("nstream", "sP")

        v.tensor_tensor(nrm8[0:8, 4:5], nrm8[0:8, 0:1], nrm8[0:8, 1:2], op=ALU.add)
        v.drain()
        p.mile(v, "sV")
        p.wait(t, "sV")
        p.guard_wait(t, "psD")
        ti = t.matmul(ps["D"][0:8, 0:1], sb["ones_row"][0:1, 0:8], nrm8[0:1, 3:4],
                      start=True, stop=True)
        p.inc(ti, "sP", 1)
        p.wait(v, "sP")
        v.tensor_tensor(nrm8[0:8, 5:6], nrm8[0:8, 2:3], ps["D"][0:8, 0:1], op=ALU.add)
        v.drain()
        v.tensor_tensor(nrm8[0:8, 6:7], nrm8[0:8, 4:5], nrm8[0:8, 5:6], op=ALU.mult)
        p.mile(v, "sV")
        p.guard_set("psD", "sV")
        p.wait(a, "sV")
        a.activation(nrm8[0:8, 7:8], nrm8[0:8, 6:7], AF.Ln)
        a.drain()
        a.activation(nrm8[0:8, 6:7], nrm8[0:8, 7:8], AF.Exp, scale=0.5)
        p.mile(a, "sA")
        p.wait(t, "sA")
        p.guard_wait(t, "psD")
        ti = t.transpose(ps["D"][0:1, 0:8], nrm8[0:8, 6:7], sb["eye_f"][0:8, 0:8])
        p.inc(ti, "sP", 1)
        p.wait(v, "sP")
        v.tensor_copy(sb["tmpA"][0:1, 0:8], ps["D"][0:1, 0:8])
        p.mile(v, "sV")
        p.wait(t, "sV")
        ti = t.matmul(ps["D"][:, 0:8], sb["ones_row"][:], sb["tmpA"][0:1, 0:8],
                      start=True, stop=True)
        p.inc(ti, "sP", 1)
        p.wait(a, "sP")
        a.activation(negU[:], ps["D"][:, 0:8], AF.Copy, scale=-SCALE1)
        p.mile(a, "sA")
        p.guard_set("psD", "sA")
        p.wait(a, "sA")
        a.activation(nrm8[0:8, 7:8], nrm8[0:8, 6:7], AF.Exp, scale=-SCALE1)
        p.mile(a, "sA")
        p.wait(t, "sA")
        p.guard_wait(t, "psE")
        ti = t.transpose(ps["E"][0:8, 0:1], crow[0:1, 0:8], sb["eye_f"][0:1, 0:1])
        p.inc(ti, "sP", 1)
        p.wait(v, "sP")
        v.tensor_tensor(nrm8[0:8, 4:5], ps["E"][0:8, 0:1], nrm8[0:8, 7:8], op=ALU.mult)
        v.drain()
        p.mile(v, "sV")
        p.wait(t, "sV")
        ti = t.transpose(ps["E"][0:1, 0:8], nrm8[0:8, 4:5], sb["eye_f"][0:8, 0:8])
        p.inc(ti, "sP", 1)
        p.wait(v, "sP")
        v.tensor_copy(crow[0:1, 0:8], ps["E"][0:1, 0:8])
        p.mile(v, "sV")
        p.guard_set("psE", "sV")

        with ExitStack() as st2:
            kst = [st2.enter_context(nc.sbuf_tensor(f"a1_kst{i}", [64, T], BF16))
                   for i in range(2)]
            qst = [st2.enter_context(nc.sbuf_tensor(f"a1_qst{i}", [64, 512], BF16))
                   for i in range(2)]
            qrst = [st2.enter_context(nc.sbuf_tensor(f"a1_qrst{i}", [32, 512], BF16))
                    for i in range(2)]
            Pt = [st2.enter_context(nc.sbuf_tensor(f"a1_pt{i}", [128, 512], BF16))
                  for i in range(2)]
            arow = st2.enter_context(nc.sbuf_tensor("a1_arow", [64, 512], BF16))
            sS = [ps["A"], ps["B"]]
            sSn = ["psA", "psB"]
            pO, pOn = ps["C"], "psC"
            pB, pBn = ps["E"], "psE"
            units = [(h, qg) for h in range(NH) for qg in range(NT)]
            p.fence(g)
            for ui, (h, qg) in enumerate(units):
                ktn = qg * 4 + 4
                bufi = ui % 2
                p.guard_wait(g, ("kst", bufi))
                p.inc(g.dma_start(out=kst[bufi][:, 0:ktn * 128],
                                  in_=io["k_dr"][h * 64:(h + 1) * 64, 0:ktn * 128]), "dG", 16)
                p.guard_wait(g, ("qst", bufi))
                p.inc(g.dma_start(out=qst[bufi][:],
                                  in_=io["q_dr"][h * 64:(h + 1) * 64,
                                                 qg * 512:(qg + 1) * 512]), "dG", 16)
                dg_ready = p.inc(g.dma_start(out=qrst[bufi][:],
                                             in_=io["qr_dr"][h * 32:(h + 1) * 32,
                                                             qg * 512:(qg + 1) * 512]),
                                 "dG", 16)
                p.guard_wait(t, pOn)
                p.wait(t, "dG", dg_ready)
                for kt in range(ktn):
                    dead = max(0, kt * 128 - qg * 512)
                    nvalid = 512 - dead
                    sbank = sS[kt % 2]
                    p.guard_wait(t, sSn[kt % 2])
                    t.matmul(sbank[:, 0:nvalid], kst[bufi][:, kt * 128:(kt + 1) * 128],
                             qst[bufi][:, dead:512], start=True, stop=False)
                    mi = t.matmul(sbank[:, 0:nvalid], krs[:, kt * 128:(kt + 1) * 128],
                                  qrst[bufi][:, dead:512], start=False, stop=True)
                    p.inc(mi, "sP", 1)
                    if kt >= qg * 4:
                        p.wait(v, "sP")
                        v.tensor_tensor(sbank[:, 0:128], sbank[:, 0:128], sb["trimask"][:],
                                        op=ALU.add)
                        p.mile(v, "sV")
                        p.wait(a, "sV")
                    p.guard_wait(v, ("Pt", kt % 2))
                    if dead > 0:
                        v.memset(Pt[kt % 2][:, 0:dead], 0.0)
                        p.mile(v, "sV")
                        p.wait(a, "sV")
                    p.wait(a, "sP")
                    a.activation(Pt[kt % 2][:, dead:512], sbank[:, 0:nvalid], AF.Exp,
                                 scale=SCALE1, bias=negU[:, h:h + 1])
                    p.mile(a, "sA")
                    p.guard_set(sSn[kt % 2], "sA")
                    p.wait(t, "sA")
                    mi = t.matmul(pO[0:65, :],
                                  V_sb[:].rearrange("p (tt c) -> p tt c", tt=32)
                                  [:, kt, h * 65:(h + 1) * 65],
                                  Pt[kt % 2][:], start=(kt == 0), stop=(kt == ktn - 1))
                    p.inc(mi, "sP", 1)
                    p.guard_set(("Pt", kt % 2), "sP")
                p.guard_set(("kst", bufi), "sP")
                p.guard_set(("qst", bufi), "sP")
                p.wait(v, "sP")
                p.guard_wait(v, ("lrow",))
                v.tensor_copy(sb["tmpB"][0:1, 0:512], pO[64:65, :])
                v.drain()
                v.tensor_single_scalar(sb["tmpB"][0:1, 0:512], sb["tmpB"][0:1, 0:512],
                                       crow[0:1, h:h + 1], op=ALU.add)
                v.drain()
                v.reciprocal(sb["tmpB"][0:1, 0:512], sb["tmpB"][0:1, 0:512])
                p.mile(v, "sV")
                p.wait(t, "sV")
                p.guard_wait(t, pBn)
                mi = t.matmul(pB[:], sb["ones_row"][:], sb["tmpB"][0:1, 0:512],
                              start=True, stop=True)
                p.inc(mi, "sP", 1)
                p.guard_set(("lrow",), "sP")
                p.wait(a, "sP")
                p.guard_wait(a, ("rlb",))
                a.copy(sb["tmpA"][0:64, 0:512], pB[0:64, :])
                p.mile(a, "sA")
                p.wait(v, "sA")
                v.tensor_tensor(arow[:], pO[0:64, :], sb["tmpA"][0:64, 0:512], op=ALU.mult)
                p.mile(v, "sV")
                p.guard_set(pOn, "sV")
                p.guard_set(pBn, "sV")
                p.guard_set(("rlb",), "sV")
                p.wait(s, "sV")
                p.guard_wait(s, "arow")
                di = s.dma_start(out=io["attn_dr"][h * 64:(h + 1) * 64,
                                                   qg * 512:(qg + 1) * 512],
                                 in_=arow[:])
                p.inc(di, "dS", 16)
                p.guard_set("arow", "dS")

        with ExitStack() as st2:
            wo = st2.enter_context(nc.sbuf_tensor("a1_wo", [128, 4 * 1024], BF16))
            astream = st2.enter_context(nc.sbuf_tensor("a1_ast", [128, 4 * 512], BF16))
            stg2 = st2.enter_context(nc.sbuf_tensor("a1_stg2", [128, 2 * 512], BF16))
            p.fence(g)
            p.inc(g.dma_start(out=wo[:].rearrange("p (c o) -> p c o", c=4),
                              in_=io["l1_wo"].rearrange("(c p) o -> p c o", p=128)), "dG", 16)
            ash = io["attn_dr"].rearrange("(c p) t -> p c t", p=128)
            p.wait(a, "dS")
            pr = [ps["A"], ps["B"]]
            prn = ["psA", "psB"]
            psh = io["partial_loc"].rearrange("(c p) t -> p c t", p=128)
            for tg in range(NT):
                p.guard_wait(g, "astream")
                di = g.dma_start(out=astream[:].rearrange("p (c t2) -> p c t2", c=4),
                                 in_=ash[:, :, tg * 512:(tg + 1) * 512])
                nst_ready = p.inc(di, "dG", 16)
                p.wait(t, "dG", nst_ready)
                for oc in range(NCH):
                    bank = pr[oc % 2]
                    p.guard_wait(t, prn[oc % 2])
                    for kc in range(4):
                        mi = t.matmul(bank[:],
                                      wo[:, kc * 1024 + oc * 128:kc * 1024 + (oc + 1) * 128],
                                      astream[:, kc * 512:(kc + 1) * 512],
                                      start=(kc == 0), stop=(kc == 3))
                    p.inc(mi, "sP", 1)
                    p.wait(a, "sP")
                    p.guard_wait(a, ("pstg", oc % 2))
                    a.copy(stg2[:, (oc % 2) * 512:(oc % 2 + 1) * 512], bank[:])
                    p.mile(a, "sA")
                    p.guard_set(prn[oc % 2], "sA")
                    p.wait(s, "sA")
                    di = s.dma_start(out=psh[:, oc:oc + 1, tg * 512:(tg + 1) * 512],
                                     in_=stg2[:, (oc % 2) * 512:(oc % 2 + 1) * 512]
                                     .rearrange("p (one t) -> p one t", one=1))
                    p.inc(di, "dS", 16)
                    p.guard_set(("pstg", oc % 2), "dS")
                p.guard_set("astream", "sP")
        p.wait(s, "dS")
        shp = io["sh_partial1"].rearrange("(sl r) t -> sl r t", sl=2)
        di = s.dma_start(out=shp[SL, :, :],
                         in_=io["partial_loc"].rearrange("(one r) t -> one r t", one=1))
        p.inc(di, "dS", 16)
        emit_exchange_flags(p, sb, io, 3, SL)
        with ExitStack() as st2:
            rd = st2.enter_context(nc.sbuf_tensor("a1_rd", [128, NCH * 2048], BF16))
            psh = io["sh_partial1"].rearrange("(sl c p) (hf t2) -> sl p c hf t2",
                                              sl=2, p=128, hf=2)
            p.fence(s)
            for slot in range(2):
                p.wait(s, "sG")
                p.guard_wait(s, "rd")
                di = s.dma_start(out=rd[:].rearrange("p (c one t) -> p c one t",
                                                     c=NCH, one=1),
                                 in_=psh[slot][:, :, SL, :])
                rd_ready = p.inc(di, "dS", 16)
                p.wait(v, "dS", rd_ready)
                for c in range(NCH):
                    for tg in range(4):
                        xs = x[:, c * TOK + tg * 512:c * TOK + (tg + 1) * 512]
                        rds = rd[:, c * 2048 + tg * 512:c * 2048 + (tg + 1) * 512]
                        v.tensor_tensor(xs, xs, rds, op=ALU.add)
                p.mile(v, "sV")
                p.guard_set("rd", "sV")


def emit_norm_update(p, sb, ps, stg, nrm8, col, tg, nchunks, selname):
    nc = p.nc
    t, v, a = nc.tensor, nc.vector, nc.scalar
    stats1 = ps["D"]
    p.wait(a, "sV")
    p.guard_wait(t, "psD")
    for oc in range(nchunks):
        p.guard_wait(a, ("sqB", 0))
        a.activation(sb["tmpB"][:, 0:512], stg[:, oc * 512:(oc + 1) * 512], AF.Square)
        p.mile(a, "sA")
        p.wait(t, "sA")
        mi = t.matmul(stats1[0:8, :], sb[selname][:, oc * 8:(oc + 1) * 8],
                      sb["tmpB"][:, 0:512], start=(oc == 0), stop=(oc == nchunks - 1))
        p.inc(mi, "sP", 1)
        p.guard_set(("sqB", 0), "sP")
    p.wait(v, "sP")
    v.tensor_reduce(sb["tmpA"][0:8, 0:1], stats1[0:8, :], axis=AXX, op=ALU.max)
    v.drain()
    if tg == 0:
        v.tensor_copy(nrm8[0:8, col:col + 1], sb["tmpA"][0:8, 0:1])
    else:
        v.tensor_tensor(nrm8[0:8, col:col + 1], nrm8[0:8, col:col + 1],
                        sb["tmpA"][0:8, 0:1], op=ALU.max)
    p.mile(v, "sV")
    p.guard_set("psD", "sV")


def emit_postamble(p, io, sb):
    nc = p.nc
    s = nc.sync
    p.wait(s, "sV")
    p.wait(s, "sA")
    p.wait(s, "sP")
    di = s.dma_start(out=io["x_out"].rearrange("(c p) t -> p c t", p=128),
                     in_=sb["x"][:].rearrange("p (c t) -> p c t", c=NCH))
    p.inc(di, "dS", 16)
    if "dbg_rows" in io:
        for i, nm in enumerate(("row_m", "row_r", "row_mr", "row_t")):
            p.inc(s.dma_start(out=io["dbg_rows"][i:i + 1, :], in_=sb[nm][0:1, :]), "dS", 16)
        p.inc(s.dma_start(out=io["dbg_gates"], in_=sb["gates"][0:8, :]), "dS", 16)
        p.inc(s.dma_start(out=io["dbg_normed"].rearrange("(c p) t -> p c t", p=128),
                          in_=sb["normed"][:].rearrange("p (c t) -> p c t", c=NCH)), "dS", 16)
        if "dbg_scr" in sb:
            p.inc(s.dma_start(out=io["dbg_scr"], in_=sb["dbg_scr"][:]), "dS", 16)
    p.wait(s, "dS")



def build_program(plan):
    nc = bass.Bass("TRN2", target_bir_lowering=False)
    io = {}
    io["x_in"] = nc.dram_tensor("x_in", [D, TOK], F32, kind="ExternalInput").ap()
    io["x_out"] = nc.dram_tensor("x_out", [D, TOK], F32, kind="ExternalOutput").ap()
    io["par"] = nc.dram_tensor("par", [1, 1], I32, kind="ExternalInput").ap()
    io["ones1"] = nc.dram_tensor("ones1", [128, 1], F32, kind="ExternalInput").ap()
    io["ones_row"] = nc.dram_tensor("ones_row", [1, 128], F32, kind="ExternalInput").ap()
    io["eye_f"] = nc.dram_tensor("eye_f", [128, 128], F32, kind="ExternalInput").ap()
    io["iotaB"] = nc.dram_tensor("iotaB", [128, 8], F32, kind="ExternalInput").ap()
    io["sel8"] = nc.dram_tensor("sel8", [8, 8 * 128], F32, kind="ExternalInput").ap()
    io["selH"] = nc.dram_tensor("selH", [128, 32], F32, kind="ExternalInput").ap()
    io["selT"] = nc.dram_tensor("selT", [8, 4 * 128], F32, kind="ExternalInput").ap()
    io["selHr"] = nc.dram_tensor("selHr", [128, 16], F32, kind="ExternalInput").ap()
    io["trimask"] = nc.dram_tensor("trimask", [128, 128], F32, kind="ExternalInput").ap()
    io["nonces"] = nc.dram_tensor("nonces", [8, 1], I32, kind="ExternalInput").ap()
    if os.environ.get("MK_DEBUG"):
        io["dbg_rows"] = nc.dram_tensor("dbg_rows", [4, TOK], F32, kind="ExternalOutput").ap()
        io["dbg_gates"] = nc.dram_tensor("dbg_gates", [8, TOK], F32, kind="ExternalOutput").ap()
        io["dbg_normed"] = nc.dram_tensor("dbg_normed", [D, TOK], BF16, kind="ExternalOutput").ap()
        io["dbg_scr"] = nc.dram_tensor("dbg_scr", [128, 320], F32, kind="ExternalOutput").ap()
    has_moe = any(sl in plan for sl in ("moe0", "moe1"))
    if "attn0" in plan:
        io["l0_wq"] = nc.dram_tensor("l0_wq", [D, 512], F32, kind="ExternalInput").ap()
        io["l0_wk"] = nc.dram_tensor("l0_wk", [D, 512], F32, kind="ExternalInput").ap()
        io["l0_wv"] = nc.dram_tensor("l0_wv", [D, 512], F32, kind="ExternalInput").ap()
        io["l0_wo"] = nc.dram_tensor("l0_wo", [512, D], F32, kind="ExternalInput").ap()
        io["qln_g"] = nc.dram_tensor("qln_g", [128, 1], F32, kind="ExternalInput").ap()
        io["qln_b"] = nc.dram_tensor("qln_b", [128, 1], F32, kind="ExternalInput").ap()
        io["kln_g"] = nc.dram_tensor("kln_g", [128, 1], F32, kind="ExternalInput").ap()
        io["kln_b"] = nc.dram_tensor("kln_b", [128, 1], F32, kind="ExternalInput").ap()
        io["l0_bo_cols"] = nc.dram_tensor("l0_bo_cols", [128, 8], F32, kind="ExternalInput").ap()
        io["sh_normed0"] = nc.dram_tensor("sh_normed0", [D, T], BF16,
                                          kind="Internal", addr_space="Shared").ap()
        io["sh_partial0"] = nc.dram_tensor("sh_partial0", [2 * D, T], BF16,
                                           kind="Internal", addr_space="Shared").ap()
        io["q_dr"] = nc.dram_tensor("q_dr", [512, T], BF16, kind="Internal").ap()
        io["normed_loc"] = nc.dram_tensor("normed_loc", [D, TOK], BF16, kind="Internal").ap()
        io["partial_loc"] = nc.dram_tensor("partial_loc", [D, T], BF16, kind="Internal").ap()
        io["k_dr"] = nc.dram_tensor("k_dr", [512, T], BF16, kind="Internal").ap()
        io["attn_dr"] = nc.dram_tensor("attn_dr", [512, T], BF16, kind="Internal").ap()
    if any(sl in plan for sl in ("ffn1a", "ffn1b")):
        io["l1fw"] = nc.dram_tensor("l1fw", [D, D], F32, kind="ExternalInput").ap()
        io["l1fb_cols"] = nc.dram_tensor("l1fb_cols", [128, 8], F32, kind="ExternalInput").ap()
    if "attn1" in plan:
        io["l1_wq"] = nc.dram_tensor("l1_wq", [D, 512], F32, kind="ExternalInput").ap()
        io["l1_wqr_cat"] = nc.dram_tensor("l1_wqr_cat", [D, 512], F32, kind="ExternalInput").ap()
        io["l1_wkvd"] = nc.dram_tensor("l1_wkvd", [D, 256], F32, kind="ExternalInput").ap()
        io["l1_wkr_cat"] = nc.dram_tensor("l1_wkr_cat", [D, 64], F32, kind="ExternalInput").ap()
        io["l1_wku"] = nc.dram_tensor("l1_wku", [256, 512], F32, kind="ExternalInput").ap()
        io["l1_wvu"] = nc.dram_tensor("l1_wvu", [256, 512], F32, kind="ExternalInput").ap()
        io["l1_wo"] = nc.dram_tensor("l1_wo", [512, D], F32, kind="ExternalInput").ap()
        io["lobo"] = nc.dram_tensor("lobo", [1, 8], F32, kind="ExternalInput").ap()
        io["ropecos"] = nc.dram_tensor("ropecos", [128, T], F32, kind="ExternalInput").ap()
        io["ropesin"] = nc.dram_tensor("ropesin", [128, T], F32, kind="ExternalInput").ap()
        io["sh_normed1"] = nc.dram_tensor("sh_normed1", [D, T], BF16,
                                          kind="Internal", addr_space="Shared").ap()
        io["sh_partial1"] = nc.dram_tensor("sh_partial1", [2 * D, T], BF16,
                                           kind="Internal", addr_space="Shared").ap()
        if "attn0" not in plan:
            io["q_dr"] = nc.dram_tensor("q_dr", [512, T], BF16, kind="Internal").ap()
            io["k_dr"] = nc.dram_tensor("k_dr", [512, T], BF16, kind="Internal").ap()
            io["attn_dr"] = nc.dram_tensor("attn_dr", [512, T], BF16, kind="Internal").ap()
            io["normed_loc"] = nc.dram_tensor("normed_loc", [D, TOK], BF16, kind="Internal").ap()
            io["partial_loc"] = nc.dram_tensor("partial_loc", [D, T], BF16, kind="Internal").ap()
        io["qr_dr"] = nc.dram_tensor("qr_dr", [256, T], BF16, kind="Internal").ap()
    if any(sl.startswith("attn") for sl in plan):
        io["sh_flags"] = nc.dram_tensor("sh_flags", [16, 1], I32,
                                        kind="Internal", addr_space="Shared").ap()
    if has_moe:
        io["gate_wT"] = nc.dram_tensor("gate_wT", [D, 8], F32, kind="ExternalInput").ap()
        io["w1"] = nc.dram_tensor("w1", [NEXP, D, HID], F32, kind="ExternalInput").ap()
        io["w2"] = nc.dram_tensor("w2", [NEXP, HID, D], F32, kind="ExternalInput").ap()
        io["w3"] = nc.dram_tensor("w3", [NEXP, D, HID], F32, kind="ExternalInput").ap()

    with ExitStack() as ctx:
        p = Prog(nc, ctx)
        for name in ("dS", "dG", "sP", "sV", "sA", "sG"):
            p.sem(name)
        sb = {}

        def alloc(name, shape, dtype):
            sb[name] = ctx.enter_context(nc.sbuf_tensor("sb_" + name, shape, dtype))
            return sb[name]

        alloc("x", [128, NCH * TOK], F32)
        alloc("ones1", [128, 1], F32)
        alloc("ones_row", [1, 128], F32)
        alloc("eye_f", [128, 128], F32)
        alloc("eye_b", [128, 128], BF16)
        alloc("iotaB", [128, 8], F32)
        alloc("sel8", [8, 8 * 128], F32)
        has_attn = any(sl.startswith("attn") for sl in plan)
        alloc("par", [1, 1], I32)
        alloc("nonces", [8, 1], I32)
        alloc("row_m", [1, TOK], F32)
        alloc("row_r", [1, TOK], F32)
        alloc("row_mr", [1, TOK], F32)
        alloc("row_t", [1, TOK], F32)
        alloc("mr_cols", [128, 32], F32)
        alloc("tmpA", [128, 1024], F32)
        alloc("tmpB", [128, 512], F32)
        if os.environ.get("MK_DEBUG"):
            alloc("dbg_scr", [128, 320], F32)

        ps = {}
        for nm in ("A", "B", "C", "D", "E", "F"):
            ps[nm] = ctx.enter_context(nc.psum_tensor(f"ps{nm}", [128, 512], F32))
        ps["H"] = ctx.enter_context(nc.psum_tensor("psH", [128, 1024], BF16))

        with nc.Block() as block:
            emit_preamble(p, io, sb)
            for name in plan:
                for eng in (nc.tensor, nc.vector, nc.scalar, nc.sync, nc.gpsimd):
                    p.fence(eng)
                if name in ("moe0", "moe1"):
                    emit_moe_sublayer(p, sb, ps, io, name)
                elif name == "attn0":
                    emit_attn0(p, sb, ps, io, ctx)
                elif name in ("ffn1a", "ffn1b"):
                    emit_ffn1(p, sb, ps, io, name)
                elif name == "attn1":
                    emit_attn1(p, sb, ps, io, ctx)
                else:
                    raise NotImplementedError(name)
            emit_postamble(p, io, sb)
    return nc, io



def _host_constants(core):
    iota = np.arange(8, dtype=np.float32)
    sel8 = np.zeros((8, 8 * 128), np.float32)
    for e in range(8):
        sel8[e, e * 128:(e + 1) * 128] = 1.0
    selH = np.zeros((128, 32), np.float32)
    selT = np.zeros((8, 4 * 128), np.float32)
    for c in range(4):
        for pp in range(128):
            h = 2 * c + pp // 64
            selH[pp, c * 8 + h] = 1.0
            selT[h, c * 128 + pp] = 1.0
    selHr = np.zeros((128, 16), np.float32)
    for c in range(2):
        for pp in range(128):
            h = 4 * c + pp // 32
            selHr[pp, c * 8 + h] = 1.0
    qp, kp = np.meshgrid(np.arange(128), np.arange(128), indexing="xy")
    trimask = np.where(kp <= qp, 0.0, NEGINF).astype(np.float32)
    return {
        "par": np.array([[core % 2]], np.int32),
        "ones1": np.ones((128, 1), np.float32),
        "ones_row": np.ones((1, 128), np.float32),
        "eye_f": np.eye(128, dtype=np.float32),
        "iotaB": np.broadcast_to((iota - BIG)[None, :], (128, 8)).copy(),
        "sel8": sel8,
        "selH": selH,
        "selT": selT,
        "selHr": selHr,
        "trimask": trimask,
    }


def kernel(**inputs):
    plan_env = os.environ.get("MK_PLAN", "fused")
    if plan_env == "unfused":
        plans = [[sl] for sl in SUBLAYERS]
    elif plan_env == "fused":
        plans = [SUBLAYERS]
    elif plan_env.startswith("pfx:"):
        k = int(plan_env.split(":")[1])
        plans = [[sl] for sl in SUBLAYERS[:k]]
    else:
        plans = [[sl] for sl in plan_env.split(",")]

    x = np.asarray(inputs["L0_attn_x"])
    B = x.shape[0]
    ncores = 8
    xs = [np.ascontiguousarray(x[c // 2, (c % 2) * TOK:(c % 2 + 1) * TOK, :].T)
          for c in range(ncores)]
    consts = [_host_constants(c) for c in range(ncores)]
    moe_w = {
        "gate_wT": np.ascontiguousarray(np.asarray(inputs["L0_ffn_gate_w"]).T),
        "w1": np.ascontiguousarray(np.asarray(inputs["L0_ffn_w1"])),
        "w2": np.ascontiguousarray(np.asarray(inputs["L0_ffn_w2"])),
        "w3": np.ascontiguousarray(np.asarray(inputs["L0_ffn_w3"])),
    }
    attn0_w = []
    for c in range(ncores):
        h0 = (c % 2) * 8
        rs = slice(h0 * 64, (h0 + 8) * 64)
        attn0_w.append({
            "l0_wq": np.ascontiguousarray(np.asarray(inputs["L0_attn_Wq"])[rs, :].T),
            "l0_wk": np.ascontiguousarray(np.asarray(inputs["L0_attn_Wk"])[rs, :].T),
            "l0_wv": np.ascontiguousarray(np.asarray(inputs["L0_attn_Wv"])[rs, :].T),
            "l0_wo": np.ascontiguousarray(np.asarray(inputs["L0_attn_Wo"])[:, rs].T),
            "qln_g": np.ascontiguousarray(np.tile(np.asarray(inputs["L0_attn_qln_g"]), 2)[:, None]),
            "qln_b": np.ascontiguousarray(np.tile(np.asarray(inputs["L0_attn_qln_b"]), 2)[:, None]),
            "kln_g": np.ascontiguousarray(np.tile(np.asarray(inputs["L0_attn_kln_g"]), 2)[:, None]),
            "kln_b": np.ascontiguousarray(np.tile(np.asarray(inputs["L0_attn_kln_b"]), 2)[:, None]),
            "l0_bo_cols": np.ascontiguousarray(np.asarray(inputs["L0_attn_bo"]).reshape(8, 128).T),
        })

    def _rope_tables():
        half = 16
        inv = (1.0 / (np.float32(10000.0) ** (np.arange(half, dtype=np.float32)
                                              / np.float32(half)))).astype(np.float32)
        pos = np.arange(T, dtype=np.float32)
        ang = (pos[:, None] * inv[None, :]).astype(np.float32)
        cos = np.cos(ang).astype(np.float32)
        sin = np.sin(ang).astype(np.float32)
        cos128 = np.empty((128, T), np.float32)
        sin128 = np.empty((128, T), np.float32)
        for j in range(128):
            b = j % 32
            if b < half:
                cos128[j] = cos[:, b]
                sin128[j] = -sin[:, b]
            else:
                cos128[j] = cos[:, b - half]
                sin128[j] = sin[:, b - half]
        return cos128, sin128

    ropecos, ropesin = _rope_tables()

    def _swap_qr(w, dh):
        w = w.reshape(w.shape[0], -1, dh)
        return np.concatenate([w[:, :, dh // 2:], w[:, :, :dh // 2]], axis=2).reshape(
            w.shape[0], -1)

    attn1_w = []
    for c in range(ncores):
        h0 = (c % 2) * 8
        wq = np.asarray(inputs["L1_attn_Wq"])[:, h0 * 64:(h0 + 8) * 64]
        wqr_all = np.asarray(inputs["L1_attn_Wqr"])
        wqr = wqr_all[:, h0 * 32:(h0 + 8) * 32]
        wqr_cat = np.concatenate([wqr, _swap_qr(wqr, 32)], axis=1)
        wkr = np.asarray(inputs["L1_attn_Wkr"])
        wkr_cat = np.concatenate([wkr, _swap_qr(wkr, 32)], axis=1)
        attn1_w.append({
            "l1_wq": np.ascontiguousarray(wq),
            "l1_wqr_cat": np.ascontiguousarray(wqr_cat),
            "l1_wkvd": np.ascontiguousarray(np.asarray(inputs["L1_attn_Wkvd"])),
            "l1_wkr_cat": np.ascontiguousarray(wkr_cat),
            "l1_wku": np.ascontiguousarray(np.asarray(inputs["L1_attn_Wku"])[:, h0 * 64:(h0 + 8) * 64]),
            "l1_wvu": np.ascontiguousarray(np.asarray(inputs["L1_attn_Wvu"])[:, h0 * 64:(h0 + 8) * 64]),
            "l1_wo": np.ascontiguousarray(np.asarray(inputs["L1_attn_Wo"])[h0 * 64:(h0 + 8) * 64, :]),
            "lobo": np.ascontiguousarray(np.asarray(inputs["L1_attn_lobo_log"])[h0:h0 + 8][None, :]),
            "ropecos": ropecos,
            "ropesin": ropesin,
        })

    ffn1_w = {
        "l1fw": np.ascontiguousarray(np.asarray(inputs["L1_ffn_weight"])),
        "l1fb_cols": np.ascontiguousarray(np.asarray(inputs["L1_ffn_bias"]).reshape(8, 128).T),
    }
    rng = np.random.RandomState()
    for plan in plans:
        nc, _io = build_program(plan)
        nonces = rng.randint(1, 2 ** 30, size=(8, 1)).astype(np.int32)
        in_maps = []
        for c in range(ncores):
            m = {"x_in": xs[c], "nonces": nonces}
            m.update(consts[c])
            if any(sl in plan for sl in ("moe0", "moe1")):
                m.update(moe_w)
            if "attn0" in plan:
                m.update(attn0_w[c])
            if "ffn1a" in plan or "ffn1b" in plan:
                m.update(ffn1_w)
            if "attn1" in plan:
                m.update(attn1_w[c])
            in_maps.append(m)
        res = run_bass_kernel_spmd(nc, in_maps, core_ids=list(range(ncores)))
        xs = [np.asarray(res.results[c]["x_out"]) for c in range(ncores)]

    out = np.empty((B, T, D), np.float32)
    for c in range(ncores):
        out[c // 2, (c % 2) * TOK:(c % 2 + 1) * TOK, :] = xs[c].T
    return out


# version 04
# speedup vs baseline: 2.0347x; 1.2559x over previous
import os
import numpy as np
import ml_dtypes
from contextlib import ExitStack

import concourse.bass as bass
import concourse.mybir as mybir
from concourse.bass_utils import run_bass_kernel_spmd

F32 = mybir.dt.float32
BF16 = mybir.dt.bfloat16
I32 = mybir.dt.int32
AF = mybir.ActivationFunctionType
ALU = mybir.AluOpType
AXX = mybir.AxisListType.X

D = 1024
TOK = 2048
T = 4096
NCH = D // 128
HID = 2816
NEXP = 8
NH = 8
DH = 64
EPS = 1e-5
BIG = 1.0e6
NEGINF = -1.0e30
SCALE0 = float(np.float32(1024.0 ** (-0.25)))
SCALE1 = float(1.0 / np.sqrt(96.0))

SUBLAYERS = ["moe0", "attn0", "moe1", "ffn1a", "attn1", "ffn1b"]


class Prog:
    def __init__(self, nc, ctx):
        self.nc = nc
        self.ctx = ctx
        self._sems = {}
        self._cnt = {}
        self._guard = {}

    def sem(self, name):
        h = self.ctx.enter_context(self.nc.semaphore(name))
        self._sems[name] = h
        self._cnt[name] = 0
        return h

    def inc(self, instr, name, n):
        self._cnt[name] += n
        instr.then_inc(self._sems[name], n)
        return self._cnt[name]

    def mile(self, eng, name):
        self._cnt[name] += 1
        eng.drain().then_inc(self._sems[name], 1)
        return self._cnt[name]

    def cnt(self, name):
        return self._cnt[name]

    def wait(self, eng, name, val=None):
        v = self._cnt[name] if val is None else val
        if v > 0:
            eng.wait_ge(self._sems[name], v)

    def fence(self, eng):
        for nm in ("sP", "sV", "sA", "dS", "dG"):
            self.wait(eng, nm)

    def guard_set(self, res, semname, val=None):
        v = self._cnt[semname] if val is None else val
        self._guard[res] = [(semname, v)]

    def guard_wait(self, eng, res):
        for semname, v in self._guard.pop(res, []):
            if v > 0:
                eng.wait_ge(self._sems[semname], v)



def emit_preamble(p, io, sb):
    nc = p.nc
    s = nc.sync
    g = nc.gpsimd
    p.inc(s.dma_start(out=sb["x"][:].rearrange("p (c t) -> p c t", c=NCH),
                      in_=io["x_in"].rearrange("(c p) t -> p c t", p=128)), "dS", 16)
    for nm in ("ones1", "ones_row", "eye_f", "iotaB", "sel8", "par", "nonces"):
        if nm in sb:
            p.inc(s.dma_start(out=sb[nm][:], in_=io[nm]), "dS", 16)
    p.inc(g.dma_start(out=sb["eye_b"][:], in_=io["eye_f"]), "dG", 16)


def emit_ln(p, sb, ps, to_dram=None, stats_only=False):

    nc = p.nc
    t, v, a = nc.tensor, nc.vector, nc.scalar
    x = sb["x"]
    p.wait(a, "dS")
    p.wait(t, "dS")
    p.wait(v, "dS")
    SLOC = [(ps["A"], 0), (ps["A"], 32), (ps["A"], 64), (ps["C"], 0)]
    SLOC2 = [(ps["B"], 0), (ps["B"], 32), (ps["B"], 64), (ps["F"], 0)]
    for nm in ("psA", "psB", "psC", "psF"):
        p.guard_wait(t, nm)
    for tg in range(4):
        bA, base = SLOC[tg]
        bB, base2 = SLOC2[tg]
        for c in range(NCH):
            xs = x[:, c * TOK + tg * 512: c * TOK + (tg + 1) * 512]
            slot = c % 2
            sqb = sb["tmpA"] if slot == 0 else sb["tmpB"]
            p.guard_wait(a, ("sq", slot))
            a.activation(sqb[:, 0:512], xs, AF.Square)
            p.mile(a, "sA")
            p.wait(t, "sA")
            t.matmul(bA[base:base + 1, :], sb["ones1"][:], xs,
                     start=(c == 0), stop=(c == NCH - 1))
            mi = t.matmul(bB[base2:base2 + 1, :], sb["ones1"][:],
                          sqb[:, 0:512],
                          start=(c == 0), stop=(c == NCH - 1))
            p.inc(mi, "sP", 1)
            p.guard_set(("sq", slot), "sP")
    p.wait(v, "sP")
    for tg in range(4):
        bA, base = SLOC[tg]
        bB, base2 = SLOC2[tg]
        sl = slice(tg * 512, (tg + 1) * 512)
        v.tensor_scalar_mul(sb["row_m"][0:1, sl], bA[base:base + 1, :], 1.0 / D)
        v.tensor_scalar_mul(sb["row_t"][0:1, sl], bB[base2:base2 + 1, :], 1.0 / D)
    p.mile(v, "sV")
    for nm in ("psA", "psB", "psC", "psF"):
        p.guard_set(nm, "sV")
    v.tensor_tensor(sb["row_r"][0:1, :], sb["row_m"][0:1, :], sb["row_m"][0:1, :], op=ALU.mult)
    v.drain()
    v.tensor_tensor(sb["row_t"][0:1, :], sb["row_t"][0:1, :], sb["row_r"][0:1, :], op=ALU.subtract)
    v.drain()
    v.tensor_scalar_add(sb["row_t"][0:1, :], sb["row_t"][0:1, :], EPS)
    p.mile(v, "sV")
    p.wait(a, "sV")
    a.activation(sb["row_mr"][0:1, :], sb["row_t"][0:1, :], AF.Ln)
    a.drain()
    a.activation(sb["row_r"][0:1, :], sb["row_mr"][0:1, :], AF.Exp, scale=-0.5)
    p.mile(a, "sA")
    p.wait(v, "sA")
    v.tensor_tensor(sb["row_mr"][0:1, :], sb["row_m"][0:1, :], sb["row_r"][0:1, :], op=ALU.mult)
    p.mile(v, "sV")
    p.wait(t, "sV")
    mrp = ps["C"]
    p.guard_wait(t, "psC")
    for tile in range(16):
        t.transpose(mrp[:, tile:tile + 1],
                    sb["row_m"][0:1, tile * 128:(tile + 1) * 128], sb["eye_f"][0:1, 0:1])
        ti = t.transpose(mrp[:, 16 + tile:17 + tile],
                         sb["row_r"][0:1, tile * 128:(tile + 1) * 128], sb["eye_f"][0:1, 0:1])
    p.inc(ti, "sP", 1)
    p.wait(v, "sP")
    v.tensor_copy(sb["mr_cols"][:], mrp[:, 0:32])
    p.mile(v, "sV")
    p.guard_set("psC", "sV")
    if stats_only:
        return
    rbb, mrb = ps["D"], ps["E"]
    s = nc.sync
    for tg in range(4):
        sl = slice(tg * 512, (tg + 1) * 512)
        p.guard_wait(t, "psD")
        p.guard_wait(t, "psE")
        t.matmul(rbb[:], sb["ones_row"][:], sb["row_r"][0:1, sl], start=True, stop=True)
        mi = t.matmul(mrb[:], sb["ones_row"][:], sb["row_mr"][0:1, sl], start=True, stop=True)
        p.inc(mi, "sP", 1)
        p.wait(v, "sP")
        p.guard_wait(v, ("sq", 0))
        if to_dram is not None:
            p.guard_wait(v, "nstage")
        for c in range(NCH):
            xs = x[:, c * TOK + tg * 512: c * TOK + (tg + 1) * 512]
            if to_dram is None:
                ns = sb["normed"][:, c * TOK + tg * 512: c * TOK + (tg + 1) * 512]
            else:
                ns = to_dram[1][:, c * 512:(c + 1) * 512]
            v.tensor_tensor(sb["tmpA"][:, 0:512], xs, rbb[:], op=ALU.mult)
            v.drain()
            v.tensor_tensor(ns, sb["tmpA"][:, 0:512], mrb[:], op=ALU.subtract)
            v.drain()
        p.mile(v, "sV")
        p.guard_set("psD", "sV")
        p.guard_set("psE", "sV")
        if to_dram is not None:
            loc_ap, stage = to_dram
            p.wait(s, "sV")
            dst = loc_ap.rearrange("(c p) (g t2) -> p c g t2", p=128, t2=512)
            di = s.dma_start(out=dst[:, :, tg, :],
                             in_=stage[:].rearrange("p (c t2) -> p c t2", c=NCH))
            p.inc(di, "dS", 16)
            p.guard_set("nstage", "dS")


def emit_moe_sublayer(p, sb, ps, io, tag):
    nc = p.nc
    with ExitStack() as st:
        def A(name, shape, dtype):
            t_ = st.enter_context(nc.sbuf_tensor(f"moe{tag}_{name}", shape, dtype))
            sb[name] = t_
            return t_
        A("normed", [128, NCH * TOK], BF16)
        A("gwT", [128, NCH * 8], F32)
        A("gsum_row", [1, 8], F32)
        A("gates", [8, TOK], F32)
        A("g_b", [128, 1024], F32)
        A("h", [128, 12 * 1024], BF16)
        sb["w1g"] = [st.enter_context(nc.sbuf_tensor(f"moe{tag}_w1g{i}", [128, NCH * 128], BF16))
                     for i in range(2)]
        sb["w3g"] = [st.enter_context(nc.sbuf_tensor(f"moe{tag}_w3g{i}", [128, NCH * 128], BF16))
                     for i in range(2)]
        sb["w2s"] = [st.enter_context(nc.sbuf_tensor(f"moe{tag}_w2s0", [128, 12 * 1024], BF16))]
        emit_ln(p, sb, ps)
        emit_moe(p, sb, ps, io)


def emit_moe(p, sb, ps, io):
    nc = p.nc
    t, v, a, g = nc.tensor, nc.vector, nc.scalar, nc.gpsimd
    s = nc.sync
    x = sb["x"]
    p.guard_wait(s, "gwT")
    p.inc(s.dma_start(out=sb["gwT"][:].rearrange("p (c e) -> p c e", c=NCH),
                      in_=io["gate_wT"].rearrange("(c p) e -> p c e", p=128)),
          "dS", 16)
    p.wait(t, "dS")
    bk = ps["A"]
    p.guard_wait(t, "psA")
    for c in range(NCH):
        mi = t.matmul(bk[0:1, 0:8], sb["ones1"][:], sb["gwT"][:, c * 8:(c + 1) * 8],
                      start=(c == 0), stop=(c == NCH - 1))
    p.inc(mi, "sP", 1)
    p.wait(v, "sP")
    v.tensor_copy(sb["gsum_row"][0:1, 0:8], bk[0:1, 0:8])
    v.tensor_scalar_mul(sb["row_t"][0:1, :], sb["row_m"][0:1, :], -1.0)
    p.mile(v, "sV")
    p.guard_set("psA", "sV")
    p.wait(t, "sV")
    scp = ps["C"]
    gtp = ps["F"]
    w = sb["tmpA"]
    for tile in range(16):
        p.guard_wait(t, "psC")
        if tile % 4 == 0:
            p.guard_wait(t, "psF")
        for c in range(NCH):
            xs = x[:, c * TOK + tile * 128:c * TOK + (tile + 1) * 128]
            t.matmul(scp[0:128, 0:8], xs, sb["gwT"][:, c * 8:(c + 1) * 8],
                     start=(c == 0), stop=False)
        mi = t.matmul(scp[0:128, 0:8], sb["row_t"][0:1, tile * 128:(tile + 1) * 128],
                      sb["gsum_row"][0:1, 0:8], start=False, stop=True)
        p.inc(mi, "sP", 1)
        p.wait(v, "sP")
        r_col = sb["mr_cols"][:, 16 + tile:17 + tile]
        v.tensor_scalar_mul(w[:, 0:8], scp[0:128, 0:8], r_col)
        p.mile(v, "sV")
        p.guard_set("psC", "sV")
        v.tensor_reduce(w[:, 16:17], w[:, 0:8], axis=AXX, op=ALU.max)
        v.drain()
        v.tensor_scalar_mul(w[:, 17:18], w[:, 16:17], -1.0)
        p.mile(v, "sV")
        p.wait(a, "sV")
        a.activation(w[:, 24:32], w[:, 0:8], AF.Exp, bias=w[:, 17:18])
        p.mile(a, "sA")
        p.wait(v, "sA")
        v.tensor_reduce(w[:, 18:19], w[:, 24:32], axis=AXX, op=ALU.add)
        v.drain()
        v.reciprocal(w[:, 19:20], w[:, 18:19])
        v.drain()
        v.tensor_scalar_mul(w[:, 32:40], w[:, 24:32], w[:, 19:20])
        v.drain()
        v.tensor_reduce(w[:, 16:17], w[:, 32:40], axis=AXX, op=ALU.max)
        v.drain()
        v.tensor_single_scalar(w[:, 8:16], w[:, 32:40], w[:, 16:17], op=ALU.is_equal)
        v.drain()
        v.tensor_tensor(w[:, 8:16], w[:, 8:16], sb["iotaB"][:], op=ALU.mult)
        v.drain()
        v.tensor_reduce(w[:, 16:17], w[:, 8:16], axis=AXX, op=ALU.min)
        v.drain()
        v.tensor_single_scalar(w[:, 40:48], sb["iotaB"][:], w[:, 16:17], op=ALU.is_equal)
        v.drain()
        v.scalar_tensor_tensor(w[:, 48:56], w[:, 40:48], -BIG, w[:, 32:40],
                               op0=ALU.mult, op1=ALU.add)
        v.drain()
        v.tensor_reduce(w[:, 16:17], w[:, 48:56], axis=AXX, op=ALU.max)
        v.drain()
        v.tensor_single_scalar(w[:, 8:16], w[:, 48:56], w[:, 16:17], op=ALU.is_equal)
        v.drain()
        v.tensor_tensor(w[:, 8:16], w[:, 8:16], sb["iotaB"][:], op=ALU.mult)
        v.drain()
        v.tensor_reduce(w[:, 16:17], w[:, 8:16], axis=AXX, op=ALU.min)
        v.drain()
        v.tensor_single_scalar(w[:, 8:16], sb["iotaB"][:], w[:, 16:17], op=ALU.is_equal)
        v.drain()
        v.tensor_tensor(w[:, 48:56], w[:, 40:48], w[:, 8:16], op=ALU.add)
        v.drain()
        v.tensor_tensor(w[:, 56:64], w[:, 48:56], w[:, 32:40], op=ALU.mult)
        if "dbg_scr" in sb:
            v.tensor_copy(sb["dbg_scr"][:, tile * 16:tile * 16 + 8], w[:, 0:8])
            v.tensor_copy(sb["dbg_scr"][:, tile * 16 + 8:tile * 16 + 16], w[:, 32:40])
            if tile == 0:
                v.tensor_copy(sb["dbg_scr"][:, 256:264], w[:, 24:32])
                v.tensor_copy(sb["dbg_scr"][:, 264:265], w[:, 16:17])
                v.tensor_copy(sb["dbg_scr"][:, 265:266], w[:, 17:18])
                v.tensor_copy(sb["dbg_scr"][:, 266:267], w[:, 18:19])
                v.tensor_copy(sb["dbg_scr"][:, 267:268], w[:, 19:20])

        p.mile(v, "sV")
        p.wait(t, "sV")
        ti = t.transpose(gtp[0:8, (tile % 4) * 128:(tile % 4 + 1) * 128],
                         w[:, 56:64], sb["eye_f"][:])
        p.inc(ti, "sP", 1)
        if tile % 4 == 3:
            p.wait(v, "sP")
            v.tensor_copy(sb["gates"][0:8, (tile // 4) * 512:(tile // 4 + 1) * 512],
                          gtp[0:8, 0:512])
            p.mile(v, "sV")
            p.guard_set("psF", "sV")
    p.guard_set("psC", "sV")

    zb = [ps["A"], ps["B"], ps["C"], ps["F"]]
    zbname = ["psA", "psB", "psC", "psF"]
    ob = [ps["D"], ps["E"]]
    obname = ["psD", "psE"]
    gates_ready = p.cnt("sV")
    HGRP = [list(range(0, 12)), list(range(12, 22))]
    HC0 = [0, 12]
    HCN = [12, 10]
    for hh in range(2):
        for e in range(NEXP):
            for m in range(2):
                tg = hh * 2 + m
                bank = ob[m]
                p.guard_wait(t, obname[m])
                if hh == 0 and e == 0 and m == 0:
                    p.wait(t, "sV", gates_ready)
                mi = t.matmul(bank[:], sb["sel8"][:, e * 128:(e + 1) * 128],
                              sb["gates"][0:8, tg * 512:(tg + 1) * 512],
                              start=True, stop=True)
                p.inc(mi, "sP", 1)
                p.wait(a, "sP")
                a.copy(sb["g_b"][:, m * 512:(m + 1) * 512], bank[:])
                p.mile(a, "sA")
                p.guard_set(obname[m], "sA")
            first_h_write = True
            for half in range(2):
                p.guard_wait(g, "w2s")
                w2b = sb["w2s"][0]
                w2_ready = p.inc(g.dma_start(
                    out=w2b[:, 0:HCN[half] * 1024].rearrange("p (gg f) -> p gg f", gg=HCN[half]),
                    in_=io["w2"][e].rearrange("(gg p) f -> p gg f", p=128)
                        [:, HC0[half]:HC0[half] + HCN[half], :]), "dG", 16)
                gensets = [("psA", "psB"), ("psC", "psF"), ("psD", "psE")]
                genb = [(ps["A"], ps["B"]), (ps["C"], ps["F"]), (ps["D"], ps["E"])]
                ucount = 0
                for wgi, hc in enumerate(HGRP[half]):
                    buf = wgi % 2
                    w1b = sb["w1g"][buf]
                    w3b = sb["w3g"][buf]
                    p.guard_wait(g, ("w1g", buf))
                    p.inc(g.dma_start(
                        out=w1b[:].rearrange("p (c h) -> p c h", c=NCH),
                        in_=io["w1"][e].rearrange("(c p) h -> p c h", p=128)
                            [:, :, hc * 128:(hc + 1) * 128]), "dG", 16)
                    dg_ready = p.inc(g.dma_start(
                        out=w3b[:].rearrange("p (c h) -> p c h", c=NCH),
                        in_=io["w3"][e].rearrange("(c p) h -> p c h", p=128)
                            [:, :, hc * 128:(hc + 1) * 128]), "dG", 16)
                    p.wait(t, "dG", dg_ready)
                    hcL = hc - HC0[half]
                    for m in range(2):
                        gsel = ucount % 3
                        ucount += 1
                        z1, z3 = genb[gsel]
                        z1n, z3n = gensets[gsel]
                        tg = hh * 2 + m
                        tsl = sb["tmpA"][:, m * 512:(m + 1) * 512]
                        p.guard_wait(t, z1n)
                        p.guard_wait(t, z3n)
                        for c in range(NCH):
                            nss = sb["normed"][:, c * TOK + tg * 512:c * TOK + (tg + 1) * 512]
                            t.matmul(z1[:], w1b[:, c * 128:(c + 1) * 128],
                                     nss, start=(c == 0), stop=(c == NCH - 1))
                        for c in range(NCH):
                            nss = sb["normed"][:, c * TOK + tg * 512:c * TOK + (tg + 1) * 512]
                            mi = t.matmul(z3[:], w3b[:, c * 128:(c + 1) * 128],
                                          nss, start=(c == 0), stop=(c == NCH - 1))
                        p.inc(mi, "sP", 1)
                        p.wait(v, "sP")
                        v.tensor_tensor(tsl, z1[:],
                                        sb["g_b"][:, m * 512:(m + 1) * 512], op=ALU.mult)
                        p.mile(v, "sV")
                        p.guard_set(z1n, "sV")
                        p.wait(a, "sV")
                        if first_h_write:
                            p.guard_wait(a, "h")
                            first_h_write = False
                        a.activation(sb["h"][:, hcL * 1024 + m * 512:hcL * 1024 + (m + 1) * 512],
                                     tsl, AF.Silu)
                        p.mile(a, "sA")
                        p.wait(v, "sA")
                        v.tensor_tensor(
                            sb["h"][:, hcL * 1024 + m * 512:hcL * 1024 + (m + 1) * 512],
                            sb["h"][:, hcL * 1024 + m * 512:hcL * 1024 + (m + 1) * 512],
                            z3[:], op=ALU.mult)
                        p.mile(v, "sV")
                        p.guard_set(z3n, "sV")
                    p.guard_set(("w1g", buf), "sP")
                h_ready = p.cnt("sV")
                p.wait(t, "dG", w2_ready)
                p.wait(t, "sV", h_ready)
                for oc in range(NCH):
                    for m in range(2):
                        ob2 = [ps["A"], ps["B"]]
                        ob2n = ["psA", "psB"]
                        u = oc * 2 + m
                        bank = ob2[u % 2]
                        bname = ob2n[u % 2]
                        tg = hh * 2 + m
                        p.guard_wait(t, bname)
                        for gg in range(HCN[half]):
                            mi = t.matmul(bank[:],
                                          w2b[:, gg * 1024 + oc * 128:gg * 1024 + (oc + 1) * 128],
                                          sb["h"][:, gg * 1024 + m * 512:gg * 1024 + (m + 1) * 512],
                                          start=(gg == 0), stop=(gg == HCN[half] - 1))
                        p.inc(mi, "sP", 1)
                        p.wait(v, "sP")
                        xs = x[:, oc * TOK + tg * 512:oc * TOK + (tg + 1) * 512]
                        v.scalar_tensor_tensor(sb["tmpA"][:, m * 512:(m + 1) * 512], bank[:],
                                               0.5, sb["g_b"][:, m * 512:(m + 1) * 512],
                                               op0=ALU.mult, op1=ALU.mult)
                        p.mile(v, "sV")
                        p.guard_set(bname, "sV")
                        v.tensor_tensor(xs, xs, sb["tmpA"][:, m * 512:(m + 1) * 512],
                                        op=ALU.add)
                    p.inc(v.drain(), "sV", 1)
                p.guard_set("w2s", "sP")
                p.guard_set("h", "sP")
                first_h_write = True
            p.guard_set("g_b", "sV")
    p.guard_set("gwT", "sP")




def emit_exchange_flags(p, sb, io, exch, SLpar):
    nc = p.nc
    s = nc.sync
    ALUo = ALU
    p.wait(s, "dS")
    fl = io["sh_flags"].rearrange("(e sl) one -> e sl one", sl=2)
    di = s.dma_start(out=fl[exch, SLpar, :],
                     in_=sb["nonces"][exch:exch + 1, :])
    p.inc(di, "dS", 16)
    p.wait(s, "dS")
    nr = s.alloc_register(f"nr{exch}")
    s.load(nr, sb["nonces"][exch:exch + 1, 0:1])
    for slot in range(2):
        c = s.alloc_register(f"c{exch}_{slot}")
        f = s.alloc_register(f"f{exch}_{slot}")
        s.reg_mov(c, 1)
        with s.While(c):
            s.load(f, io["sh_flags"][exch * 2 + slot:exch * 2 + slot + 1, 0:1])
            s.reg_alu(c, f, nr, ALUo.not_equal)
    s.nop().then_inc(p._sems["sG"], 1)
    p._cnt["sG"] += 1


def emit_attn0(p, sb, ps, io, ctx):
    nc = p.nc
    t, v, a, g, s = nc.tensor, nc.vector, nc.scalar, nc.gpsimd, nc.sync
    x = sb["x"]
    NT = T // 512

    with ExitStack() as st:
        V_sb = st.enter_context(nc.sbuf_tensor("at_V", [128, 32 * 520], BF16))
        lncols = st.enter_context(nc.sbuf_tensor("at_lncols", [128, 4], F32))
        negU = st.enter_context(nc.sbuf_tensor("at_negU", [128, 8], F32))
        nrm8 = st.enter_context(nc.sbuf_tensor("at_nrm8", [8, 8], F32))
        bocols = st.enter_context(nc.sbuf_tensor("at_bo", [128, 8], F32))
        sb["selH"] = st.enter_context(nc.sbuf_tensor("at_selH", [128, 32], F32))
        sb["selT"] = st.enter_context(nc.sbuf_tensor("at_selT", [8, 4 * 128], F32))
        sb["trimask"] = st.enter_context(nc.sbuf_tensor("at_trimask", [128, 128], F32))
        p.fence(s)
        for nm in ("selH", "selT", "trimask"):
            p.inc(s.dma_start(out=sb[nm][:], in_=io[nm]), "dS", 16)
        lnv = {}
        for nm in ("qln_g", "qln_b", "kln_g", "kln_b"):
            lnv[nm] = st.enter_context(nc.sbuf_tensor("at_" + nm, [128, 1], F32))
            p.inc(s.dma_start(out=lnv[nm][:], in_=io[nm]), "dS", 16)
        p.inc(s.dma_start(out=bocols[:], in_=io["l0_bo_cols"]), "dS", 16)
        p.wait(s, "dS")
        preg_s = nc.values_load(sb["par"][0:1, 0:1], engines=[s.engine],
                                skip_runtime_bounds_check=True)
        SL = bass.ts(preg_s, 1)

        with ExitStack() as st2:
            stage = st2.enter_context(nc.sbuf_tensor("at_nstage", [128, NCH * 512], BF16))
            emit_ln(p, sb, ps, to_dram=(io["normed_loc"], stage))
            p.wait(s, "dS")
            shn = io["sh_normed0"].rearrange("r (hf t2) -> r hf t2", hf=2)
            di = s.dma_start(out=shn[:, SL, :],
                             in_=io["normed_loc"].rearrange("r (one t) -> r one t", one=1))
            p.inc(di, "dS", 16)
            emit_exchange_flags(p, sb, io, 0, SL)

        p.wait(v, "dS")
        v.tensor_scalar_mul(lncols[:, 0:1], lnv["qln_g"][:], SCALE0)
        v.tensor_scalar_mul(lncols[:, 1:2], lnv["qln_b"][:], SCALE0)
        v.tensor_scalar_mul(lncols[:, 2:3], lnv["kln_g"][:], SCALE0)
        v.tensor_scalar_mul(lncols[:, 3:4], lnv["kln_b"][:], SCALE0)
        v.memset(V_sb[:], 1.0)
        p.mile(v, "sV")

        with ExitStack() as st2:
            wq = st2.enter_context(nc.sbuf_tensor("at_wq", [128, NCH * 512], BF16))
            wk = st2.enter_context(nc.sbuf_tensor("at_wk", [128, NCH * 512], BF16))
            wv = st2.enter_context(nc.sbuf_tensor("at_wv", [128, NCH * 512], BF16))
            nstream = st2.enter_context(nc.sbuf_tensor("at_nst", [128, NCH * 512], BF16))
            stg = st2.enter_context(nc.sbuf_tensor("at_stg", [128, 4 * 512], BF16))
            qtmp = st2.enter_context(nc.sbuf_tensor("at_qtmp", [128, 4 * 512], F32))
            hrows = st2.enter_context(nc.sbuf_tensor("at_hrows", [8, 3 * 512], F32))
            p.wait(g, "dS")
            for nm, ionm in ((wq, "l0_wq"), (wk, "l0_wk"), (wv, "l0_wv")):
                p.inc(g.dma_start(out=nm[:].rearrange("p (c o) -> p c o", c=NCH),
                                  in_=io[ionm].rearrange("(c p) o -> p c o", p=128)),
                      "dG", 16)
            p.wait(g, "sG")
            nsh = io["sh_normed0"].rearrange("(c p) t -> p c t", p=128)
            pq = [ps["A"], ps["B"], ps["C"], ps["F"]]
            pqn = ["psA", "psB", "psC", "psF"]
            stats1, stats2 = ps["D"], ps["E"]
            for tg in range(NT):
                p.guard_wait(g, "nstream")
                di = g.dma_start(out=nstream[:].rearrange("p (c t2) -> p c t2", c=NCH),
                                 in_=nsh[:, :, tg * 512:(tg + 1) * 512])
                nst_ready = p.inc(di, "dG", 16)
                p.wait(t, "dG", nst_ready)
                for proj in range(2):
                    wgt = wq if proj == 0 else wk
                    gcol = lncols[:, 2 * proj:2 * proj + 1]
                    bcol = lncols[:, 2 * proj + 1:2 * proj + 2]
                    dr = io["q_dr"] if proj == 0 else io["k_dr"]
                    for oc in range(4):
                        p.guard_wait(t, pqn[oc])
                        for kc in range(NCH):
                            mi = t.matmul(pq[oc][:],
                                          wgt[:, kc * 512 + oc * 128:kc * 512 + (oc + 1) * 128],
                                          nstream[:, kc * 512:(kc + 1) * 512],
                                          start=(kc == 0), stop=(kc == NCH - 1))
                        p.inc(mi, "sP", 1)
                    p.wait(a, "sP")
                    for oc in range(4):
                        a.copy(qtmp[:, oc * 512:(oc + 1) * 512], pq[oc][:])
                    p.mile(a, "sA")
                    for oc in range(4):
                        p.guard_set(pqn[oc], "sA")
                    p.guard_wait(t, "psD")
                    p.guard_wait(t, "psE")
                    p.wait(t, "sA")
                    for oc in range(4):
                        sqb = sb["tmpB"] if oc % 2 == 0 else sb["tmpA"]
                        p.guard_wait(a, ("sqB", oc % 2))
                        a.activation(sqb[:, 0:512], qtmp[:, oc * 512:(oc + 1) * 512], AF.Square)
                        p.mile(a, "sA")
                        p.wait(t, "sA")
                        t.matmul(stats1[0:8, :], sb["selH"][:, oc * 8:(oc + 1) * 8],
                                 qtmp[:, oc * 512:(oc + 1) * 512],
                                 start=(oc == 0), stop=(oc == 3))
                        mi = t.matmul(stats2[0:8, :], sb["selH"][:, oc * 8:(oc + 1) * 8],
                                      sqb[:, 0:512], start=(oc == 0), stop=(oc == 3))
                        p.inc(mi, "sP", 1)
                        p.guard_set(("sqB", oc % 2), "sP")
                    p.wait(v, "sP")
                    v.tensor_scalar_mul(hrows[0:8, 0:512], stats1[0:8, :], 1.0 / DH)
                    v.tensor_scalar_mul(hrows[0:8, 512:1024], stats2[0:8, :], 1.0 / DH)
                    v.drain()
                    v.tensor_tensor(hrows[0:8, 1024:1536], hrows[0:8, 0:512],
                                    hrows[0:8, 0:512], op=ALU.mult)
                    v.drain()
                    v.tensor_tensor(hrows[0:8, 512:1024], hrows[0:8, 512:1024],
                                    hrows[0:8, 1024:1536], op=ALU.subtract)
                    v.drain()
                    v.tensor_scalar_add(hrows[0:8, 512:1024], hrows[0:8, 512:1024], EPS)
                    p.mile(v, "sV")
                    p.guard_set("psD", "sV")
                    p.guard_set("psE", "sV")
                    p.wait(a, "sV")
                    a.activation(hrows[0:8, 1024:1536], hrows[0:8, 512:1024], AF.Ln)
                    a.drain()
                    a.activation(hrows[0:8, 512:1024], hrows[0:8, 1024:1536], AF.Exp,
                                 scale=-0.5)
                    p.mile(a, "sA")
                    p.wait(t, "sA")
                    for oc in range(4):
                        p.guard_wait(t, "psD")
                        p.guard_wait(t, "psE")
                        t.matmul(stats1[:], sb["selT"][0:8, oc * 128:(oc + 1) * 128],
                                 hrows[0:8, 0:512], start=True, stop=True)
                        mi = t.matmul(stats2[:], sb["selT"][0:8, oc * 128:(oc + 1) * 128],
                                      hrows[0:8, 512:1024], start=True, stop=True)
                        p.inc(mi, "sP", 1)
                        p.wait(v, "sP")
                        v.tensor_tensor(sb["tmpA"][:, 0:512], qtmp[:, oc * 512:(oc + 1) * 512],
                                        stats1[:], op=ALU.subtract)
                        v.drain()
                        v.tensor_tensor(sb["tmpB"][:, 0:512], sb["tmpA"][:, 0:512],
                                        stats2[:], op=ALU.mult)
                        v.drain()
                        v.tensor_scalar(stg[:, oc * 512:(oc + 1) * 512], sb["tmpB"][:, 0:512],
                                        gcol, bcol, op0=ALU.mult, op1=ALU.add)
                        p.mile(v, "sV")
                        p.guard_set("psD", "sV")
                        p.guard_set("psE", "sV")
                    p.guard_wait(t, "psD")
                    p.wait(t, "sV")
                    p.wait(a, "sV")
                    for oc in range(4):
                        p.guard_wait(a, ("sqB", 0))
                        a.activation(sb["tmpB"][:, 0:512], stg[:, oc * 512:(oc + 1) * 512],
                                     AF.Square)
                        p.mile(a, "sA")
                        p.wait(t, "sA")
                        mi = t.matmul(stats1[0:8, :], sb["selH"][:, oc * 8:(oc + 1) * 8],
                                      sb["tmpB"][:, 0:512], start=(oc == 0), stop=(oc == 3))
                        p.inc(mi, "sP", 1)
                        p.guard_set(("sqB", 0), "sP")
                    p.wait(v, "sP")
                    v.tensor_reduce(sb["tmpA"][0:8, 0:1], stats1[0:8, :], axis=AXX, op=ALU.max)
                    v.drain()
                    if tg == 0:
                        v.tensor_copy(nrm8[0:8, proj:proj + 1], sb["tmpA"][0:8, 0:1])
                    else:
                        v.tensor_tensor(nrm8[0:8, proj:proj + 1], nrm8[0:8, proj:proj + 1],
                                        sb["tmpA"][0:8, 0:1], op=ALU.max)
                    p.mile(v, "sV")
                    p.guard_set("psD", "sV")
                    p.wait(s, "sV")
                    p.guard_wait(s, "stg")
                    di = s.dma_start(out=dr.rearrange("(oc p2) t -> p2 oc t", p2=128)
                                     [:, :, tg * 512:(tg + 1) * 512],
                                     in_=stg[:].rearrange("p (oc t2) -> p oc t2", oc=4))
                    p.inc(di, "dS", 16)
                    p.guard_set("stg", "dS")
                for ttl in range(4):
                    tt = tg * 4 + ttl
                    bank = pq[ttl % 2]
                    p.guard_wait(t, pqn[ttl % 2])
                    for kc in range(NCH):
                        mi = t.matmul(bank[:],
                                      nstream[:, kc * 512 + ttl * 128:kc * 512 + (ttl + 1) * 128],
                                      wv[:, kc * 512:(kc + 1) * 512],
                                      start=(kc == 0), stop=(kc == NCH - 1))
                    p.inc(mi, "sP", 1)
                    p.wait(v, "sP")
                    vv = V_sb[:].rearrange("p (tt hb d) -> p tt hb d", tt=32, hb=8)
                    v.tensor_copy(vv[:, tt, :, 0:64],
                                  bank[:].rearrange("p (hb d) -> p hb d", hb=8))
                    p.mile(v, "sV")
                    p.guard_set(pqn[ttl % 2], "sV")
                p.guard_set("nstream", "sP")

        v.tensor_tensor(nrm8[0:8, 2:3], nrm8[0:8, 0:1], nrm8[0:8, 1:2], op=ALU.mult)
        v.drain()
        p.mile(v, "sV")
        p.wait(a, "sV")
        a.activation(nrm8[0:8, 3:4], nrm8[0:8, 2:3], AF.Ln)
        a.drain()
        a.activation(nrm8[0:8, 4:5], nrm8[0:8, 3:4], AF.Exp, scale=0.5)
        p.mile(a, "sA")
        p.wait(t, "sA")
        p.guard_wait(t, "psD")
        ti = t.transpose(ps["D"][0:1, 0:8], nrm8[0:8, 4:5], sb["eye_f"][0:8, 0:8])
        p.inc(ti, "sP", 1)
        p.wait(v, "sP")
        v.tensor_copy(sb["tmpA"][0:1, 0:8], ps["D"][0:1, 0:8])
        p.mile(v, "sV")
        p.wait(t, "sV")
        ti = t.matmul(ps["D"][:, 0:8], sb["ones_row"][:], sb["tmpA"][0:1, 0:8],
                      start=True, stop=True)
        p.inc(ti, "sP", 1)
        p.wait(a, "sP")
        a.activation(negU[:], ps["D"][:, 0:8], AF.Copy, scale=-1.0)
        p.mile(a, "sA")
        p.guard_set("psD", "sA")

        with ExitStack() as st2:
            kst = [st2.enter_context(nc.sbuf_tensor(f"at_kst{i}", [64, T], BF16))
                   for i in range(2)]
            qst = [st2.enter_context(nc.sbuf_tensor(f"at_qst{i}", [64, 512], BF16))
                   for i in range(2)]
            Pt = [st2.enter_context(nc.sbuf_tensor(f"at_pt{i}", [128, 512], BF16))
                  for i in range(2)]
            arow = st2.enter_context(nc.sbuf_tensor("at_arow", [64, 512], BF16))
            sS = [ps["A"], ps["B"]]
            sSn = ["psA", "psB"]
            pO, pOn = ps["C"], "psC"
            pB, pBn = ps["E"], "psE"
            units = [(h, qg) for h in range(NH) for qg in range(NT)]
            p.fence(g)
            for ui, (h, qg) in enumerate(units):
                ktn = qg * 4 + 4
                bufi = ui % 2
                p.guard_wait(g, ("kst", bufi))
                p.inc(g.dma_start(out=kst[bufi][:, 0:ktn * 128],
                                  in_=io["k_dr"][h * 64:(h + 1) * 64, 0:ktn * 128]), "dG", 16)
                p.guard_wait(g, ("qst", bufi))
                dg_ready = p.inc(g.dma_start(out=qst[bufi][:],
                                             in_=io["q_dr"][h * 64:(h + 1) * 64,
                                                            qg * 512:(qg + 1) * 512]),
                                 "dG", 16)
                p.guard_wait(t, pOn)
                p.wait(t, "dG", dg_ready)
                for kt in range(ktn):
                    dead = max(0, kt * 128 - qg * 512)
                    nvalid = 512 - dead
                    sbank = sS[kt % 2]
                    p.guard_wait(t, sSn[kt % 2])
                    mi = t.matmul(sbank[:, 0:nvalid], kst[bufi][:, kt * 128:(kt + 1) * 128],
                                  qst[bufi][:, dead:512], start=True, stop=True)
                    p.inc(mi, "sP", 1)
                    if kt >= qg * 4:
                        p.wait(v, "sP")
                        v.tensor_tensor(sbank[:, 0:128], sbank[:, 0:128], sb["trimask"][:],
                                        op=ALU.add)
                        p.mile(v, "sV")
                        p.wait(a, "sV")
                    p.guard_wait(v, ("Pt", kt % 2))
                    if dead > 0:
                        v.memset(Pt[kt % 2][:, 0:dead], 0.0)
                        p.mile(v, "sV")
                        p.wait(a, "sV")
                    p.wait(a, "sP")
                    a.activation(Pt[kt % 2][:, dead:512], sbank[:, 0:nvalid], AF.Exp,
                                 bias=negU[:, h:h + 1])
                    p.mile(a, "sA")
                    p.guard_set(sSn[kt % 2], "sA")
                    p.wait(t, "sA")
                    mi = t.matmul(pO[0:65, :],
                                  V_sb[:].rearrange("p (tt c) -> p tt c", tt=32)
                                  [:, kt, h * 65:(h + 1) * 65],
                                  Pt[kt % 2][:], start=(kt == 0), stop=(kt == ktn - 1))
                    p.inc(mi, "sP", 1)
                    p.guard_set(("Pt", kt % 2), "sP")
                p.guard_set(("kst", bufi), "sP")
                p.guard_set(("qst", bufi), "sP")
                p.wait(v, "sP")
                p.guard_wait(v, ("lrow",))
                v.tensor_copy(sb["tmpB"][0:1, 0:512], pO[64:65, :])
                v.drain()
                v.reciprocal(sb["tmpB"][0:1, 0:512], sb["tmpB"][0:1, 0:512])
                p.mile(v, "sV")
                p.wait(t, "sV")
                p.guard_wait(t, pBn)
                mi = t.matmul(pB[:], sb["ones_row"][:], sb["tmpB"][0:1, 0:512],
                              start=True, stop=True)
                p.inc(mi, "sP", 1)
                p.guard_set(("lrow",), "sP")
                p.wait(a, "sP")
                p.guard_wait(a, ("rlb",))
                a.copy(sb["tmpA"][0:64, 0:512], pB[0:64, :])
                p.mile(a, "sA")
                p.wait(v, "sA")
                v.tensor_tensor(arow[:], pO[0:64, :], sb["tmpA"][0:64, 0:512], op=ALU.mult)
                p.mile(v, "sV")
                p.guard_set(pOn, "sV")
                p.guard_set(pBn, "sV")
                p.guard_set(("rlb",), "sV")
                p.wait(s, "sV")
                p.guard_wait(s, "arow")
                di = s.dma_start(out=io["attn_dr"][h * 64:(h + 1) * 64,
                                                   qg * 512:(qg + 1) * 512],
                                 in_=arow[:])
                p.inc(di, "dS", 16)
                p.guard_set("arow", "dS")

        with ExitStack() as st2:
            wo = st2.enter_context(nc.sbuf_tensor("at_wo", [128, 4 * 1024], BF16))
            astream = st2.enter_context(nc.sbuf_tensor("at_ast", [128, 4 * 512], BF16))
            stg2 = st2.enter_context(nc.sbuf_tensor("at_stg2", [128, 2 * 512], BF16))
            p.fence(g)
            p.inc(g.dma_start(out=wo[:].rearrange("p (c o) -> p c o", c=4),
                              in_=io["l0_wo"].rearrange("(c p) o -> p c o", p=128)), "dG", 16)
            ash = io["attn_dr"].rearrange("(c p) t -> p c t", p=128)
            p.wait(a, "dS")
            pr = [ps["A"], ps["B"]]
            prn = ["psA", "psB"]
            psh = io["partial_loc"].rearrange("(c p) t -> p c t", p=128)
            for tg in range(NT):
                p.guard_wait(g, "astream")
                di = g.dma_start(out=astream[:].rearrange("p (c t2) -> p c t2", c=4),
                                 in_=ash[:, :, tg * 512:(tg + 1) * 512])
                nst_ready = p.inc(di, "dG", 16)
                p.wait(t, "dG", nst_ready)
                for oc in range(NCH):
                    bank = pr[oc % 2]
                    p.guard_wait(t, prn[oc % 2])
                    for kc in range(4):
                        mi = t.matmul(bank[:],
                                      wo[:, kc * 1024 + oc * 128:kc * 1024 + (oc + 1) * 128],
                                      astream[:, kc * 512:(kc + 1) * 512],
                                      start=(kc == 0), stop=(kc == 3))
                    p.inc(mi, "sP", 1)
                    p.wait(a, "sP")
                    p.guard_wait(a, ("pstg", oc % 2))
                    a.copy(stg2[:, (oc % 2) * 512:(oc % 2 + 1) * 512], bank[:])
                    p.mile(a, "sA")
                    p.guard_set(prn[oc % 2], "sA")
                    p.wait(s, "sA")
                    di = s.dma_start(out=psh[:, oc:oc + 1, tg * 512:(tg + 1) * 512],
                                     in_=stg2[:, (oc % 2) * 512:(oc % 2 + 1) * 512]
                                     .rearrange("p (one t) -> p one t", one=1))
                    p.inc(di, "dS", 16)
                    p.guard_set(("pstg", oc % 2), "dS")
                p.guard_set("astream", "sP")
        p.wait(s, "dS")
        shp = io["sh_partial0"].rearrange("(sl r) t -> sl r t", sl=2)
        di = s.dma_start(out=shp[SL, :, :],
                         in_=io["partial_loc"].rearrange("(one r) t -> one r t", one=1))
        p.inc(di, "dS", 16)
        emit_exchange_flags(p, sb, io, 1, SL)
        with ExitStack() as st2:
            rd = st2.enter_context(nc.sbuf_tensor("at_rd", [128, NCH * 2048], BF16))
            psh = io["sh_partial0"].rearrange("(sl c p) (hf t2) -> sl p c hf t2",
                                              sl=2, p=128, hf=2)
            p.fence(s)
            for slot in range(2):
                p.wait(s, "sG")
                p.guard_wait(s, "rd")
                di = s.dma_start(out=rd[:].rearrange("p (c one t) -> p c one t",
                                                     c=NCH, one=1),
                                 in_=psh[slot][:, :, SL, :])
                rd_ready = p.inc(di, "dS", 16)
                p.wait(v, "dS", rd_ready)
                for c in range(NCH):
                    for tg in range(4):
                        xs = x[:, c * TOK + tg * 512:c * TOK + (tg + 1) * 512]
                        rds = rd[:, c * 2048 + tg * 512:c * 2048 + (tg + 1) * 512]
                        if slot == 0:
                            v.tensor_tensor(sb["tmpA"][:, 0:512], xs, rds, op=ALU.add)
                            v.drain()
                            v.tensor_scalar_add(xs, sb["tmpA"][:, 0:512],
                                                bocols[:, c:c + 1])
                        else:
                            v.tensor_tensor(xs, xs, rds, op=ALU.add)
                p.mile(v, "sV")
                p.guard_set("rd", "sV")




def emit_quant_tile(p, q, out_bf, in_f32, scr):
    nc = p.nc
    v, a = nc.vector, nc.scalar
    bm3 = scr["bm"][:].rearrange("p (b one) -> p b one", one=1)
    in3 = in_f32.rearrange("p (b w) -> p b w", w=32)
    v.tensor_reduce(bm3, in3, axis=AXX, op=ALU.max, apply_absolute_value=True)
    v.drain()
    v.tensor_scalar(scr["sc"][:], scr["bm"][:], 1.0 / 7.5, 1e-30, op0=ALU.mult,
                    op1=ALU.max)
    v.drain()
    v.reciprocal(scr["rs"][:], scr["sc"][:])
    v.drain()
    for b in range(32):
        v.tensor_scalar_mul(scr["vq"][:, b * 32:(b + 1) * 32],
                            in_f32[:, b * 32:(b + 1) * 32], scr["rs"][:, b:b + 1])
    p.mile(v, "sV")
    p.wait(a, "sV")
    a.activation(scr["av"][:], scr["vq"][:], AF.Abs)
    p.mile(a, "sA")
    p.wait(v, "sA")
    v.tensor_single_scalar(scr["g2"][:], scr["av"][:], 2.0, op=ALU.is_ge)
    v.tensor_single_scalar(scr["g4"][:], scr["av"][:], 4.0, op=ALU.is_ge)
    v.drain()
    v.tensor_scalar(scr["rr"][:], scr["g2"][:], -4.0, 8.0, op0=ALU.mult, op1=ALU.add)
    v.tensor_scalar(scr["ss"][:], scr["g2"][:], 0.125, 0.125, op0=ALU.mult, op1=ALU.add)
    v.drain()
    v.scalar_tensor_tensor(scr["rr"][:], scr["g4"][:], -2.0, scr["rr"][:],
                           op0=ALU.mult, op1=ALU.add)
    v.scalar_tensor_tensor(scr["ss"][:], scr["g4"][:], 0.25, scr["ss"][:],
                           op0=ALU.mult, op1=ALU.add)
    v.drain()
    v.tensor_tensor(scr["yy"][:], scr["vq"][:], scr["rr"][:], op=ALU.mult)
    v.drain()
    v.tensor_single_scalar(scr["ff"][:], scr["yy"][:], 12582912.0, op=ALU.add)
    v.drain()
    v.tensor_single_scalar(scr["ff"][:], scr["ff"][:], -12582912.0, op=ALU.add)
    v.drain()
    v.tensor_tensor(scr["g2"][:], scr["ff"][:], scr["yy"][:], op=ALU.subtract)
    v.drain()
    v.tensor_single_scalar(scr["g2"][:], scr["g2"][:], 0.5, op=ALU.is_equal)
    v.drain()
    v.tensor_tensor(scr["ff"][:], scr["ff"][:], scr["g2"][:], op=ALU.subtract)
    v.drain()
    v.tensor_tensor(scr["yy"][:], scr["ff"][:], scr["ss"][:], op=ALU.mult)
    v.drain()
    for b in range(32):
        v.tensor_scalar_mul(out_bf[:, b * 32:(b + 1) * 32],
                            scr["yy"][:, b * 32:(b + 1) * 32], scr["sc"][:, b:b + 1])
    p.mile(v, "sV")


def emit_ffn1(p, sb, ps, io, tag):
    nc = p.nc
    t, v, a, g, s = nc.tensor, nc.vector, nc.scalar, nc.gpsimd, nc.sync
    x = sb["x"]
    emit_ln(p, sb, ps, stats_only=True)
    with ExitStack() as st:
        wd = st.enter_context(nc.sbuf_tensor(f"{tag}_wd", [128, NCH * 1024], BF16))
        xd = st.enter_context(nc.sbuf_tensor(f"{tag}_xd", [128, NCH * TOK], BF16))
        scr = {}
        for nm in ("bm", "sc", "rs"):
            scr[nm] = st.enter_context(nc.sbuf_tensor(f"{tag}_" + nm, [128, 32], F32))
        for nm in ("vq", "av", "g2", "g4", "rr", "ss", "yy", "ff", "wtmp"):
            scr[nm] = st.enter_context(nc.sbuf_tensor(f"{tag}_" + nm, [128, 1024], F32))
        qtile = st.enter_context(nc.sbuf_tensor(f"{tag}_qt", [128, 1024], BF16))
        hbias = st.enter_context(nc.sbuf_tensor(f"{tag}_hb", [128, 8], F32))
        bcols = st.enter_context(nc.sbuf_tensor(f"{tag}_bc", [128, 8], F32))
        p.fence(s)
        p.inc(s.dma_start(out=bcols[:], in_=io["l1fb_cols"]), "dS", 16)
        p.wait(v, "dS")
        v.tensor_scalar_mul(hbias[:], bcols[:], 0.5)
        p.mile(v, "sV")
        pT = ps["H"]
        for rc in range(NCH):
            p.guard_wait(s, "wtmp")
            di = s.dma_start(out=scr["wtmp"][:],
                             in_=io["l1fw"][rc * 128:(rc + 1) * 128, :])
            p.inc(di, "dS", 16)
            p.wait(v, "dS")
            emit_quant_tile(p, None, qtile[:], scr["wtmp"][:], scr)
            p.guard_set("wtmp", "sV")
            p.wait(t, "sV")
            p.guard_wait(t, "psH")
            for cc in range(NCH):
                ti = t.transpose(pT[:, cc * 128:(cc + 1) * 128],
                                 qtile[:, cc * 128:(cc + 1) * 128], sb["eye_b"][:])
            p.inc(ti, "sP", 1)
            p.wait(v, "sP")
            for cc in range(NCH):
                v.tensor_copy(wd[:, cc * 1024 + rc * 128:cc * 1024 + (rc + 1) * 128],
                              pT[:, cc * 128:(cc + 1) * 128])
            p.mile(v, "sV")
            p.guard_set("psH", "sV")
        pF = ps["A"]
        for tile in range(16):
            m_col = sb["mr_cols"][:, tile:tile + 1]
            r_col = sb["mr_cols"][:, 16 + tile:17 + tile]
            for half in range(2):
                p.wait(t, "sV")
                p.guard_wait(t, "psA")
                for cl in range(4):
                    c = half * 4 + cl
                    ti = t.transpose(pF[:, cl * 128:(cl + 1) * 128],
                                     x[:, c * TOK + tile * 128:c * TOK + (tile + 1) * 128],
                                     sb["eye_f"][:])
                p.inc(ti, "sP", 1)
                p.wait(v, "sP")
                if half == 0:
                    p.guard_wait(v, "wtmp")
                v.tensor_scalar(scr["wtmp"][:, half * 512:(half + 1) * 512], pF[:],
                                m_col, r_col, op0=ALU.subtract, op1=ALU.mult)
                p.mile(v, "sV")
                p.guard_set("psA", "sV")
            emit_quant_tile(p, None, qtile[:], scr["wtmp"][:], scr)
            p.guard_set("wtmp", "sV")
            p.wait(t, "sV")
            p.guard_wait(t, "psH")
            for c in range(NCH):
                ti = t.transpose(pT[:, c * 128:(c + 1) * 128],
                                 qtile[:, c * 128:(c + 1) * 128], sb["eye_b"][:])
            p.inc(ti, "sP", 1)
            p.wait(v, "sP")
            for c in range(NCH):
                v.tensor_copy(xd[:, c * TOK + tile * 128:c * TOK + (tile + 1) * 128],
                              pT[:, c * 128:(c + 1) * 128])
            p.mile(v, "sV")
            p.guard_set("psH", "sV")
        xd_ready = p.cnt("sV")
        p.wait(t, "sV", xd_ready)
        pr = [ps["B"], ps["C"]]
        prn = ["psB", "psC"]
        for oc in range(NCH):
            for tg in range(4):
                bank = pr[(oc * 4 + tg) % 2]
                bname = prn[(oc * 4 + tg) % 2]
                p.guard_wait(t, bname)
                for kc in range(NCH):
                    mi = t.matmul(bank[:],
                                  wd[:, kc * 1024 + oc * 128:kc * 1024 + (oc + 1) * 128],
                                  xd[:, kc * TOK + tg * 512:kc * TOK + (tg + 1) * 512],
                                  start=(kc == 0), stop=(kc == NCH - 1))
                p.inc(mi, "sP", 1)
                p.wait(v, "sP")
                p.guard_wait(v, ("fstg", 0))
                v.tensor_scalar(sb["tmpB"][:, 0:512], bank[:], 0.5, hbias[:, oc:oc + 1],
                                op0=ALU.mult, op1=ALU.add)
                v.drain()
                p.mile(v, "sV")
                p.guard_set(bname, "sV")
                xs = x[:, oc * TOK + tg * 512:oc * TOK + (tg + 1) * 512]
                v.tensor_tensor(xs, xs, sb["tmpB"][:, 0:512], op=ALU.add)
                p.mile(v, "sV")
                p.guard_set(("fstg", 0), "sV")




def emit_attn1(p, sb, ps, io, ctx):
    nc = p.nc
    t, v, a, g, s = nc.tensor, nc.vector, nc.scalar, nc.gpsimd, nc.sync
    x = sb["x"]
    NT = T // 512

    with ExitStack() as st:
        V_sb = st.enter_context(nc.sbuf_tensor("a1_V", [128, 32 * 520], BF16))
        krs = st.enter_context(nc.sbuf_tensor("a1_krs", [32, T], BF16))
        negU = st.enter_context(nc.sbuf_tensor("a1_negU", [128, 8], F32))
        nrm8 = st.enter_context(nc.sbuf_tensor("a1_nrm8", [8, 8], F32))
        crow = st.enter_context(nc.sbuf_tensor("a1_crow", [1, 8], F32))
        sb["selH"] = st.enter_context(nc.sbuf_tensor("a1_selH", [128, 32], F32))
        sb["selHr"] = st.enter_context(nc.sbuf_tensor("a1_selHr", [128, 16], F32))
        sb["trimask"] = st.enter_context(nc.sbuf_tensor("a1_trimask", [128, 128], F32))
        p.fence(s)
        for nm in ("selH", "selHr", "trimask"):
            p.inc(s.dma_start(out=sb[nm][:], in_=io[nm]), "dS", 16)
        p.inc(s.dma_start(out=crow[:], in_=io["lobo"]), "dS", 16)
        p.wait(s, "dS")
        preg_s = nc.values_load(sb["par"][0:1, 0:1], engines=[s.engine],
                                skip_runtime_bounds_check=True)
        SL = bass.ts(preg_s, 1)

        p.wait(a, "dS")
        a.activation(crow[:], crow[:], AF.Exp)
        p.mile(a, "sA")

        with ExitStack() as st2:
            stage = st2.enter_context(nc.sbuf_tensor("a1_nstage", [128, NCH * 512], BF16))
            emit_ln(p, sb, ps, to_dram=(io["normed_loc"], stage))
            p.wait(s, "dS")
            shn = io["sh_normed1"].rearrange("r (hf t2) -> r hf t2", hf=2)
            di = s.dma_start(out=shn[:, SL, :],
                             in_=io["normed_loc"].rearrange("r (one t) -> r one t", one=1))
            p.inc(di, "dS", 16)
            emit_exchange_flags(p, sb, io, 2, SL)

        v.memset(V_sb[:], 1.0)
        p.mile(v, "sV")

        with ExitStack() as st2:
            wq = st2.enter_context(nc.sbuf_tensor("a1_wq", [128, NCH * 512], BF16))
            wqr = st2.enter_context(nc.sbuf_tensor("a1_wqr", [128, NCH * 512], BF16))
            wkvd = st2.enter_context(nc.sbuf_tensor("a1_wkvd", [128, NCH * 256], BF16))
            wkr = st2.enter_context(nc.sbuf_tensor("a1_wkr", [128, NCH * 64], BF16))
            wku = st2.enter_context(nc.sbuf_tensor("a1_wku", [128, 2 * 512], BF16))
            wvu = st2.enter_context(nc.sbuf_tensor("a1_wvu", [128, 2 * 512], BF16))
            nstream = st2.enter_context(nc.sbuf_tensor("a1_nst", [128, NCH * 512], BF16))
            stg = st2.enter_context(nc.sbuf_tensor("a1_stg", [128, 4 * 512], BF16))
            lat = st2.enter_context(nc.sbuf_tensor("a1_lat", [128, 2 * 512], BF16))
            rope = [st2.enter_context(nc.sbuf_tensor(f"a1_rope{i}", [128, 2 * 512], F32))
                    for i in range(2)]
            p.fence(g)
            for nm, ionm in ((wq, "l1_wq"), (wqr, "l1_wqr_cat"), (wkvd, "l1_wkvd"),
                             (wkr, "l1_wkr_cat"), (wku, "l1_wku"), (wvu, "l1_wvu")):
                ncol = nm.shape[1] // NCH if nm not in (wku, wvu) else 512
                p.inc(g.dma_start(out=nm[:].rearrange("p (c o) -> p c o",
                                                      c=NCH if nm not in (wku, wvu) else 2),
                                  in_=io[ionm].rearrange("(c p) o -> p c o", p=128)),
                      "dG", 16)
            p.wait(g, "sG")
            nsh = io["sh_normed1"].rearrange("(c p) t -> p c t", p=128)
            pq = [ps["A"], ps["B"], ps["C"], ps["F"]]
            pqn = ["psA", "psB", "psC", "psF"]
            stats1, stats2 = ps["D"], ps["E"]
            for tg in range(NT):
                p.guard_wait(g, "nstream")
                di = g.dma_start(out=nstream[:].rearrange("p (c t2) -> p c t2", c=NCH),
                                 in_=nsh[:, :, tg * 512:(tg + 1) * 512])
                nst_ready = p.inc(di, "dG", 16)
                rb = rope[tg % 2]
                p.guard_wait(g, ("rope", tg % 2))
                p.inc(g.dma_start(out=rb[:, 0:512],
                                  in_=io["ropecos"][:, tg * 512:(tg + 1) * 512]), "dG", 16)
                rope_ready = p.inc(g.dma_start(out=rb[:, 512:1024],
                                               in_=io["ropesin"][:, tg * 512:(tg + 1) * 512]),
                                   "dG", 16)
                p.wait(t, "dG", nst_ready)
                for oc in range(4):
                    p.guard_wait(t, pqn[oc])
                    for kc in range(NCH):
                        mi = t.matmul(pq[oc][:],
                                      wq[:, kc * 512 + oc * 128:kc * 512 + (oc + 1) * 128],
                                      nstream[:, kc * 512:(kc + 1) * 512],
                                      start=(kc == 0), stop=(kc == NCH - 1))
                    p.inc(mi, "sP", 1)
                p.wait(v, "sP")
                for oc in range(4):
                    v.tensor_copy(stg[:, oc * 512:(oc + 1) * 512], pq[oc][:])
                p.mile(v, "sV")
                for oc in range(4):
                    p.guard_set(pqn[oc], "sV")
                emit_norm_update(p, sb, ps, stg, nrm8, 0, tg, 4, "selH")
                p.wait(s, "sV")
                p.guard_wait(s, "stg")
                di = s.dma_start(out=io["q_dr"].rearrange("(oc p2) t -> p2 oc t", p2=128)
                                 [:, :, tg * 512:(tg + 1) * 512],
                                 in_=stg[:].rearrange("p (oc t2) -> p oc t2", oc=4))
                p.inc(di, "dS", 16)
                p.guard_set("stg", "dS")
                for oc in range(4):
                    p.guard_wait(t, pqn[oc])
                    for kc in range(NCH):
                        mi = t.matmul(pq[oc][:],
                                      wqr[:, kc * 512 + oc * 128:kc * 512 + (oc + 1) * 128],
                                      nstream[:, kc * 512:(kc + 1) * 512],
                                      start=(kc == 0), stop=(kc == NCH - 1))
                    p.inc(mi, "sP", 1)
                p.wait(v, "sP")
                p.wait(v, "dG", rope_ready)
                p.guard_wait(v, "stg")
                for c in range(2):
                    v.tensor_tensor(sb["tmpA"][:, 0:512], pq[c][:], rb[:, 0:512], op=ALU.mult)
                    v.drain()
                    v.tensor_tensor(sb["tmpB"][:, 0:512], pq[2 + c][:], rb[:, 512:1024],
                                    op=ALU.mult)
                    v.drain()
                    v.tensor_tensor(stg[:, c * 512:(c + 1) * 512], sb["tmpA"][:, 0:512],
                                    sb["tmpB"][:, 0:512], op=ALU.add)
                    v.drain()
                p.mile(v, "sV")
                for oc in range(4):
                    p.guard_set(pqn[oc], "sV")
                emit_norm_update(p, sb, ps, stg, nrm8, 1, tg, 2, "selHr")
                p.wait(s, "sV")
                di = s.dma_start(out=io["qr_dr"].rearrange("(oc p2) t -> p2 oc t", p2=128)
                                 [:, :, tg * 512:(tg + 1) * 512],
                                 in_=stg[:, 0:1024].rearrange("p (oc t2) -> p oc t2", oc=2))
                p.inc(di, "dS", 16)
                p.guard_set("stg", "dS")
                for oc in range(2):
                    p.guard_wait(t, pqn[oc])
                    for kc in range(NCH):
                        mi = t.matmul(pq[oc][:],
                                      wkvd[:, kc * 256 + oc * 128:kc * 256 + (oc + 1) * 128],
                                      nstream[:, kc * 512:(kc + 1) * 512],
                                      start=(kc == 0), stop=(kc == NCH - 1))
                    p.inc(mi, "sP", 1)
                p.wait(v, "sP")
                p.guard_wait(v, "lat")
                for oc in range(2):
                    v.tensor_copy(lat[:, oc * 512:(oc + 1) * 512], pq[oc][:])
                p.mile(v, "sV")
                for oc in range(2):
                    p.guard_set(pqn[oc], "sV")
                p.guard_wait(t, pqn[2])
                for kc in range(NCH):
                    mi = t.matmul(pq[2][0:64, :], wkr[:, kc * 64:(kc + 1) * 64],
                                  nstream[:, kc * 512:(kc + 1) * 512],
                                  start=(kc == 0), stop=(kc == NCH - 1))
                p.inc(mi, "sP", 1)
                p.wait(v, "sP")
                v.tensor_tensor(sb["tmpA"][0:32, 0:512], pq[2][0:32, :], rb[0:32, 0:512],
                                op=ALU.mult)
                v.drain()
                v.tensor_tensor(sb["tmpB"][0:32, 0:512], pq[2][32:64, :], rb[0:32, 512:1024],
                                op=ALU.mult)
                v.drain()
                v.tensor_tensor(krs[:, tg * 512:(tg + 1) * 512], sb["tmpA"][0:32, 0:512],
                                sb["tmpB"][0:32, 0:512], op=ALU.add)
                p.mile(v, "sV")
                p.guard_set(pqn[2], "sV")
                p.guard_set(("rope", tg % 2), "sV")
                p.wait(a, "sV")
                p.guard_wait(a, ("sqB", 0))
                a.activation(sb["tmpB"][0:32, 0:512], krs[:, tg * 512:(tg + 1) * 512],
                             AF.Square)
                p.mile(a, "sA")
                p.wait(t, "sA")
                p.guard_wait(t, "psD")
                mi = t.matmul(stats1[0:1, :], sb["ones1"][0:32, :], sb["tmpB"][0:32, 0:512],
                              start=True, stop=True)
                p.inc(mi, "sP", 1)
                p.guard_set(("sqB", 0), "sP")
                p.wait(v, "sP")
                v.tensor_reduce(sb["tmpA"][0:1, 0:1], stats1[0:1, :], axis=AXX, op=ALU.max)
                v.drain()
                if tg == 0:
                    v.tensor_copy(nrm8[0:1, 3:4], sb["tmpA"][0:1, 0:1])
                else:
                    v.tensor_tensor(nrm8[0:1, 3:4], nrm8[0:1, 3:4], sb["tmpA"][0:1, 0:1],
                                    op=ALU.max)
                p.mile(v, "sV")
                p.guard_set("psD", "sV")
                p.wait(t, "sV")
                for oc in range(4):
                    p.guard_wait(t, pqn[oc])
                    for kc in range(2):
                        mi = t.matmul(pq[oc][:],
                                      wku[:, kc * 512 + oc * 128:kc * 512 + (oc + 1) * 128],
                                      lat[:, kc * 512:(kc + 1) * 512],
                                      start=(kc == 0), stop=(kc == 1))
                    p.inc(mi, "sP", 1)
                p.wait(v, "sP")
                p.guard_wait(v, "stg")
                for oc in range(4):
                    v.tensor_copy(stg[:, oc * 512:(oc + 1) * 512], pq[oc][:])
                p.mile(v, "sV")
                for oc in range(4):
                    p.guard_set(pqn[oc], "sV")
                emit_norm_update(p, sb, ps, stg, nrm8, 2, tg, 4, "selH")
                p.wait(s, "sV")
                di = s.dma_start(out=io["k_dr"].rearrange("(oc p2) t -> p2 oc t", p2=128)
                                 [:, :, tg * 512:(tg + 1) * 512],
                                 in_=stg[:].rearrange("p (oc t2) -> p oc t2", oc=4))
                p.inc(di, "dS", 16)
                p.guard_set("stg", "dS")
                for ttl in range(4):
                    tt = tg * 4 + ttl
                    bank = pq[ttl % 2]
                    p.guard_wait(t, pqn[ttl % 2])
                    for kc in range(2):
                        mi = t.matmul(bank[:],
                                      lat[:, kc * 512 + ttl * 128:kc * 512 + (ttl + 1) * 128],
                                      wvu[:, kc * 512:(kc + 1) * 512],
                                      start=(kc == 0), stop=(kc == 1))
                    p.inc(mi, "sP", 1)
                    p.wait(v, "sP")
                    vv = V_sb[:].rearrange("p (tt hb d) -> p tt hb d", tt=32, hb=8)
                    v.tensor_copy(vv[:, tt, :, 0:64],
                                  bank[:].rearrange("p (hb d) -> p hb d", hb=8))
                    p.mile(v, "sV")
                    p.guard_set(pqn[ttl % 2], "sV")
                p.guard_set("lat", "sP")
                p.guard_set("nstream", "sP")

        v.tensor_tensor(nrm8[0:8, 4:5], nrm8[0:8, 0:1], nrm8[0:8, 1:2], op=ALU.add)
        v.drain()
        p.mile(v, "sV")
        p.wait(t, "sV")
        p.guard_wait(t, "psD")
        ti = t.matmul(ps["D"][0:8, 0:1], sb["ones_row"][0:1, 0:8], nrm8[0:1, 3:4],
                      start=True, stop=True)
        p.inc(ti, "sP", 1)
        p.wait(v, "sP")
        v.tensor_tensor(nrm8[0:8, 5:6], nrm8[0:8, 2:3], ps["D"][0:8, 0:1], op=ALU.add)
        v.drain()
        v.tensor_tensor(nrm8[0:8, 6:7], nrm8[0:8, 4:5], nrm8[0:8, 5:6], op=ALU.mult)
        p.mile(v, "sV")
        p.guard_set("psD", "sV")
        p.wait(a, "sV")
        a.activation(nrm8[0:8, 7:8], nrm8[0:8, 6:7], AF.Ln)
        a.drain()
        a.activation(nrm8[0:8, 6:7], nrm8[0:8, 7:8], AF.Exp, scale=0.5)
        p.mile(a, "sA")
        p.wait(t, "sA")
        p.guard_wait(t, "psD")
        ti = t.transpose(ps["D"][0:1, 0:8], nrm8[0:8, 6:7], sb["eye_f"][0:8, 0:8])
        p.inc(ti, "sP", 1)
        p.wait(v, "sP")
        v.tensor_copy(sb["tmpA"][0:1, 0:8], ps["D"][0:1, 0:8])
        p.mile(v, "sV")
        p.wait(t, "sV")
        ti = t.matmul(ps["D"][:, 0:8], sb["ones_row"][:], sb["tmpA"][0:1, 0:8],
                      start=True, stop=True)
        p.inc(ti, "sP", 1)
        p.wait(a, "sP")
        a.activation(negU[:], ps["D"][:, 0:8], AF.Copy, scale=-SCALE1)
        p.mile(a, "sA")
        p.guard_set("psD", "sA")
        p.wait(a, "sA")
        a.activation(nrm8[0:8, 7:8], nrm8[0:8, 6:7], AF.Exp, scale=-SCALE1)
        p.mile(a, "sA")
        p.wait(t, "sA")
        p.guard_wait(t, "psE")
        ti = t.transpose(ps["E"][0:8, 0:1], crow[0:1, 0:8], sb["eye_f"][0:1, 0:1])
        p.inc(ti, "sP", 1)
        p.wait(v, "sP")
        v.tensor_tensor(nrm8[0:8, 4:5], ps["E"][0:8, 0:1], nrm8[0:8, 7:8], op=ALU.mult)
        v.drain()
        p.mile(v, "sV")
        p.wait(t, "sV")
        ti = t.transpose(ps["E"][0:1, 0:8], nrm8[0:8, 4:5], sb["eye_f"][0:8, 0:8])
        p.inc(ti, "sP", 1)
        p.wait(v, "sP")
        v.tensor_copy(crow[0:1, 0:8], ps["E"][0:1, 0:8])
        p.mile(v, "sV")
        p.guard_set("psE", "sV")

        with ExitStack() as st2:
            kst = [st2.enter_context(nc.sbuf_tensor(f"a1_kst{i}", [64, T], BF16))
                   for i in range(2)]
            qst = [st2.enter_context(nc.sbuf_tensor(f"a1_qst{i}", [64, 512], BF16))
                   for i in range(2)]
            qrst = [st2.enter_context(nc.sbuf_tensor(f"a1_qrst{i}", [32, 512], BF16))
                    for i in range(2)]
            Pt = [st2.enter_context(nc.sbuf_tensor(f"a1_pt{i}", [128, 512], BF16))
                  for i in range(2)]
            arow = st2.enter_context(nc.sbuf_tensor("a1_arow", [64, 512], BF16))
            sS = [ps["A"], ps["B"]]
            sSn = ["psA", "psB"]
            pO, pOn = ps["C"], "psC"
            pB, pBn = ps["E"], "psE"
            units = [(h, qg) for h in range(NH) for qg in range(NT)]
            p.fence(g)
            for ui, (h, qg) in enumerate(units):
                ktn = qg * 4 + 4
                bufi = ui % 2
                p.guard_wait(g, ("kst", bufi))
                p.inc(g.dma_start(out=kst[bufi][:, 0:ktn * 128],
                                  in_=io["k_dr"][h * 64:(h + 1) * 64, 0:ktn * 128]), "dG", 16)
                p.guard_wait(g, ("qst", bufi))
                p.inc(g.dma_start(out=qst[bufi][:],
                                  in_=io["q_dr"][h * 64:(h + 1) * 64,
                                                 qg * 512:(qg + 1) * 512]), "dG", 16)
                dg_ready = p.inc(g.dma_start(out=qrst[bufi][:],
                                             in_=io["qr_dr"][h * 32:(h + 1) * 32,
                                                             qg * 512:(qg + 1) * 512]),
                                 "dG", 16)
                p.guard_wait(t, pOn)
                p.wait(t, "dG", dg_ready)
                for kt in range(ktn):
                    dead = max(0, kt * 128 - qg * 512)
                    nvalid = 512 - dead
                    sbank = sS[kt % 2]
                    p.guard_wait(t, sSn[kt % 2])
                    t.matmul(sbank[:, 0:nvalid], kst[bufi][:, kt * 128:(kt + 1) * 128],
                             qst[bufi][:, dead:512], start=True, stop=False)
                    mi = t.matmul(sbank[:, 0:nvalid], krs[:, kt * 128:(kt + 1) * 128],
                                  qrst[bufi][:, dead:512], start=False, stop=True)
                    p.inc(mi, "sP", 1)
                    if kt >= qg * 4:
                        p.wait(v, "sP")
                        v.tensor_tensor(sbank[:, 0:128], sbank[:, 0:128], sb["trimask"][:],
                                        op=ALU.add)
                        p.mile(v, "sV")
                        p.wait(a, "sV")
                    p.guard_wait(v, ("Pt", kt % 2))
                    if dead > 0:
                        v.memset(Pt[kt % 2][:, 0:dead], 0.0)
                        p.mile(v, "sV")
                        p.wait(a, "sV")
                    p.wait(a, "sP")
                    a.activation(Pt[kt % 2][:, dead:512], sbank[:, 0:nvalid], AF.Exp,
                                 scale=SCALE1, bias=negU[:, h:h + 1])
                    p.mile(a, "sA")
                    p.guard_set(sSn[kt % 2], "sA")
                    p.wait(t, "sA")
                    mi = t.matmul(pO[0:65, :],
                                  V_sb[:].rearrange("p (tt c) -> p tt c", tt=32)
                                  [:, kt, h * 65:(h + 1) * 65],
                                  Pt[kt % 2][:], start=(kt == 0), stop=(kt == ktn - 1))
                    p.inc(mi, "sP", 1)
                    p.guard_set(("Pt", kt % 2), "sP")
                p.guard_set(("kst", bufi), "sP")
                p.guard_set(("qst", bufi), "sP")
                p.wait(v, "sP")
                p.guard_wait(v, ("lrow",))
                v.tensor_copy(sb["tmpB"][0:1, 0:512], pO[64:65, :])
                v.drain()
                v.tensor_single_scalar(sb["tmpB"][0:1, 0:512], sb["tmpB"][0:1, 0:512],
                                       crow[0:1, h:h + 1], op=ALU.add)
                v.drain()
                v.reciprocal(sb["tmpB"][0:1, 0:512], sb["tmpB"][0:1, 0:512])
                p.mile(v, "sV")
                p.wait(t, "sV")
                p.guard_wait(t, pBn)
                mi = t.matmul(pB[:], sb["ones_row"][:], sb["tmpB"][0:1, 0:512],
                              start=True, stop=True)
                p.inc(mi, "sP", 1)
                p.guard_set(("lrow",), "sP")
                p.wait(a, "sP")
                p.guard_wait(a, ("rlb",))
                a.copy(sb["tmpA"][0:64, 0:512], pB[0:64, :])
                p.mile(a, "sA")
                p.wait(v, "sA")
                v.tensor_tensor(arow[:], pO[0:64, :], sb["tmpA"][0:64, 0:512], op=ALU.mult)
                p.mile(v, "sV")
                p.guard_set(pOn, "sV")
                p.guard_set(pBn, "sV")
                p.guard_set(("rlb",), "sV")
                p.wait(s, "sV")
                p.guard_wait(s, "arow")
                di = s.dma_start(out=io["attn_dr"][h * 64:(h + 1) * 64,
                                                   qg * 512:(qg + 1) * 512],
                                 in_=arow[:])
                p.inc(di, "dS", 16)
                p.guard_set("arow", "dS")

        with ExitStack() as st2:
            wo = st2.enter_context(nc.sbuf_tensor("a1_wo", [128, 4 * 1024], BF16))
            astream = st2.enter_context(nc.sbuf_tensor("a1_ast", [128, 4 * 512], BF16))
            stg2 = st2.enter_context(nc.sbuf_tensor("a1_stg2", [128, 2 * 512], BF16))
            p.fence(g)
            p.inc(g.dma_start(out=wo[:].rearrange("p (c o) -> p c o", c=4),
                              in_=io["l1_wo"].rearrange("(c p) o -> p c o", p=128)), "dG", 16)
            ash = io["attn_dr"].rearrange("(c p) t -> p c t", p=128)
            p.wait(a, "dS")
            pr = [ps["A"], ps["B"]]
            prn = ["psA", "psB"]
            psh = io["partial_loc"].rearrange("(c p) t -> p c t", p=128)
            for tg in range(NT):
                p.guard_wait(g, "astream")
                di = g.dma_start(out=astream[:].rearrange("p (c t2) -> p c t2", c=4),
                                 in_=ash[:, :, tg * 512:(tg + 1) * 512])
                nst_ready = p.inc(di, "dG", 16)
                p.wait(t, "dG", nst_ready)
                for oc in range(NCH):
                    bank = pr[oc % 2]
                    p.guard_wait(t, prn[oc % 2])
                    for kc in range(4):
                        mi = t.matmul(bank[:],
                                      wo[:, kc * 1024 + oc * 128:kc * 1024 + (oc + 1) * 128],
                                      astream[:, kc * 512:(kc + 1) * 512],
                                      start=(kc == 0), stop=(kc == 3))
                    p.inc(mi, "sP", 1)
                    p.wait(a, "sP")
                    p.guard_wait(a, ("pstg", oc % 2))
                    a.copy(stg2[:, (oc % 2) * 512:(oc % 2 + 1) * 512], bank[:])
                    p.mile(a, "sA")
                    p.guard_set(prn[oc % 2], "sA")
                    p.wait(s, "sA")
                    di = s.dma_start(out=psh[:, oc:oc + 1, tg * 512:(tg + 1) * 512],
                                     in_=stg2[:, (oc % 2) * 512:(oc % 2 + 1) * 512]
                                     .rearrange("p (one t) -> p one t", one=1))
                    p.inc(di, "dS", 16)
                    p.guard_set(("pstg", oc % 2), "dS")
                p.guard_set("astream", "sP")
        p.wait(s, "dS")
        shp = io["sh_partial1"].rearrange("(sl r) t -> sl r t", sl=2)
        di = s.dma_start(out=shp[SL, :, :],
                         in_=io["partial_loc"].rearrange("(one r) t -> one r t", one=1))
        p.inc(di, "dS", 16)
        emit_exchange_flags(p, sb, io, 3, SL)
        with ExitStack() as st2:
            rd = st2.enter_context(nc.sbuf_tensor("a1_rd", [128, NCH * 2048], BF16))
            psh = io["sh_partial1"].rearrange("(sl c p) (hf t2) -> sl p c hf t2",
                                              sl=2, p=128, hf=2)
            p.fence(s)
            for slot in range(2):
                p.wait(s, "sG")
                p.guard_wait(s, "rd")
                di = s.dma_start(out=rd[:].rearrange("p (c one t) -> p c one t",
                                                     c=NCH, one=1),
                                 in_=psh[slot][:, :, SL, :])
                rd_ready = p.inc(di, "dS", 16)
                p.wait(v, "dS", rd_ready)
                for c in range(NCH):
                    for tg in range(4):
                        xs = x[:, c * TOK + tg * 512:c * TOK + (tg + 1) * 512]
                        rds = rd[:, c * 2048 + tg * 512:c * 2048 + (tg + 1) * 512]
                        v.tensor_tensor(xs, xs, rds, op=ALU.add)
                p.mile(v, "sV")
                p.guard_set("rd", "sV")


def emit_norm_update(p, sb, ps, stg, nrm8, col, tg, nchunks, selname):
    nc = p.nc
    t, v, a = nc.tensor, nc.vector, nc.scalar
    stats1 = ps["D"]
    p.wait(a, "sV")
    p.guard_wait(t, "psD")
    for oc in range(nchunks):
        p.guard_wait(a, ("sqB", 0))
        a.activation(sb["tmpB"][:, 0:512], stg[:, oc * 512:(oc + 1) * 512], AF.Square)
        p.mile(a, "sA")
        p.wait(t, "sA")
        mi = t.matmul(stats1[0:8, :], sb[selname][:, oc * 8:(oc + 1) * 8],
                      sb["tmpB"][:, 0:512], start=(oc == 0), stop=(oc == nchunks - 1))
        p.inc(mi, "sP", 1)
        p.guard_set(("sqB", 0), "sP")
    p.wait(v, "sP")
    v.tensor_reduce(sb["tmpA"][0:8, 0:1], stats1[0:8, :], axis=AXX, op=ALU.max)
    v.drain()
    if tg == 0:
        v.tensor_copy(nrm8[0:8, col:col + 1], sb["tmpA"][0:8, 0:1])
    else:
        v.tensor_tensor(nrm8[0:8, col:col + 1], nrm8[0:8, col:col + 1],
                        sb["tmpA"][0:8, 0:1], op=ALU.max)
    p.mile(v, "sV")
    p.guard_set("psD", "sV")


def emit_postamble(p, io, sb):
    nc = p.nc
    s = nc.sync
    p.wait(s, "sV")
    p.wait(s, "sA")
    p.wait(s, "sP")
    di = s.dma_start(out=io["x_out"].rearrange("(c p) t -> p c t", p=128),
                     in_=sb["x"][:].rearrange("p (c t) -> p c t", c=NCH))
    p.inc(di, "dS", 16)
    if "dbg_rows" in io:
        for i, nm in enumerate(("row_m", "row_r", "row_mr", "row_t")):
            p.inc(s.dma_start(out=io["dbg_rows"][i:i + 1, :], in_=sb[nm][0:1, :]), "dS", 16)
        p.inc(s.dma_start(out=io["dbg_gates"], in_=sb["gates"][0:8, :]), "dS", 16)
        p.inc(s.dma_start(out=io["dbg_normed"].rearrange("(c p) t -> p c t", p=128),
                          in_=sb["normed"][:].rearrange("p (c t) -> p c t", c=NCH)), "dS", 16)
        if "dbg_scr" in sb:
            p.inc(s.dma_start(out=io["dbg_scr"], in_=sb["dbg_scr"][:]), "dS", 16)
    p.wait(s, "dS")



def build_program(plan):
    nc = bass.Bass("TRN2", target_bir_lowering=False)
    io = {}
    io["x_in"] = nc.dram_tensor("x_in", [D, TOK], F32, kind="ExternalInput").ap()
    io["x_out"] = nc.dram_tensor("x_out", [D, TOK], F32, kind="ExternalOutput").ap()
    io["par"] = nc.dram_tensor("par", [1, 1], I32, kind="ExternalInput").ap()
    io["ones1"] = nc.dram_tensor("ones1", [128, 1], F32, kind="ExternalInput").ap()
    io["ones_row"] = nc.dram_tensor("ones_row", [1, 128], F32, kind="ExternalInput").ap()
    io["eye_f"] = nc.dram_tensor("eye_f", [128, 128], F32, kind="ExternalInput").ap()
    io["iotaB"] = nc.dram_tensor("iotaB", [128, 8], F32, kind="ExternalInput").ap()
    io["sel8"] = nc.dram_tensor("sel8", [8, 8 * 128], F32, kind="ExternalInput").ap()
    io["selH"] = nc.dram_tensor("selH", [128, 32], F32, kind="ExternalInput").ap()
    io["selT"] = nc.dram_tensor("selT", [8, 4 * 128], F32, kind="ExternalInput").ap()
    io["selHr"] = nc.dram_tensor("selHr", [128, 16], F32, kind="ExternalInput").ap()
    io["trimask"] = nc.dram_tensor("trimask", [128, 128], F32, kind="ExternalInput").ap()
    io["nonces"] = nc.dram_tensor("nonces", [8, 1], I32, kind="ExternalInput").ap()
    if os.environ.get("MK_DEBUG"):
        io["dbg_rows"] = nc.dram_tensor("dbg_rows", [4, TOK], F32, kind="ExternalOutput").ap()
        io["dbg_gates"] = nc.dram_tensor("dbg_gates", [8, TOK], F32, kind="ExternalOutput").ap()
        io["dbg_normed"] = nc.dram_tensor("dbg_normed", [D, TOK], BF16, kind="ExternalOutput").ap()
        io["dbg_scr"] = nc.dram_tensor("dbg_scr", [128, 320], F32, kind="ExternalOutput").ap()
    has_moe = any(sl in plan for sl in ("moe0", "moe1"))
    if "attn0" in plan:
        io["l0_wq"] = nc.dram_tensor("l0_wq", [D, 512], F32, kind="ExternalInput").ap()
        io["l0_wk"] = nc.dram_tensor("l0_wk", [D, 512], F32, kind="ExternalInput").ap()
        io["l0_wv"] = nc.dram_tensor("l0_wv", [D, 512], F32, kind="ExternalInput").ap()
        io["l0_wo"] = nc.dram_tensor("l0_wo", [512, D], F32, kind="ExternalInput").ap()
        io["qln_g"] = nc.dram_tensor("qln_g", [128, 1], F32, kind="ExternalInput").ap()
        io["qln_b"] = nc.dram_tensor("qln_b", [128, 1], F32, kind="ExternalInput").ap()
        io["kln_g"] = nc.dram_tensor("kln_g", [128, 1], F32, kind="ExternalInput").ap()
        io["kln_b"] = nc.dram_tensor("kln_b", [128, 1], F32, kind="ExternalInput").ap()
        io["l0_bo_cols"] = nc.dram_tensor("l0_bo_cols", [128, 8], F32, kind="ExternalInput").ap()
        io["sh_normed0"] = nc.dram_tensor("sh_normed0", [D, T], BF16,
                                          kind="Internal", addr_space="Shared").ap()
        io["sh_partial0"] = nc.dram_tensor("sh_partial0", [2 * D, T], BF16,
                                           kind="Internal", addr_space="Shared").ap()
        io["q_dr"] = nc.dram_tensor("q_dr", [512, T], BF16, kind="Internal").ap()
        io["normed_loc"] = nc.dram_tensor("normed_loc", [D, TOK], BF16, kind="Internal").ap()
        io["partial_loc"] = nc.dram_tensor("partial_loc", [D, T], BF16, kind="Internal").ap()
        io["k_dr"] = nc.dram_tensor("k_dr", [512, T], BF16, kind="Internal").ap()
        io["attn_dr"] = nc.dram_tensor("attn_dr", [512, T], BF16, kind="Internal").ap()
    if any(sl in plan for sl in ("ffn1a", "ffn1b")):
        io["l1fw"] = nc.dram_tensor("l1fw", [D, D], F32, kind="ExternalInput").ap()
        io["l1fb_cols"] = nc.dram_tensor("l1fb_cols", [128, 8], F32, kind="ExternalInput").ap()
    if "attn1" in plan:
        io["l1_wq"] = nc.dram_tensor("l1_wq", [D, 512], F32, kind="ExternalInput").ap()
        io["l1_wqr_cat"] = nc.dram_tensor("l1_wqr_cat", [D, 512], F32, kind="ExternalInput").ap()
        io["l1_wkvd"] = nc.dram_tensor("l1_wkvd", [D, 256], F32, kind="ExternalInput").ap()
        io["l1_wkr_cat"] = nc.dram_tensor("l1_wkr_cat", [D, 64], F32, kind="ExternalInput").ap()
        io["l1_wku"] = nc.dram_tensor("l1_wku", [256, 512], F32, kind="ExternalInput").ap()
        io["l1_wvu"] = nc.dram_tensor("l1_wvu", [256, 512], F32, kind="ExternalInput").ap()
        io["l1_wo"] = nc.dram_tensor("l1_wo", [512, D], F32, kind="ExternalInput").ap()
        io["lobo"] = nc.dram_tensor("lobo", [1, 8], F32, kind="ExternalInput").ap()
        io["ropecos"] = nc.dram_tensor("ropecos", [128, T], F32, kind="ExternalInput").ap()
        io["ropesin"] = nc.dram_tensor("ropesin", [128, T], F32, kind="ExternalInput").ap()
        io["sh_normed1"] = nc.dram_tensor("sh_normed1", [D, T], BF16,
                                          kind="Internal", addr_space="Shared").ap()
        io["sh_partial1"] = nc.dram_tensor("sh_partial1", [2 * D, T], BF16,
                                           kind="Internal", addr_space="Shared").ap()
        if "attn0" not in plan:
            io["q_dr"] = nc.dram_tensor("q_dr", [512, T], BF16, kind="Internal").ap()
            io["k_dr"] = nc.dram_tensor("k_dr", [512, T], BF16, kind="Internal").ap()
            io["attn_dr"] = nc.dram_tensor("attn_dr", [512, T], BF16, kind="Internal").ap()
            io["normed_loc"] = nc.dram_tensor("normed_loc", [D, TOK], BF16, kind="Internal").ap()
            io["partial_loc"] = nc.dram_tensor("partial_loc", [D, T], BF16, kind="Internal").ap()
        io["qr_dr"] = nc.dram_tensor("qr_dr", [256, T], BF16, kind="Internal").ap()
    if any(sl.startswith("attn") for sl in plan):
        io["sh_flags"] = nc.dram_tensor("sh_flags", [16, 1], I32,
                                        kind="Internal", addr_space="Shared").ap()
    if has_moe:
        io["gate_wT"] = nc.dram_tensor("gate_wT", [D, 8], F32, kind="ExternalInput").ap()
        io["w1"] = nc.dram_tensor("w1", [NEXP, D, HID], F32, kind="ExternalInput").ap()
        io["w2"] = nc.dram_tensor("w2", [NEXP, HID, D], F32, kind="ExternalInput").ap()
        io["w3"] = nc.dram_tensor("w3", [NEXP, D, HID], F32, kind="ExternalInput").ap()

    with ExitStack() as ctx:
        p = Prog(nc, ctx)
        for name in ("dS", "dG", "sP", "sV", "sA", "sG"):
            p.sem(name)
        sb = {}

        def alloc(name, shape, dtype):
            sb[name] = ctx.enter_context(nc.sbuf_tensor("sb_" + name, shape, dtype))
            return sb[name]

        alloc("x", [128, NCH * TOK], F32)
        alloc("ones1", [128, 1], F32)
        alloc("ones_row", [1, 128], F32)
        alloc("eye_f", [128, 128], F32)
        alloc("eye_b", [128, 128], BF16)
        alloc("iotaB", [128, 8], F32)
        alloc("sel8", [8, 8 * 128], F32)
        has_attn = any(sl.startswith("attn") for sl in plan)
        alloc("par", [1, 1], I32)
        alloc("nonces", [8, 1], I32)
        alloc("row_m", [1, TOK], F32)
        alloc("row_r", [1, TOK], F32)
        alloc("row_mr", [1, TOK], F32)
        alloc("row_t", [1, TOK], F32)
        alloc("mr_cols", [128, 32], F32)
        alloc("tmpA", [128, 1024], F32)
        alloc("tmpB", [128, 512], F32)
        if os.environ.get("MK_DEBUG"):
            alloc("dbg_scr", [128, 320], F32)

        ps = {}
        for nm in ("A", "B", "C", "D", "E", "F"):
            ps[nm] = ctx.enter_context(nc.psum_tensor(f"ps{nm}", [128, 512], F32))
        ps["H"] = ctx.enter_context(nc.psum_tensor("psH", [128, 1024], BF16))

        with nc.Block() as block:
            emit_preamble(p, io, sb)
            for name in plan:
                for eng in (nc.tensor, nc.vector, nc.scalar, nc.sync, nc.gpsimd):
                    p.fence(eng)
                if name in ("moe0", "moe1"):
                    emit_moe_sublayer(p, sb, ps, io, name)
                elif name == "attn0":
                    emit_attn0(p, sb, ps, io, ctx)
                elif name in ("ffn1a", "ffn1b"):
                    emit_ffn1(p, sb, ps, io, name)
                elif name == "attn1":
                    emit_attn1(p, sb, ps, io, ctx)
                else:
                    raise NotImplementedError(name)
            emit_postamble(p, io, sb)
    return nc, io



def _host_constants(core):
    iota = np.arange(8, dtype=np.float32)
    sel8 = np.zeros((8, 8 * 128), np.float32)
    for e in range(8):
        sel8[e, e * 128:(e + 1) * 128] = 1.0
    selH = np.zeros((128, 32), np.float32)
    selT = np.zeros((8, 4 * 128), np.float32)
    for c in range(4):
        for pp in range(128):
            h = 2 * c + pp // 64
            selH[pp, c * 8 + h] = 1.0
            selT[h, c * 128 + pp] = 1.0
    selHr = np.zeros((128, 16), np.float32)
    for c in range(2):
        for pp in range(128):
            h = 4 * c + pp // 32
            selHr[pp, c * 8 + h] = 1.0
    qp, kp = np.meshgrid(np.arange(128), np.arange(128), indexing="xy")
    trimask = np.where(kp <= qp, 0.0, NEGINF).astype(np.float32)
    return {
        "par": np.array([[core % 2]], np.int32),
        "ones1": np.ones((128, 1), np.float32),
        "ones_row": np.ones((1, 128), np.float32),
        "eye_f": np.eye(128, dtype=np.float32),
        "iotaB": np.broadcast_to((iota - BIG)[None, :], (128, 8)).copy(),
        "sel8": sel8,
        "selH": selH,
        "selT": selT,
        "selHr": selHr,
        "trimask": trimask,
    }


def kernel(**inputs):
    plan_env = os.environ.get("MK_PLAN", "fused")
    if plan_env == "unfused":
        plans = [[sl] for sl in SUBLAYERS]
    elif plan_env == "fused":
        plans = [SUBLAYERS]
    elif plan_env.startswith("pfx:"):
        k = int(plan_env.split(":")[1])
        plans = [[sl] for sl in SUBLAYERS[:k]]
    else:
        plans = [[sl] for sl in plan_env.split(",")]

    x = np.asarray(inputs["L0_attn_x"])
    B = x.shape[0]
    ncores = 8
    xs = [np.ascontiguousarray(x[c // 2, (c % 2) * TOK:(c % 2 + 1) * TOK, :].T)
          for c in range(ncores)]
    consts = [_host_constants(c) for c in range(ncores)]
    moe_w = {
        "gate_wT": np.ascontiguousarray(np.asarray(inputs["L0_ffn_gate_w"]).T),
        "w1": np.ascontiguousarray(np.asarray(inputs["L0_ffn_w1"])),
        "w2": np.ascontiguousarray(np.asarray(inputs["L0_ffn_w2"])),
        "w3": np.ascontiguousarray(np.asarray(inputs["L0_ffn_w3"])),
    }
    attn0_w = []
    for c in range(ncores):
        h0 = (c % 2) * 8
        rs = slice(h0 * 64, (h0 + 8) * 64)
        attn0_w.append({
            "l0_wq": np.ascontiguousarray(np.asarray(inputs["L0_attn_Wq"])[rs, :].T),
            "l0_wk": np.ascontiguousarray(np.asarray(inputs["L0_attn_Wk"])[rs, :].T),
            "l0_wv": np.ascontiguousarray(np.asarray(inputs["L0_attn_Wv"])[rs, :].T),
            "l0_wo": np.ascontiguousarray(np.asarray(inputs["L0_attn_Wo"])[:, rs].T),
            "qln_g": np.ascontiguousarray(np.tile(np.asarray(inputs["L0_attn_qln_g"]), 2)[:, None]),
            "qln_b": np.ascontiguousarray(np.tile(np.asarray(inputs["L0_attn_qln_b"]), 2)[:, None]),
            "kln_g": np.ascontiguousarray(np.tile(np.asarray(inputs["L0_attn_kln_g"]), 2)[:, None]),
            "kln_b": np.ascontiguousarray(np.tile(np.asarray(inputs["L0_attn_kln_b"]), 2)[:, None]),
            "l0_bo_cols": np.ascontiguousarray(np.asarray(inputs["L0_attn_bo"]).reshape(8, 128).T),
        })

    def _rope_tables():
        half = 16
        inv = (1.0 / (np.float32(10000.0) ** (np.arange(half, dtype=np.float32)
                                              / np.float32(half)))).astype(np.float32)
        pos = np.arange(T, dtype=np.float32)
        ang = (pos[:, None] * inv[None, :]).astype(np.float32)
        cos = np.cos(ang).astype(np.float32)
        sin = np.sin(ang).astype(np.float32)
        cos128 = np.empty((128, T), np.float32)
        sin128 = np.empty((128, T), np.float32)
        for j in range(128):
            b = j % 32
            if b < half:
                cos128[j] = cos[:, b]
                sin128[j] = -sin[:, b]
            else:
                cos128[j] = cos[:, b - half]
                sin128[j] = sin[:, b - half]
        return cos128, sin128

    ropecos, ropesin = _rope_tables()

    def _swap_qr(w, dh):
        w = w.reshape(w.shape[0], -1, dh)
        return np.concatenate([w[:, :, dh // 2:], w[:, :, :dh // 2]], axis=2).reshape(
            w.shape[0], -1)

    attn1_w = []
    for c in range(ncores):
        h0 = (c % 2) * 8
        wq = np.asarray(inputs["L1_attn_Wq"])[:, h0 * 64:(h0 + 8) * 64]
        wqr_all = np.asarray(inputs["L1_attn_Wqr"])
        wqr = wqr_all[:, h0 * 32:(h0 + 8) * 32]
        wqr_cat = np.concatenate([wqr, _swap_qr(wqr, 32)], axis=1)
        wkr = np.asarray(inputs["L1_attn_Wkr"])
        wkr_cat = np.concatenate([wkr, _swap_qr(wkr, 32)], axis=1)
        attn1_w.append({
            "l1_wq": np.ascontiguousarray(wq),
            "l1_wqr_cat": np.ascontiguousarray(wqr_cat),
            "l1_wkvd": np.ascontiguousarray(np.asarray(inputs["L1_attn_Wkvd"])),
            "l1_wkr_cat": np.ascontiguousarray(wkr_cat),
            "l1_wku": np.ascontiguousarray(np.asarray(inputs["L1_attn_Wku"])[:, h0 * 64:(h0 + 8) * 64]),
            "l1_wvu": np.ascontiguousarray(np.asarray(inputs["L1_attn_Wvu"])[:, h0 * 64:(h0 + 8) * 64]),
            "l1_wo": np.ascontiguousarray(np.asarray(inputs["L1_attn_Wo"])[h0 * 64:(h0 + 8) * 64, :]),
            "lobo": np.ascontiguousarray(np.asarray(inputs["L1_attn_lobo_log"])[h0:h0 + 8][None, :]),
            "ropecos": ropecos,
            "ropesin": ropesin,
        })

    ffn1_w = {
        "l1fw": np.ascontiguousarray(np.asarray(inputs["L1_ffn_weight"])),
        "l1fb_cols": np.ascontiguousarray(np.asarray(inputs["L1_ffn_bias"]).reshape(8, 128).T),
    }
    rng = np.random.RandomState()
    for plan in plans:
        nc, _io = build_program(plan)
        nonces = rng.randint(1, 2 ** 30, size=(8, 1)).astype(np.int32)
        in_maps = []
        for c in range(ncores):
            m = {"x_in": xs[c], "nonces": nonces}
            m.update(consts[c])
            if any(sl in plan for sl in ("moe0", "moe1")):
                m.update(moe_w)
            if "attn0" in plan:
                m.update(attn0_w[c])
            if "ffn1a" in plan or "ffn1b" in plan:
                m.update(ffn1_w)
            if "attn1" in plan:
                m.update(attn1_w[c])
            in_maps.append(m)
        res = run_bass_kernel_spmd(nc, in_maps, core_ids=list(range(ncores)))
        xs = [np.asarray(res.results[c]["x_out"]) for c in range(ncores)]

    out = np.empty((B, T, D), np.float32)
    for c in range(ncores):
        out[c // 2, (c % 2) * TOK:(c % 2 + 1) * TOK, :] = xs[c].T
    return out


# version 06
# speedup vs baseline: 2.3094x; 1.1350x over previous
import os
import numpy as np
import ml_dtypes
from contextlib import ExitStack

import concourse.bass as bass
import concourse.mybir as mybir
from concourse.bass_utils import run_bass_kernel_spmd

F32 = mybir.dt.float32
BF16 = mybir.dt.bfloat16
I32 = mybir.dt.int32
AF = mybir.ActivationFunctionType
ALU = mybir.AluOpType
AXX = mybir.AxisListType.X

D = 1024
TOK = 2048
T = 4096
NCH = D // 128
HID = 2816
NEXP = 8
NH = 8
DH = 64
EPS = 1e-5
BIG = 1.0e6
NEGINF = -1.0e30
SCALE0 = float(np.float32(1024.0 ** (-0.25)))
SCALE1 = float(1.0 / np.sqrt(96.0))

SUBLAYERS = ["moe0", "attn0", "moe1", "ffn1a", "attn1", "ffn1b"]


class Prog:
    def __init__(self, nc, ctx):
        self.nc = nc
        self.ctx = ctx
        self._sems = {}
        self._cnt = {}
        self._guard = {}

    def sem(self, name):
        h = self.ctx.enter_context(self.nc.semaphore(name))
        self._sems[name] = h
        self._cnt[name] = 0
        return h

    def inc(self, instr, name, n):
        self._cnt[name] += n
        instr.then_inc(self._sems[name], n)
        return self._cnt[name]

    def mile(self, eng, name):
        self._cnt[name] += 1
        eng.drain().then_inc(self._sems[name], 1)
        return self._cnt[name]

    def cnt(self, name):
        return self._cnt[name]

    def wait(self, eng, name, val=None):
        v = self._cnt[name] if val is None else val
        if v > 0:
            eng.wait_ge(self._sems[name], v)

    def fence(self, eng):
        for nm in ("sP", "sV", "sA", "dS", "dG"):
            self.wait(eng, nm)

    def guard_set(self, res, semname, val=None):
        v = self._cnt[semname] if val is None else val
        self._guard[res] = [(semname, v)]

    def guard_wait(self, eng, res):
        for semname, v in self._guard.pop(res, []):
            if v > 0:
                eng.wait_ge(self._sems[semname], v)



def emit_preamble(p, io, sb):
    nc = p.nc
    s = nc.sync
    g = nc.gpsimd
    p.inc(s.dma_start(out=sb["x"][:].rearrange("p (c t) -> p c t", c=NCH),
                      in_=io["x_in"].rearrange("(c p) t -> p c t", p=128)), "dS", 16)
    for nm in ("ones1", "ones_row", "eye_f", "iotaB", "sel8", "par", "nonces"):
        if nm in sb:
            p.inc(s.dma_start(out=sb[nm][:], in_=io[nm]), "dS", 16)
    p.inc(g.dma_start(out=sb["eye_b"][:], in_=io["eye_f"]), "dG", 16)


def emit_ln(p, sb, ps, to_dram=None, stats_only=False):

    nc = p.nc
    t, v, a = nc.tensor, nc.vector, nc.scalar
    x = sb["x"]
    p.wait(a, "dS")
    p.wait(t, "dS")
    p.wait(v, "dS")
    SLOC = [(ps["A"], 0), (ps["A"], 32), (ps["A"], 64), (ps["C"], 0)]
    SLOC2 = [(ps["B"], 0), (ps["B"], 32), (ps["B"], 64), (ps["F"], 0)]
    for nm in ("psA", "psB", "psC", "psF"):
        p.guard_wait(t, nm)
    for tg in range(4):
        bA, base = SLOC[tg]
        bB, base2 = SLOC2[tg]
        for c in range(NCH):
            xs = x[:, c * TOK + tg * 512: c * TOK + (tg + 1) * 512]
            slot = c % 2
            sqb = sb["tmpA"] if slot == 0 else sb["tmpB"]
            p.guard_wait(a, ("sq", slot))
            a.activation(sqb[:, 0:512], xs, AF.Square)
            p.mile(a, "sA")
            p.wait(t, "sA")
            t.matmul(bA[base:base + 1, :], sb["ones1"][:], xs,
                     start=(c == 0), stop=(c == NCH - 1))
            mi = t.matmul(bB[base2:base2 + 1, :], sb["ones1"][:],
                          sqb[:, 0:512],
                          start=(c == 0), stop=(c == NCH - 1))
            p.inc(mi, "sP", 1)
            p.guard_set(("sq", slot), "sP")
    p.wait(v, "sP")
    for tg in range(4):
        bA, base = SLOC[tg]
        bB, base2 = SLOC2[tg]
        sl = slice(tg * 512, (tg + 1) * 512)
        v.tensor_scalar_mul(sb["row_m"][0:1, sl], bA[base:base + 1, :], 1.0 / D)
        v.tensor_scalar_mul(sb["row_t"][0:1, sl], bB[base2:base2 + 1, :], 1.0 / D)
    p.mile(v, "sV")
    for nm in ("psA", "psB", "psC", "psF"):
        p.guard_set(nm, "sV")
    v.tensor_tensor(sb["row_r"][0:1, :], sb["row_m"][0:1, :], sb["row_m"][0:1, :], op=ALU.mult)
    v.drain()
    v.tensor_tensor(sb["row_t"][0:1, :], sb["row_t"][0:1, :], sb["row_r"][0:1, :], op=ALU.subtract)
    v.drain()
    v.tensor_scalar_add(sb["row_t"][0:1, :], sb["row_t"][0:1, :], EPS)
    p.mile(v, "sV")
    p.wait(a, "sV")
    a.activation(sb["row_mr"][0:1, :], sb["row_t"][0:1, :], AF.Ln)
    a.drain()
    a.activation(sb["row_r"][0:1, :], sb["row_mr"][0:1, :], AF.Exp, scale=-0.5)
    p.mile(a, "sA")
    p.wait(v, "sA")
    v.tensor_tensor(sb["row_mr"][0:1, :], sb["row_m"][0:1, :], sb["row_r"][0:1, :], op=ALU.mult)
    p.mile(v, "sV")
    p.wait(t, "sV")
    mrp = ps["C"]
    p.guard_wait(t, "psC")
    for tile in range(16):
        t.transpose(mrp[:, tile:tile + 1],
                    sb["row_m"][0:1, tile * 128:(tile + 1) * 128], sb["eye_f"][0:1, 0:1])
        ti = t.transpose(mrp[:, 16 + tile:17 + tile],
                         sb["row_r"][0:1, tile * 128:(tile + 1) * 128], sb["eye_f"][0:1, 0:1])
    p.inc(ti, "sP", 1)
    p.wait(v, "sP")
    v.tensor_copy(sb["mr_cols"][:], mrp[:, 0:32])
    p.mile(v, "sV")
    p.guard_set("psC", "sV")
    if stats_only:
        return
    rbb, mrb = ps["D"], ps["E"]
    s = nc.sync
    for tg in range(4):
        sl = slice(tg * 512, (tg + 1) * 512)
        p.guard_wait(t, "psD")
        p.guard_wait(t, "psE")
        t.matmul(rbb[:], sb["ones_row"][:], sb["row_r"][0:1, sl], start=True, stop=True)
        mi = t.matmul(mrb[:], sb["ones_row"][:], sb["row_mr"][0:1, sl], start=True, stop=True)
        p.inc(mi, "sP", 1)
        p.wait(v, "sP")
        p.guard_wait(v, ("sq", 0))
        if to_dram is not None:
            p.guard_wait(v, "nstage")
        for c in range(NCH):
            xs = x[:, c * TOK + tg * 512: c * TOK + (tg + 1) * 512]
            if to_dram is None:
                ns = sb["normed"][:, c * TOK + tg * 512: c * TOK + (tg + 1) * 512]
            else:
                ns = to_dram[1][:, c * 512:(c + 1) * 512]
            v.tensor_tensor(sb["tmpA"][:, 0:512], xs, rbb[:], op=ALU.mult)
            v.drain()
            v.tensor_tensor(ns, sb["tmpA"][:, 0:512], mrb[:], op=ALU.subtract)
            v.drain()
        p.mile(v, "sV")
        p.guard_set("psD", "sV")
        p.guard_set("psE", "sV")
        if to_dram is not None:
            loc_ap, stage = to_dram
            p.wait(s, "sV")
            dst = loc_ap.rearrange("(c p) (g t2) -> p c g t2", p=128, t2=512)
            di = s.dma_start(out=dst[:, :, tg, :],
                             in_=stage[:].rearrange("p (c t2) -> p c t2", c=NCH))
            p.inc(di, "dS", 16)
            p.guard_set("nstage", "dS")


def emit_moe_sublayer(p, sb, ps, io, tag):
    nc = p.nc
    with ExitStack() as st:
        def A(name, shape, dtype):
            t_ = st.enter_context(nc.sbuf_tensor(f"moe{tag}_{name}", shape, dtype))
            sb[name] = t_
            return t_
        A("normed", [128, NCH * TOK], BF16)
        A("gwT", [128, NCH * 8], F32)
        A("gsum_row", [1, 8], F32)
        A("gates", [8, TOK], F32)
        A("g_b", [128, 1024], F32)
        A("h", [128, 12 * 1024], BF16)
        sb["w1g"] = [st.enter_context(nc.sbuf_tensor(f"moe{tag}_w1g{i}", [128, NCH * 128], BF16))
                     for i in range(2)]
        sb["w3g"] = [st.enter_context(nc.sbuf_tensor(f"moe{tag}_w3g{i}", [128, NCH * 128], BF16))
                     for i in range(2)]
        sb["w2s"] = [st.enter_context(nc.sbuf_tensor(f"moe{tag}_w2s0", [128, 12 * 1024], BF16))]
        emit_ln(p, sb, ps)
        emit_moe(p, sb, ps, io)


def emit_moe(p, sb, ps, io):
    nc = p.nc
    t, v, a, g = nc.tensor, nc.vector, nc.scalar, nc.gpsimd
    s = nc.sync
    x = sb["x"]
    p.guard_wait(s, "gwT")
    p.inc(s.dma_start(out=sb["gwT"][:].rearrange("p (c e) -> p c e", c=NCH),
                      in_=io["gate_wT"].rearrange("(c p) e -> p c e", p=128)),
          "dS", 16)
    p.wait(t, "dS")
    bk = ps["A"]
    p.guard_wait(t, "psA")
    for c in range(NCH):
        mi = t.matmul(bk[0:1, 0:8], sb["ones1"][:], sb["gwT"][:, c * 8:(c + 1) * 8],
                      start=(c == 0), stop=(c == NCH - 1))
    p.inc(mi, "sP", 1)
    p.wait(v, "sP")
    v.tensor_copy(sb["gsum_row"][0:1, 0:8], bk[0:1, 0:8])
    v.tensor_scalar_mul(sb["row_t"][0:1, :], sb["row_m"][0:1, :], -1.0)
    p.mile(v, "sV")
    p.guard_set("psA", "sV")
    p.wait(t, "sV")
    scp = ps["C"]
    gtp = ps["F"]
    w = sb["tmpA"]
    for tile in range(16):
        p.guard_wait(t, "psC")
        if tile % 4 == 0:
            p.guard_wait(t, "psF")
        for c in range(NCH):
            xs = x[:, c * TOK + tile * 128:c * TOK + (tile + 1) * 128]
            t.matmul(scp[0:128, 0:8], xs, sb["gwT"][:, c * 8:(c + 1) * 8],
                     start=(c == 0), stop=False)
        mi = t.matmul(scp[0:128, 0:8], sb["row_t"][0:1, tile * 128:(tile + 1) * 128],
                      sb["gsum_row"][0:1, 0:8], start=False, stop=True)
        p.inc(mi, "sP", 1)
        p.wait(v, "sP")
        r_col = sb["mr_cols"][:, 16 + tile:17 + tile]
        v.tensor_scalar_mul(w[:, 0:8], scp[0:128, 0:8], r_col)
        p.mile(v, "sV")
        p.guard_set("psC", "sV")
        v.tensor_reduce(w[:, 16:17], w[:, 0:8], axis=AXX, op=ALU.max)
        v.drain()
        v.tensor_scalar_mul(w[:, 17:18], w[:, 16:17], -1.0)
        p.mile(v, "sV")
        p.wait(a, "sV")
        a.activation(w[:, 24:32], w[:, 0:8], AF.Exp, bias=w[:, 17:18])
        p.mile(a, "sA")
        p.wait(v, "sA")
        v.tensor_reduce(w[:, 18:19], w[:, 24:32], axis=AXX, op=ALU.add)
        v.drain()
        v.reciprocal(w[:, 19:20], w[:, 18:19])
        v.drain()
        v.tensor_scalar_mul(w[:, 32:40], w[:, 24:32], w[:, 19:20])
        v.drain()
        v.tensor_reduce(w[:, 16:17], w[:, 32:40], axis=AXX, op=ALU.max)
        v.drain()
        v.tensor_single_scalar(w[:, 8:16], w[:, 32:40], w[:, 16:17], op=ALU.is_equal)
        v.drain()
        v.tensor_tensor(w[:, 8:16], w[:, 8:16], sb["iotaB"][:], op=ALU.mult)
        v.drain()
        v.tensor_reduce(w[:, 16:17], w[:, 8:16], axis=AXX, op=ALU.min)
        v.drain()
        v.tensor_single_scalar(w[:, 40:48], sb["iotaB"][:], w[:, 16:17], op=ALU.is_equal)
        v.drain()
        v.scalar_tensor_tensor(w[:, 48:56], w[:, 40:48], -BIG, w[:, 32:40],
                               op0=ALU.mult, op1=ALU.add)
        v.drain()
        v.tensor_reduce(w[:, 16:17], w[:, 48:56], axis=AXX, op=ALU.max)
        v.drain()
        v.tensor_single_scalar(w[:, 8:16], w[:, 48:56], w[:, 16:17], op=ALU.is_equal)
        v.drain()
        v.tensor_tensor(w[:, 8:16], w[:, 8:16], sb["iotaB"][:], op=ALU.mult)
        v.drain()
        v.tensor_reduce(w[:, 16:17], w[:, 8:16], axis=AXX, op=ALU.min)
        v.drain()
        v.tensor_single_scalar(w[:, 8:16], sb["iotaB"][:], w[:, 16:17], op=ALU.is_equal)
        v.drain()
        v.tensor_tensor(w[:, 48:56], w[:, 40:48], w[:, 8:16], op=ALU.add)
        v.drain()
        v.tensor_tensor(w[:, 56:64], w[:, 48:56], w[:, 32:40], op=ALU.mult)
        if "dbg_scr" in sb:
            v.tensor_copy(sb["dbg_scr"][:, tile * 16:tile * 16 + 8], w[:, 0:8])
            v.tensor_copy(sb["dbg_scr"][:, tile * 16 + 8:tile * 16 + 16], w[:, 32:40])
            if tile == 0:
                v.tensor_copy(sb["dbg_scr"][:, 256:264], w[:, 24:32])
                v.tensor_copy(sb["dbg_scr"][:, 264:265], w[:, 16:17])
                v.tensor_copy(sb["dbg_scr"][:, 265:266], w[:, 17:18])
                v.tensor_copy(sb["dbg_scr"][:, 266:267], w[:, 18:19])
                v.tensor_copy(sb["dbg_scr"][:, 267:268], w[:, 19:20])

        p.mile(v, "sV")
        p.wait(t, "sV")
        ti = t.transpose(gtp[0:8, (tile % 4) * 128:(tile % 4 + 1) * 128],
                         w[:, 56:64], sb["eye_f"][:])
        p.inc(ti, "sP", 1)
        if tile % 4 == 3:
            p.wait(v, "sP")
            v.tensor_copy(sb["gates"][0:8, (tile // 4) * 512:(tile // 4 + 1) * 512],
                          gtp[0:8, 0:512])
            p.mile(v, "sV")
            p.guard_set("psF", "sV")
    p.guard_set("psC", "sV")

    zb = [ps["A"], ps["B"], ps["C"], ps["F"]]
    zbname = ["psA", "psB", "psC", "psF"]
    ob = [ps["D"], ps["E"]]
    obname = ["psD", "psE"]
    gates_ready = p.cnt("sV")
    HGRP = [list(range(0, 12)), list(range(12, 22))]
    HC0 = [0, 12]
    HCN = [12, 10]
    for hh in range(2):
        for e in range(NEXP):
            for m in range(2):
                tg = hh * 2 + m
                bank = ob[m]
                p.guard_wait(t, obname[m])
                if hh == 0 and e == 0 and m == 0:
                    p.wait(t, "sV", gates_ready)
                mi = t.matmul(bank[:], sb["sel8"][:, e * 128:(e + 1) * 128],
                              sb["gates"][0:8, tg * 512:(tg + 1) * 512],
                              start=True, stop=True)
                p.inc(mi, "sP", 1)
                p.wait(a, "sP")
                a.copy(sb["g_b"][:, m * 512:(m + 1) * 512], bank[:])
                p.mile(a, "sA")
                p.guard_set(obname[m], "sA")
            first_h_write = True
            for half in range(2):
                p.guard_wait(g, "w2s")
                w2b = sb["w2s"][0]
                w2_ready = p.inc(g.dma_start(
                    out=w2b[:, 0:HCN[half] * 1024].rearrange("p (gg f) -> p gg f", gg=HCN[half]),
                    in_=io["w2"][e].rearrange("(gg p) f -> p gg f", p=128)
                        [:, HC0[half]:HC0[half] + HCN[half], :]), "dG", 16)
                gensets = [("psA", "psB"), ("psC", "psF"), ("psD", "psE")]
                genb = [(ps["A"], ps["B"]), (ps["C"], ps["F"]), (ps["D"], ps["E"])]
                ucount = 0
                for wgi, hc in enumerate(HGRP[half]):
                    buf = wgi % 2
                    w1b = sb["w1g"][buf]
                    w3b = sb["w3g"][buf]
                    p.guard_wait(g, ("w1g", buf))
                    p.inc(g.dma_start(
                        out=w1b[:].rearrange("p (c h) -> p c h", c=NCH),
                        in_=io["w1"][e].rearrange("(c p) h -> p c h", p=128)
                            [:, :, hc * 128:(hc + 1) * 128]), "dG", 16)
                    dg_ready = p.inc(g.dma_start(
                        out=w3b[:].rearrange("p (c h) -> p c h", c=NCH),
                        in_=io["w3"][e].rearrange("(c p) h -> p c h", p=128)
                            [:, :, hc * 128:(hc + 1) * 128]), "dG", 16)
                    p.wait(t, "dG", dg_ready)
                    hcL = hc - HC0[half]
                    for m in range(2):
                        gsel = ucount % 3
                        ucount += 1
                        z1, z3 = genb[gsel]
                        z1n, z3n = gensets[gsel]
                        tg = hh * 2 + m
                        tsl = sb["tmpA"][:, m * 512:(m + 1) * 512]
                        p.guard_wait(t, z1n)
                        p.guard_wait(t, z3n)
                        for c in range(NCH):
                            nss = sb["normed"][:, c * TOK + tg * 512:c * TOK + (tg + 1) * 512]
                            t.matmul(z1[:], w1b[:, c * 128:(c + 1) * 128],
                                     nss, start=(c == 0), stop=(c == NCH - 1))
                        for c in range(NCH):
                            nss = sb["normed"][:, c * TOK + tg * 512:c * TOK + (tg + 1) * 512]
                            mi = t.matmul(z3[:], w3b[:, c * 128:(c + 1) * 128],
                                          nss, start=(c == 0), stop=(c == NCH - 1))
                        p.inc(mi, "sP", 1)
                        p.wait(v, "sP")
                        v.tensor_tensor(tsl, z1[:],
                                        sb["g_b"][:, m * 512:(m + 1) * 512], op=ALU.mult)
                        p.mile(v, "sV")
                        p.guard_set(z1n, "sV")
                        p.wait(a, "sV")
                        if first_h_write:
                            p.guard_wait(a, "h")
                            first_h_write = False
                        a.activation(sb["h"][:, hcL * 1024 + m * 512:hcL * 1024 + (m + 1) * 512],
                                     tsl, AF.Silu)
                        p.mile(a, "sA")
                        p.wait(v, "sA")
                        v.tensor_tensor(
                            sb["h"][:, hcL * 1024 + m * 512:hcL * 1024 + (m + 1) * 512],
                            sb["h"][:, hcL * 1024 + m * 512:hcL * 1024 + (m + 1) * 512],
                            z3[:], op=ALU.mult)
                        p.mile(v, "sV")
                        p.guard_set(z3n, "sV")
                    p.guard_set(("w1g", buf), "sP")
                h_ready = p.cnt("sV")
                p.wait(t, "dG", w2_ready)
                p.wait(t, "sV", h_ready)
                for oc in range(NCH):
                    for m in range(2):
                        ob2 = [ps["A"], ps["B"]]
                        ob2n = ["psA", "psB"]
                        u = oc * 2 + m
                        bank = ob2[u % 2]
                        bname = ob2n[u % 2]
                        tg = hh * 2 + m
                        p.guard_wait(t, bname)
                        for gg in range(HCN[half]):
                            mi = t.matmul(bank[:],
                                          w2b[:, gg * 1024 + oc * 128:gg * 1024 + (oc + 1) * 128],
                                          sb["h"][:, gg * 1024 + m * 512:gg * 1024 + (m + 1) * 512],
                                          start=(gg == 0), stop=(gg == HCN[half] - 1))
                        p.inc(mi, "sP", 1)
                        p.wait(v, "sP")
                        xs = x[:, oc * TOK + tg * 512:oc * TOK + (tg + 1) * 512]
                        v.scalar_tensor_tensor(sb["tmpA"][:, m * 512:(m + 1) * 512], bank[:],
                                               0.5, sb["g_b"][:, m * 512:(m + 1) * 512],
                                               op0=ALU.mult, op1=ALU.mult)
                        p.mile(v, "sV")
                        p.guard_set(bname, "sV")
                        v.tensor_tensor(xs, xs, sb["tmpA"][:, m * 512:(m + 1) * 512],
                                        op=ALU.add)
                    p.inc(v.drain(), "sV", 1)
                p.guard_set("w2s", "sP")
                p.guard_set("h", "sP")
                first_h_write = True
            p.guard_set("g_b", "sV")
    p.guard_set("gwT", "sP")






def emit_flash(p, sb, ps, io, bufs, h, qg, ui, mode, negU, crow=None):
    nc = p.nc
    t, v, a, g, s = nc.tensor, nc.vector, nc.scalar, nc.gpsimd, nc.sync
    NT = T // 512
    ktn = qg * 4 + 4
    bufi = ui % 2
    kst, qst, Pt = bufs["kst"], bufs["qst"], bufs["Pt"]
    arow = bufs["arow"]
    sS = [ps["A"], ps["B"], ps["C"], ps["F"]]
    sSn = ["psA", "psB", "psC", "psF"]
    pO, pOn = ps["D"], "psD"
    pB, pBn = ps["E"], "psE"
    p.guard_wait(g, ("kst", bufi))
    p.inc(g.dma_start(out=kst[bufi][:, 0:ktn * 128],
                      in_=io["k_dr"][h * 64:(h + 1) * 64, 0:ktn * 128]), "dG", 16)
    p.guard_wait(g, ("qst", bufi))
    dg_ready = p.inc(g.dma_start(out=qst[bufi][:],
                                 in_=io["q_dr"][h * 64:(h + 1) * 64,
                                                qg * 512:(qg + 1) * 512]), "dG", 16)
    if mode == 1:
        qrst = bufs["qrst"]
        p.guard_wait(g, ("qrst", bufi))
        dg_ready = p.inc(g.dma_start(out=qrst[bufi][:],
                                     in_=io["qr_dr"][h * 32:(h + 1) * 32,
                                                     qg * 512:(qg + 1) * 512]), "dG", 16)
    p.guard_wait(t, pOn)
    p.wait(t, "dG", dg_ready)
    pairs = [[kt for kt in (k0, k0 + 1) if kt < ktn] for k0 in range(0, ktn, 2)]

    def s_pair(kts):
        for kt in kts:
            dead = max(0, kt * 128 - qg * 512)
            sbank = sS[kt % 4]
            p.guard_wait(t, sSn[kt % 4])
            if mode == 0:
                mi = t.matmul(sbank[:, 0:512 - dead],
                              kst[bufi][:, kt * 128:(kt + 1) * 128],
                              qst[bufi][:, dead:512], start=True, stop=True)
            else:
                t.matmul(sbank[:, 0:512 - dead],
                         kst[bufi][:, kt * 128:(kt + 1) * 128],
                         qst[bufi][:, dead:512], start=True, stop=False)
                mi = t.matmul(sbank[:, 0:512 - dead],
                              sb["krs_cur"][:, kt * 128:(kt + 1) * 128],
                              bufs["qrst"][bufi][:, dead:512], start=False, stop=True)
            p.inc(mi, "sP", 1)
        return p.cnt("sP")

    def fin_pair(kts, sp_val):
        deads = {kt: max(0, kt * 128 - qg * 512) for kt in kts}
        need_v = any(kt >= qg * 4 for kt in kts) or any(deads[kt] > 0 for kt in kts)
        if need_v:
            p.wait(v, "sP", sp_val)
            for kt in kts:
                if kt >= qg * 4:
                    v.tensor_tensor(sS[kt % 4][:, 0:128], sS[kt % 4][:, 0:128],
                                    sb["trimask"][:], op=ALU.add)
            for kt in kts:
                p.guard_wait(v, ("Pt", kt % 2))
                if deads[kt] > 0:
                    v.memset(Pt[kt % 2][:, 0:deads[kt]], 0.0)
            p.mile(v, "sV")
            p.wait(a, "sV")
        else:
            for kt in kts:
                p.guard_wait(a, ("Pt", kt % 2))
        p.wait(a, "sP", sp_val)
        for kt in kts:
            a.activation(Pt[kt % 2][:, deads[kt]:512],
                         sS[kt % 4][:, 0:512 - deads[kt]], AF.Exp,
                         scale=(1.0 if mode == 0 else SCALE1), bias=negU[:, h:h + 1])
        p.mile(a, "sA")
        for kt in kts:
            p.guard_set(sSn[kt % 4], "sA")
        p.wait(t, "sA")
        for kt in kts:
            mi = t.matmul(pO[0:65, :],
                          sb["V_cur"][:].rearrange("p (tt c) -> p tt c", tt=32)
                          [:, kt, h * 65:(h + 1) * 65],
                          Pt[kt % 2][:], start=(kt == 0), stop=(kt == ktn - 1))
            p.inc(mi, "sP", 1)
            p.guard_set(("Pt", kt % 2), "sP")

    prev = None
    prev_sp = 0
    for kts in pairs:
        sp = s_pair(kts)
        if prev is not None:
            fin_pair(prev, prev_sp)
        prev, prev_sp = kts, sp
    fin_pair(prev, prev_sp)
    p.guard_set(("kst", bufi), "sP")
    p.guard_set(("qst", bufi), "sP")
    if mode == 1:
        p.guard_set(("qrst", bufi), "sP")
    p.wait(v, "sP")
    p.guard_wait(v, ("lrow",))
    if mode == 0:
        v.reciprocal(sb["tmpB"][0:1, 0:512], pO[64:65, :])
    else:
        v.tensor_copy(sb["tmpB"][0:1, 0:512], pO[64:65, :])
        v.drain()
        v.tensor_single_scalar(sb["tmpB"][0:1, 0:512], sb["tmpB"][0:1, 0:512],
                               crow[0:1, h:h + 1], op=ALU.add)
        v.drain()
        v.reciprocal(sb["tmpB"][0:1, 0:512], sb["tmpB"][0:1, 0:512])
    p.mile(v, "sV")
    p.wait(t, "sV")
    p.guard_wait(t, pBn)
    mi = t.matmul(pB[:], sb["ones_row"][:], sb["tmpB"][0:1, 0:512],
                  start=True, stop=True)
    p.inc(mi, "sP", 1)
    p.guard_set(("lrow",), "sP")
    p.wait(a, "sP")
    p.guard_wait(a, ("rlb", ui % 2))
    a.copy(sb["tmpA"][0:64, (ui % 2) * 512:(ui % 2) * 512 + 512], pB[0:64, :])
    p.mile(a, "sA")
    p.guard_set(pBn, "sA")
    p.wait(v, "sA")
    p.guard_wait(v, ("arow", ui % 2))
    v.tensor_tensor(arow[(ui % 2)][:], pO[0:64, :],
                    sb["tmpA"][0:64, (ui % 2) * 512:(ui % 2) * 512 + 512], op=ALU.mult)
    p.mile(v, "sV")
    p.guard_set(pOn, "sV")
    p.guard_set(("rlb", ui % 2), "sV")
    p.wait(s, "sV")
    di = s.dma_start(out=io["attn_dr"][h * 64:(h + 1) * 64, qg * 512:(qg + 1) * 512],
                     in_=arow[(ui % 2)][:])
    p.inc(di, "dS", 16)
    p.guard_set(("arow", ui % 2), "dS")


def emit_exchange_flags(p, sb, io, exch, SLpar):
    nc = p.nc
    s = nc.sync
    ALUo = ALU
    p.wait(s, "dS")
    fl = io["sh_flags"].rearrange("(e sl) one -> e sl one", sl=2)
    di = s.dma_start(out=fl[exch, SLpar, :],
                     in_=sb["nonces"][exch:exch + 1, :])
    p.inc(di, "dS", 16)
    p.wait(s, "dS")
    nr = s.alloc_register(f"nr{exch}")
    s.load(nr, sb["nonces"][exch:exch + 1, 0:1])
    for slot in range(2):
        c = s.alloc_register(f"c{exch}_{slot}")
        f = s.alloc_register(f"f{exch}_{slot}")
        s.reg_mov(c, 1)
        with s.While(c):
            s.load(f, io["sh_flags"][exch * 2 + slot:exch * 2 + slot + 1, 0:1])
            s.reg_alu(c, f, nr, ALUo.not_equal)
    s.nop().then_inc(p._sems["sG"], 1)
    p._cnt["sG"] += 1


def emit_attn0(p, sb, ps, io, ctx):
    nc = p.nc
    t, v, a, g, s = nc.tensor, nc.vector, nc.scalar, nc.gpsimd, nc.sync
    x = sb["x"]
    NT = T // 512

    with ExitStack() as st:
        V_sb = st.enter_context(nc.sbuf_tensor("at_V", [128, 32 * 520], BF16))
        lncols = st.enter_context(nc.sbuf_tensor("at_lncols", [128, 4], F32))
        negU = st.enter_context(nc.sbuf_tensor("at_negU", [128, 8], F32))
        nrm8 = st.enter_context(nc.sbuf_tensor("at_nrm8", [8, 8], F32))
        bocols = st.enter_context(nc.sbuf_tensor("at_bo", [128, 8], F32))
        sb["selH"] = st.enter_context(nc.sbuf_tensor("at_selH", [128, 32], F32))
        sb["selT"] = st.enter_context(nc.sbuf_tensor("at_selT", [8, 4 * 128], F32))
        sb["trimask"] = st.enter_context(nc.sbuf_tensor("at_trimask", [128, 128], F32))
        p.fence(s)
        for nm in ("selH", "selT", "trimask"):
            p.inc(s.dma_start(out=sb[nm][:], in_=io[nm]), "dS", 16)
        lnv = {}
        for nm in ("qln_g", "qln_b", "kln_g", "kln_b"):
            lnv[nm] = st.enter_context(nc.sbuf_tensor("at_" + nm, [128, 1], F32))
            p.inc(s.dma_start(out=lnv[nm][:], in_=io[nm]), "dS", 16)
        p.inc(s.dma_start(out=bocols[:], in_=io["l0_bo_cols"]), "dS", 16)
        p.wait(s, "dS")
        preg_s = nc.values_load(sb["par"][0:1, 0:1], engines=[s.engine],
                                skip_runtime_bounds_check=True)
        SL = bass.ts(preg_s, 1)

        with ExitStack() as st2:
            stage = st2.enter_context(nc.sbuf_tensor("at_nstage", [128, NCH * 512], BF16))
            emit_ln(p, sb, ps, to_dram=(io["normed_loc"], stage))
            p.wait(s, "dS")
            shn = io["sh_normed0"].rearrange("r (hf t2) -> r hf t2", hf=2)
            di = s.dma_start(out=shn[:, SL, :],
                             in_=io["normed_loc"].rearrange("r (one t) -> r one t", one=1))
            p.inc(di, "dS", 16)
            emit_exchange_flags(p, sb, io, 0, SL)

        p.wait(v, "dS")
        v.tensor_scalar_mul(lncols[:, 0:1], lnv["qln_g"][:], SCALE0)
        v.tensor_scalar_mul(lncols[:, 1:2], lnv["qln_b"][:], SCALE0)
        v.tensor_scalar_mul(lncols[:, 2:3], lnv["kln_g"][:], SCALE0)
        v.tensor_scalar_mul(lncols[:, 3:4], lnv["kln_b"][:], SCALE0)
        v.memset(V_sb[:], 1.0)
        p.mile(v, "sV")

        with ExitStack() as st2:
            wq = st2.enter_context(nc.sbuf_tensor("at_wq", [128, NCH * 512], BF16))
            wk = st2.enter_context(nc.sbuf_tensor("at_wk", [128, NCH * 512], BF16))
            wv = st2.enter_context(nc.sbuf_tensor("at_wv", [128, NCH * 512], BF16))
            nstream = st2.enter_context(nc.sbuf_tensor("at_nst", [128, NCH * 512], BF16))
            stg = st2.enter_context(nc.sbuf_tensor("at_stg", [128, 4 * 512], BF16))
            qtmp = st2.enter_context(nc.sbuf_tensor("at_qtmp", [128, 4 * 512], F32))
            hrows = st2.enter_context(nc.sbuf_tensor("at_hrows", [8, 3 * 512], F32))
            p.wait(g, "dS")
            for nm, ionm in ((wq, "l0_wq"), (wk, "l0_wk"), (wv, "l0_wv")):
                p.inc(g.dma_start(out=nm[:].rearrange("p (c o) -> p c o", c=NCH),
                                  in_=io[ionm].rearrange("(c p) o -> p c o", p=128)),
                      "dG", 16)
            p.wait(g, "sG")
            nsh = io["sh_normed0"].rearrange("(c p) t -> p c t", p=128)
            pq = [ps["A"], ps["B"], ps["C"], ps["F"]]
            pqn = ["psA", "psB", "psC", "psF"]
            stats1, stats2 = ps["D"], ps["E"]
            for tg in range(NT):
                p.guard_wait(g, "nstream")
                di = g.dma_start(out=nstream[:].rearrange("p (c t2) -> p c t2", c=NCH),
                                 in_=nsh[:, :, tg * 512:(tg + 1) * 512])
                nst_ready = p.inc(di, "dG", 16)
                p.wait(t, "dG", nst_ready)
                for proj in range(2):
                    wgt = wq if proj == 0 else wk
                    gcol = lncols[:, 2 * proj:2 * proj + 1]
                    bcol = lncols[:, 2 * proj + 1:2 * proj + 2]
                    dr = io["q_dr"] if proj == 0 else io["k_dr"]
                    for oc in range(4):
                        p.guard_wait(t, pqn[oc])
                        for kc in range(NCH):
                            mi = t.matmul(pq[oc][:],
                                          wgt[:, kc * 512 + oc * 128:kc * 512 + (oc + 1) * 128],
                                          nstream[:, kc * 512:(kc + 1) * 512],
                                          start=(kc == 0), stop=(kc == NCH - 1))
                        p.inc(mi, "sP", 1)
                    p.wait(a, "sP")
                    for oc in range(4):
                        a.copy(qtmp[:, oc * 512:(oc + 1) * 512], pq[oc][:])
                    p.mile(a, "sA")
                    for oc in range(4):
                        p.guard_set(pqn[oc], "sA")
                    p.guard_wait(t, "psD")
                    p.guard_wait(t, "psE")
                    p.wait(t, "sA")
                    for oc in range(4):
                        sqb = sb["tmpB"] if oc % 2 == 0 else sb["tmpA"]
                        p.guard_wait(a, ("sqB", oc % 2))
                        a.activation(sqb[:, 0:512], qtmp[:, oc * 512:(oc + 1) * 512], AF.Square)
                        p.mile(a, "sA")
                        p.wait(t, "sA")
                        t.matmul(stats1[0:8, :], sb["selH"][:, oc * 8:(oc + 1) * 8],
                                 qtmp[:, oc * 512:(oc + 1) * 512],
                                 start=(oc == 0), stop=(oc == 3))
                        mi = t.matmul(stats2[0:8, :], sb["selH"][:, oc * 8:(oc + 1) * 8],
                                      sqb[:, 0:512], start=(oc == 0), stop=(oc == 3))
                        p.inc(mi, "sP", 1)
                        p.guard_set(("sqB", oc % 2), "sP")
                    p.wait(v, "sP")
                    v.tensor_scalar_mul(hrows[0:8, 0:512], stats1[0:8, :], 1.0 / DH)
                    v.tensor_scalar_mul(hrows[0:8, 512:1024], stats2[0:8, :], 1.0 / DH)
                    v.drain()
                    v.tensor_tensor(hrows[0:8, 1024:1536], hrows[0:8, 0:512],
                                    hrows[0:8, 0:512], op=ALU.mult)
                    v.drain()
                    v.tensor_tensor(hrows[0:8, 512:1024], hrows[0:8, 512:1024],
                                    hrows[0:8, 1024:1536], op=ALU.subtract)
                    v.drain()
                    v.tensor_scalar_add(hrows[0:8, 512:1024], hrows[0:8, 512:1024], EPS)
                    p.mile(v, "sV")
                    p.guard_set("psD", "sV")
                    p.guard_set("psE", "sV")
                    p.wait(a, "sV")
                    a.activation(hrows[0:8, 1024:1536], hrows[0:8, 512:1024], AF.Ln)
                    a.drain()
                    a.activation(hrows[0:8, 512:1024], hrows[0:8, 1024:1536], AF.Exp,
                                 scale=-0.5)
                    p.mile(a, "sA")
                    p.wait(t, "sA")
                    for oc in range(4):
                        p.guard_wait(t, "psD")
                        p.guard_wait(t, "psE")
                        t.matmul(stats1[:], sb["selT"][0:8, oc * 128:(oc + 1) * 128],
                                 hrows[0:8, 0:512], start=True, stop=True)
                        mi = t.matmul(stats2[:], sb["selT"][0:8, oc * 128:(oc + 1) * 128],
                                      hrows[0:8, 512:1024], start=True, stop=True)
                        p.inc(mi, "sP", 1)
                        p.wait(v, "sP")
                        v.tensor_tensor(sb["tmpA"][:, 0:512], qtmp[:, oc * 512:(oc + 1) * 512],
                                        stats1[:], op=ALU.subtract)
                        v.drain()
                        v.tensor_tensor(sb["tmpB"][:, 0:512], sb["tmpA"][:, 0:512],
                                        stats2[:], op=ALU.mult)
                        v.drain()
                        v.tensor_scalar(stg[:, oc * 512:(oc + 1) * 512], sb["tmpB"][:, 0:512],
                                        gcol, bcol, op0=ALU.mult, op1=ALU.add)
                        p.mile(v, "sV")
                        p.guard_set("psD", "sV")
                        p.guard_set("psE", "sV")
                    p.guard_wait(t, "psD")
                    p.wait(t, "sV")
                    p.wait(a, "sV")
                    for oc in range(4):
                        p.guard_wait(a, ("sqB", 0))
                        a.activation(sb["tmpB"][:, 0:512], stg[:, oc * 512:(oc + 1) * 512],
                                     AF.Square)
                        p.mile(a, "sA")
                        p.wait(t, "sA")
                        mi = t.matmul(stats1[0:8, :], sb["selH"][:, oc * 8:(oc + 1) * 8],
                                      sb["tmpB"][:, 0:512], start=(oc == 0), stop=(oc == 3))
                        p.inc(mi, "sP", 1)
                        p.guard_set(("sqB", 0), "sP")
                    p.wait(v, "sP")
                    v.tensor_reduce(sb["tmpA"][0:8, 0:1], stats1[0:8, :], axis=AXX, op=ALU.max)
                    v.drain()
                    if tg == 0:
                        v.tensor_copy(nrm8[0:8, proj:proj + 1], sb["tmpA"][0:8, 0:1])
                    else:
                        v.tensor_tensor(nrm8[0:8, proj:proj + 1], nrm8[0:8, proj:proj + 1],
                                        sb["tmpA"][0:8, 0:1], op=ALU.max)
                    p.mile(v, "sV")
                    p.guard_set("psD", "sV")
                    p.wait(s, "sV")
                    p.guard_wait(s, "stg")
                    di = s.dma_start(out=dr.rearrange("(oc p2) t -> p2 oc t", p2=128)
                                     [:, :, tg * 512:(tg + 1) * 512],
                                     in_=stg[:].rearrange("p (oc t2) -> p oc t2", oc=4))
                    p.inc(di, "dS", 16)
                    p.guard_set("stg", "dS")
                for ttl in range(4):
                    tt = tg * 4 + ttl
                    bank = pq[ttl % 2]
                    p.guard_wait(t, pqn[ttl % 2])
                    for kc in range(NCH):
                        mi = t.matmul(bank[:],
                                      nstream[:, kc * 512 + ttl * 128:kc * 512 + (ttl + 1) * 128],
                                      wv[:, kc * 512:(kc + 1) * 512],
                                      start=(kc == 0), stop=(kc == NCH - 1))
                    p.inc(mi, "sP", 1)
                    p.wait(v, "sP")
                    vv = V_sb[:].rearrange("p (tt hb d) -> p tt hb d", tt=32, hb=8)
                    v.tensor_copy(vv[:, tt, :, 0:64],
                                  bank[:].rearrange("p (hb d) -> p hb d", hb=8))
                    p.mile(v, "sV")
                    p.guard_set(pqn[ttl % 2], "sV")
                p.guard_set("nstream", "sP")

        v.tensor_tensor(nrm8[0:8, 2:3], nrm8[0:8, 0:1], nrm8[0:8, 1:2], op=ALU.mult)
        v.drain()
        p.mile(v, "sV")
        p.wait(a, "sV")
        a.activation(nrm8[0:8, 3:4], nrm8[0:8, 2:3], AF.Ln)
        a.drain()
        a.activation(nrm8[0:8, 4:5], nrm8[0:8, 3:4], AF.Exp, scale=0.5)
        p.mile(a, "sA")
        p.wait(t, "sA")
        p.guard_wait(t, "psD")
        ti = t.transpose(ps["D"][0:1, 0:8], nrm8[0:8, 4:5], sb["eye_f"][0:8, 0:8])
        p.inc(ti, "sP", 1)
        p.wait(v, "sP")
        v.tensor_copy(sb["tmpA"][0:1, 0:8], ps["D"][0:1, 0:8])
        p.mile(v, "sV")
        p.wait(t, "sV")
        ti = t.matmul(ps["D"][:, 0:8], sb["ones_row"][:], sb["tmpA"][0:1, 0:8],
                      start=True, stop=True)
        p.inc(ti, "sP", 1)
        p.wait(a, "sP")
        a.activation(negU[:], ps["D"][:, 0:8], AF.Copy, scale=-1.0)
        p.mile(a, "sA")
        p.guard_set("psD", "sA")

        with ExitStack() as st2:
            bufs = {
                "kst": [st2.enter_context(nc.sbuf_tensor(f"at_kst{i}", [64, T], BF16))
                        for i in range(2)],
                "qst": [st2.enter_context(nc.sbuf_tensor(f"at_qst{i}", [64, 512], BF16))
                        for i in range(2)],
                "Pt": [st2.enter_context(nc.sbuf_tensor(f"at_pt{i}", [128, 512], BF16))
                       for i in range(2)],
                "arow": [st2.enter_context(nc.sbuf_tensor(f"at_arow{i}", [64, 512], BF16))
                         for i in range(2)],
            }
            sb["V_cur"] = V_sb
            units = [(h, qg) for h in range(NH) for qg in range(NT)]
            p.fence(g)
            for ui, (h, qg) in enumerate(units):
                emit_flash(p, sb, ps, io, bufs, h, qg, ui, 0, negU)

        with ExitStack() as st2:
            wo = st2.enter_context(nc.sbuf_tensor("at_wo", [128, 4 * 1024], BF16))
            astream = st2.enter_context(nc.sbuf_tensor("at_ast", [128, 4 * 512], BF16))
            stg2 = st2.enter_context(nc.sbuf_tensor("at_stg2", [128, 2 * 512], BF16))
            p.fence(g)
            p.inc(g.dma_start(out=wo[:].rearrange("p (c o) -> p c o", c=4),
                              in_=io["l0_wo"].rearrange("(c p) o -> p c o", p=128)), "dG", 16)
            ash = io["attn_dr"].rearrange("(c p) t -> p c t", p=128)
            p.wait(a, "dS")
            pr = [ps["A"], ps["B"]]
            prn = ["psA", "psB"]
            psh = io["partial_loc"].rearrange("(c p) t -> p c t", p=128)
            for tg in range(NT):
                p.guard_wait(g, "astream")
                di = g.dma_start(out=astream[:].rearrange("p (c t2) -> p c t2", c=4),
                                 in_=ash[:, :, tg * 512:(tg + 1) * 512])
                nst_ready = p.inc(di, "dG", 16)
                p.wait(t, "dG", nst_ready)
                for oc in range(NCH):
                    bank = pr[oc % 2]
                    p.guard_wait(t, prn[oc % 2])
                    for kc in range(4):
                        mi = t.matmul(bank[:],
                                      wo[:, kc * 1024 + oc * 128:kc * 1024 + (oc + 1) * 128],
                                      astream[:, kc * 512:(kc + 1) * 512],
                                      start=(kc == 0), stop=(kc == 3))
                    p.inc(mi, "sP", 1)
                    p.wait(a, "sP")
                    p.guard_wait(a, ("pstg", oc % 2))
                    a.copy(stg2[:, (oc % 2) * 512:(oc % 2 + 1) * 512], bank[:])
                    p.mile(a, "sA")
                    p.guard_set(prn[oc % 2], "sA")
                    p.wait(s, "sA")
                    di = s.dma_start(out=psh[:, oc:oc + 1, tg * 512:(tg + 1) * 512],
                                     in_=stg2[:, (oc % 2) * 512:(oc % 2 + 1) * 512]
                                     .rearrange("p (one t) -> p one t", one=1))
                    p.inc(di, "dS", 16)
                    p.guard_set(("pstg", oc % 2), "dS")
                p.guard_set("astream", "sP")
        p.wait(s, "dS")
        shp = io["sh_partial0"].rearrange("(sl r) t -> sl r t", sl=2)
        di = s.dma_start(out=shp[SL, :, :],
                         in_=io["partial_loc"].rearrange("(one r) t -> one r t", one=1))
        p.inc(di, "dS", 16)
        emit_exchange_flags(p, sb, io, 1, SL)
        with ExitStack() as st2:
            rd = st2.enter_context(nc.sbuf_tensor("at_rd", [128, NCH * 2048], BF16))
            psh = io["sh_partial0"].rearrange("(sl c p) (hf t2) -> sl p c hf t2",
                                              sl=2, p=128, hf=2)
            p.fence(s)
            for slot in range(2):
                p.wait(s, "sG")
                p.guard_wait(s, "rd")
                di = s.dma_start(out=rd[:].rearrange("p (c one t) -> p c one t",
                                                     c=NCH, one=1),
                                 in_=psh[slot][:, :, SL, :])
                rd_ready = p.inc(di, "dS", 16)
                p.wait(v, "dS", rd_ready)
                for c in range(NCH):
                    for tg in range(4):
                        xs = x[:, c * TOK + tg * 512:c * TOK + (tg + 1) * 512]
                        rds = rd[:, c * 2048 + tg * 512:c * 2048 + (tg + 1) * 512]
                        if slot == 0:
                            v.tensor_tensor(sb["tmpA"][:, 0:512], xs, rds, op=ALU.add)
                            v.drain()
                            v.tensor_scalar_add(xs, sb["tmpA"][:, 0:512],
                                                bocols[:, c:c + 1])
                        else:
                            v.tensor_tensor(xs, xs, rds, op=ALU.add)
                p.mile(v, "sV")
                p.guard_set("rd", "sV")




def emit_quant_tile(p, q, out_bf, in_f32, scr):
    nc = p.nc
    v, a = nc.vector, nc.scalar
    bm3 = scr["bm"][:].rearrange("p (b one) -> p b one", one=1)
    in3 = in_f32.rearrange("p (b w) -> p b w", w=32)
    v.tensor_reduce(bm3, in3, axis=AXX, op=ALU.max, apply_absolute_value=True)
    v.drain()
    v.tensor_scalar(scr["sc"][:], scr["bm"][:], 1.0 / 7.5, 1e-30, op0=ALU.mult,
                    op1=ALU.max)
    v.drain()
    v.reciprocal(scr["rs"][:], scr["sc"][:])
    v.drain()
    for b in range(32):
        v.tensor_scalar_mul(scr["vq"][:, b * 32:(b + 1) * 32],
                            in_f32[:, b * 32:(b + 1) * 32], scr["rs"][:, b:b + 1])
    p.mile(v, "sV")
    p.wait(a, "sV")
    a.activation(scr["av"][:], scr["vq"][:], AF.Abs)
    p.mile(a, "sA")
    p.wait(v, "sA")
    v.tensor_single_scalar(scr["g2"][:], scr["av"][:], 2.0, op=ALU.is_ge)
    v.tensor_single_scalar(scr["g4"][:], scr["av"][:], 4.0, op=ALU.is_ge)
    v.drain()
    v.tensor_scalar(scr["rr"][:], scr["g2"][:], -4.0, 8.0, op0=ALU.mult, op1=ALU.add)
    v.tensor_scalar(scr["ss"][:], scr["g2"][:], 0.125, 0.125, op0=ALU.mult, op1=ALU.add)
    v.drain()
    v.scalar_tensor_tensor(scr["rr"][:], scr["g4"][:], -2.0, scr["rr"][:],
                           op0=ALU.mult, op1=ALU.add)
    v.scalar_tensor_tensor(scr["ss"][:], scr["g4"][:], 0.25, scr["ss"][:],
                           op0=ALU.mult, op1=ALU.add)
    v.drain()
    v.tensor_tensor(scr["yy"][:], scr["vq"][:], scr["rr"][:], op=ALU.mult)
    v.drain()
    v.tensor_single_scalar(scr["ff"][:], scr["yy"][:], 12582912.0, op=ALU.add)
    v.drain()
    v.tensor_single_scalar(scr["ff"][:], scr["ff"][:], -12582912.0, op=ALU.add)
    v.drain()
    v.tensor_tensor(scr["g2"][:], scr["ff"][:], scr["yy"][:], op=ALU.subtract)
    v.drain()
    v.tensor_single_scalar(scr["g2"][:], scr["g2"][:], 0.5, op=ALU.is_equal)
    v.drain()
    v.tensor_tensor(scr["ff"][:], scr["ff"][:], scr["g2"][:], op=ALU.subtract)
    v.drain()
    v.tensor_tensor(scr["yy"][:], scr["ff"][:], scr["ss"][:], op=ALU.mult)
    v.drain()
    for b in range(32):
        v.tensor_scalar_mul(out_bf[:, b * 32:(b + 1) * 32],
                            scr["yy"][:, b * 32:(b + 1) * 32], scr["sc"][:, b:b + 1])
    p.mile(v, "sV")


def emit_ffn1(p, sb, ps, io, tag):
    nc = p.nc
    t, v, a, g, s = nc.tensor, nc.vector, nc.scalar, nc.gpsimd, nc.sync
    x = sb["x"]
    emit_ln(p, sb, ps, stats_only=True)
    with ExitStack() as st:
        wd = st.enter_context(nc.sbuf_tensor(f"{tag}_wd", [128, NCH * 1024], BF16))
        xd = st.enter_context(nc.sbuf_tensor(f"{tag}_xd", [128, NCH * TOK], BF16))
        scr = {}
        for nm in ("bm", "sc", "rs"):
            scr[nm] = st.enter_context(nc.sbuf_tensor(f"{tag}_" + nm, [128, 32], F32))
        for nm in ("vq", "av", "g2", "g4", "rr", "ss", "yy", "ff", "wtmp"):
            scr[nm] = st.enter_context(nc.sbuf_tensor(f"{tag}_" + nm, [128, 1024], F32))
        qtile = st.enter_context(nc.sbuf_tensor(f"{tag}_qt", [128, 1024], BF16))
        hbias = st.enter_context(nc.sbuf_tensor(f"{tag}_hb", [128, 8], F32))
        bcols = st.enter_context(nc.sbuf_tensor(f"{tag}_bc", [128, 8], F32))
        p.fence(s)
        p.inc(s.dma_start(out=bcols[:], in_=io["l1fb_cols"]), "dS", 16)
        p.wait(v, "dS")
        v.tensor_scalar_mul(hbias[:], bcols[:], 0.5)
        p.mile(v, "sV")
        pT = ps["H"]
        for rc in range(NCH):
            p.guard_wait(s, "wtmp")
            di = s.dma_start(out=scr["wtmp"][:],
                             in_=io["l1fw"][rc * 128:(rc + 1) * 128, :])
            p.inc(di, "dS", 16)
            p.wait(v, "dS")
            emit_quant_tile(p, None, qtile[:], scr["wtmp"][:], scr)
            p.guard_set("wtmp", "sV")
            p.wait(t, "sV")
            p.guard_wait(t, "psH")
            for cc in range(NCH):
                ti = t.transpose(pT[:, cc * 128:(cc + 1) * 128],
                                 qtile[:, cc * 128:(cc + 1) * 128], sb["eye_b"][:])
            p.inc(ti, "sP", 1)
            p.wait(v, "sP")
            for cc in range(NCH):
                v.tensor_copy(wd[:, cc * 1024 + rc * 128:cc * 1024 + (rc + 1) * 128],
                              pT[:, cc * 128:(cc + 1) * 128])
            p.mile(v, "sV")
            p.guard_set("psH", "sV")
        pF = ps["A"]
        for tile in range(16):
            m_col = sb["mr_cols"][:, tile:tile + 1]
            r_col = sb["mr_cols"][:, 16 + tile:17 + tile]
            for half in range(2):
                p.wait(t, "sV")
                p.guard_wait(t, "psA")
                for cl in range(4):
                    c = half * 4 + cl
                    ti = t.transpose(pF[:, cl * 128:(cl + 1) * 128],
                                     x[:, c * TOK + tile * 128:c * TOK + (tile + 1) * 128],
                                     sb["eye_f"][:])
                p.inc(ti, "sP", 1)
                p.wait(v, "sP")
                if half == 0:
                    p.guard_wait(v, "wtmp")
                v.tensor_scalar(scr["wtmp"][:, half * 512:(half + 1) * 512], pF[:],
                                m_col, r_col, op0=ALU.subtract, op1=ALU.mult)
                p.mile(v, "sV")
                p.guard_set("psA", "sV")
            emit_quant_tile(p, None, qtile[:], scr["wtmp"][:], scr)
            p.guard_set("wtmp", "sV")
            p.wait(t, "sV")
            p.guard_wait(t, "psH")
            for c in range(NCH):
                ti = t.transpose(pT[:, c * 128:(c + 1) * 128],
                                 qtile[:, c * 128:(c + 1) * 128], sb["eye_b"][:])
            p.inc(ti, "sP", 1)
            p.wait(v, "sP")
            for c in range(NCH):
                v.tensor_copy(xd[:, c * TOK + tile * 128:c * TOK + (tile + 1) * 128],
                              pT[:, c * 128:(c + 1) * 128])
            p.mile(v, "sV")
            p.guard_set("psH", "sV")
        xd_ready = p.cnt("sV")
        p.wait(t, "sV", xd_ready)
        pr = [ps["B"], ps["C"]]
        prn = ["psB", "psC"]
        for oc in range(NCH):
            for tg in range(4):
                bank = pr[(oc * 4 + tg) % 2]
                bname = prn[(oc * 4 + tg) % 2]
                p.guard_wait(t, bname)
                for kc in range(NCH):
                    mi = t.matmul(bank[:],
                                  wd[:, kc * 1024 + oc * 128:kc * 1024 + (oc + 1) * 128],
                                  xd[:, kc * TOK + tg * 512:kc * TOK + (tg + 1) * 512],
                                  start=(kc == 0), stop=(kc == NCH - 1))
                p.inc(mi, "sP", 1)
                p.wait(v, "sP")
                p.guard_wait(v, ("fstg", 0))
                v.tensor_scalar(sb["tmpB"][:, 0:512], bank[:], 0.5, hbias[:, oc:oc + 1],
                                op0=ALU.mult, op1=ALU.add)
                v.drain()
                p.mile(v, "sV")
                p.guard_set(bname, "sV")
                xs = x[:, oc * TOK + tg * 512:oc * TOK + (tg + 1) * 512]
                v.tensor_tensor(xs, xs, sb["tmpB"][:, 0:512], op=ALU.add)
                p.mile(v, "sV")
                p.guard_set(("fstg", 0), "sV")




def emit_attn1(p, sb, ps, io, ctx):
    nc = p.nc
    t, v, a, g, s = nc.tensor, nc.vector, nc.scalar, nc.gpsimd, nc.sync
    x = sb["x"]
    NT = T // 512

    with ExitStack() as st:
        V_sb = st.enter_context(nc.sbuf_tensor("a1_V", [128, 32 * 520], BF16))
        krs = st.enter_context(nc.sbuf_tensor("a1_krs", [32, T], BF16))
        negU = st.enter_context(nc.sbuf_tensor("a1_negU", [128, 8], F32))
        nrm8 = st.enter_context(nc.sbuf_tensor("a1_nrm8", [8, 8], F32))
        crow = st.enter_context(nc.sbuf_tensor("a1_crow", [1, 8], F32))
        sb["selH"] = st.enter_context(nc.sbuf_tensor("a1_selH", [128, 32], F32))
        sb["selHr"] = st.enter_context(nc.sbuf_tensor("a1_selHr", [128, 16], F32))
        sb["trimask"] = st.enter_context(nc.sbuf_tensor("a1_trimask", [128, 128], F32))
        p.fence(s)
        for nm in ("selH", "selHr", "trimask"):
            p.inc(s.dma_start(out=sb[nm][:], in_=io[nm]), "dS", 16)
        p.inc(s.dma_start(out=crow[:], in_=io["lobo"]), "dS", 16)
        p.wait(s, "dS")
        preg_s = nc.values_load(sb["par"][0:1, 0:1], engines=[s.engine],
                                skip_runtime_bounds_check=True)
        SL = bass.ts(preg_s, 1)

        p.wait(a, "dS")
        a.activation(crow[:], crow[:], AF.Exp)
        p.mile(a, "sA")

        with ExitStack() as st2:
            stage = st2.enter_context(nc.sbuf_tensor("a1_nstage", [128, NCH * 512], BF16))
            emit_ln(p, sb, ps, to_dram=(io["normed_loc"], stage))
            p.wait(s, "dS")
            shn = io["sh_normed1"].rearrange("r (hf t2) -> r hf t2", hf=2)
            di = s.dma_start(out=shn[:, SL, :],
                             in_=io["normed_loc"].rearrange("r (one t) -> r one t", one=1))
            p.inc(di, "dS", 16)
            emit_exchange_flags(p, sb, io, 2, SL)

        v.memset(V_sb[:], 1.0)
        p.mile(v, "sV")

        with ExitStack() as st2:
            wq = st2.enter_context(nc.sbuf_tensor("a1_wq", [128, NCH * 512], BF16))
            wqr = st2.enter_context(nc.sbuf_tensor("a1_wqr", [128, NCH * 512], BF16))
            wkvd = st2.enter_context(nc.sbuf_tensor("a1_wkvd", [128, NCH * 256], BF16))
            wkr = st2.enter_context(nc.sbuf_tensor("a1_wkr", [128, NCH * 64], BF16))
            wku = st2.enter_context(nc.sbuf_tensor("a1_wku", [128, 2 * 512], BF16))
            wvu = st2.enter_context(nc.sbuf_tensor("a1_wvu", [128, 2 * 512], BF16))
            nstream = st2.enter_context(nc.sbuf_tensor("a1_nst", [128, NCH * 512], BF16))
            stg = st2.enter_context(nc.sbuf_tensor("a1_stg", [128, 4 * 512], BF16))
            lat = st2.enter_context(nc.sbuf_tensor("a1_lat", [128, 2 * 512], BF16))
            rope = [st2.enter_context(nc.sbuf_tensor(f"a1_rope{i}", [128, 2 * 512], F32))
                    for i in range(2)]
            p.fence(g)
            for nm, ionm in ((wq, "l1_wq"), (wqr, "l1_wqr_cat"), (wkvd, "l1_wkvd"),
                             (wkr, "l1_wkr_cat"), (wku, "l1_wku"), (wvu, "l1_wvu")):
                ncol = nm.shape[1] // NCH if nm not in (wku, wvu) else 512
                p.inc(g.dma_start(out=nm[:].rearrange("p (c o) -> p c o",
                                                      c=NCH if nm not in (wku, wvu) else 2),
                                  in_=io[ionm].rearrange("(c p) o -> p c o", p=128)),
                      "dG", 16)
            p.wait(g, "sG")
            nsh = io["sh_normed1"].rearrange("(c p) t -> p c t", p=128)
            pq = [ps["A"], ps["B"], ps["C"], ps["F"]]
            pqn = ["psA", "psB", "psC", "psF"]
            stats1, stats2 = ps["D"], ps["E"]
            for tg in range(NT):
                p.guard_wait(g, "nstream")
                di = g.dma_start(out=nstream[:].rearrange("p (c t2) -> p c t2", c=NCH),
                                 in_=nsh[:, :, tg * 512:(tg + 1) * 512])
                nst_ready = p.inc(di, "dG", 16)
                rb = rope[tg % 2]
                p.guard_wait(g, ("rope", tg % 2))
                p.inc(g.dma_start(out=rb[:, 0:512],
                                  in_=io["ropecos"][:, tg * 512:(tg + 1) * 512]), "dG", 16)
                rope_ready = p.inc(g.dma_start(out=rb[:, 512:1024],
                                               in_=io["ropesin"][:, tg * 512:(tg + 1) * 512]),
                                   "dG", 16)
                p.wait(t, "dG", nst_ready)
                for oc in range(4):
                    p.guard_wait(t, pqn[oc])
                    for kc in range(NCH):
                        mi = t.matmul(pq[oc][:],
                                      wq[:, kc * 512 + oc * 128:kc * 512 + (oc + 1) * 128],
                                      nstream[:, kc * 512:(kc + 1) * 512],
                                      start=(kc == 0), stop=(kc == NCH - 1))
                    p.inc(mi, "sP", 1)
                p.wait(v, "sP")
                for oc in range(4):
                    v.tensor_copy(stg[:, oc * 512:(oc + 1) * 512], pq[oc][:])
                p.mile(v, "sV")
                for oc in range(4):
                    p.guard_set(pqn[oc], "sV")
                emit_norm_update(p, sb, ps, stg, nrm8, 0, tg, 4, "selH")
                p.wait(s, "sV")
                p.guard_wait(s, "stg")
                di = s.dma_start(out=io["q_dr"].rearrange("(oc p2) t -> p2 oc t", p2=128)
                                 [:, :, tg * 512:(tg + 1) * 512],
                                 in_=stg[:].rearrange("p (oc t2) -> p oc t2", oc=4))
                p.inc(di, "dS", 16)
                p.guard_set("stg", "dS")
                for oc in range(4):
                    p.guard_wait(t, pqn[oc])
                    for kc in range(NCH):
                        mi = t.matmul(pq[oc][:],
                                      wqr[:, kc * 512 + oc * 128:kc * 512 + (oc + 1) * 128],
                                      nstream[:, kc * 512:(kc + 1) * 512],
                                      start=(kc == 0), stop=(kc == NCH - 1))
                    p.inc(mi, "sP", 1)
                p.wait(v, "sP")
                p.wait(v, "dG", rope_ready)
                p.guard_wait(v, "stg")
                for c in range(2):
                    v.tensor_tensor(sb["tmpA"][:, 0:512], pq[c][:], rb[:, 0:512], op=ALU.mult)
                    v.drain()
                    v.tensor_tensor(sb["tmpB"][:, 0:512], pq[2 + c][:], rb[:, 512:1024],
                                    op=ALU.mult)
                    v.drain()
                    v.tensor_tensor(stg[:, c * 512:(c + 1) * 512], sb["tmpA"][:, 0:512],
                                    sb["tmpB"][:, 0:512], op=ALU.add)
                    v.drain()
                p.mile(v, "sV")
                for oc in range(4):
                    p.guard_set(pqn[oc], "sV")
                emit_norm_update(p, sb, ps, stg, nrm8, 1, tg, 2, "selHr")
                p.wait(s, "sV")
                di = s.dma_start(out=io["qr_dr"].rearrange("(oc p2) t -> p2 oc t", p2=128)
                                 [:, :, tg * 512:(tg + 1) * 512],
                                 in_=stg[:, 0:1024].rearrange("p (oc t2) -> p oc t2", oc=2))
                p.inc(di, "dS", 16)
                p.guard_set("stg", "dS")
                for oc in range(2):
                    p.guard_wait(t, pqn[oc])
                    for kc in range(NCH):
                        mi = t.matmul(pq[oc][:],
                                      wkvd[:, kc * 256 + oc * 128:kc * 256 + (oc + 1) * 128],
                                      nstream[:, kc * 512:(kc + 1) * 512],
                                      start=(kc == 0), stop=(kc == NCH - 1))
                    p.inc(mi, "sP", 1)
                p.wait(v, "sP")
                p.guard_wait(v, "lat")
                for oc in range(2):
                    v.tensor_copy(lat[:, oc * 512:(oc + 1) * 512], pq[oc][:])
                p.mile(v, "sV")
                for oc in range(2):
                    p.guard_set(pqn[oc], "sV")
                p.guard_wait(t, pqn[2])
                for kc in range(NCH):
                    mi = t.matmul(pq[2][0:64, :], wkr[:, kc * 64:(kc + 1) * 64],
                                  nstream[:, kc * 512:(kc + 1) * 512],
                                  start=(kc == 0), stop=(kc == NCH - 1))
                p.inc(mi, "sP", 1)
                p.wait(v, "sP")
                v.tensor_tensor(sb["tmpA"][0:32, 0:512], pq[2][0:32, :], rb[0:32, 0:512],
                                op=ALU.mult)
                v.drain()
                v.tensor_tensor(sb["tmpB"][0:32, 0:512], pq[2][32:64, :], rb[0:32, 512:1024],
                                op=ALU.mult)
                v.drain()
                v.tensor_tensor(krs[:, tg * 512:(tg + 1) * 512], sb["tmpA"][0:32, 0:512],
                                sb["tmpB"][0:32, 0:512], op=ALU.add)
                p.mile(v, "sV")
                p.guard_set(pqn[2], "sV")
                p.guard_set(("rope", tg % 2), "sV")
                p.wait(a, "sV")
                p.guard_wait(a, ("sqB", 0))
                a.activation(sb["tmpB"][0:32, 0:512], krs[:, tg * 512:(tg + 1) * 512],
                             AF.Square)
                p.mile(a, "sA")
                p.wait(t, "sA")
                p.guard_wait(t, "psD")
                mi = t.matmul(stats1[0:1, :], sb["ones1"][0:32, :], sb["tmpB"][0:32, 0:512],
                              start=True, stop=True)
                p.inc(mi, "sP", 1)
                p.guard_set(("sqB", 0), "sP")
                p.wait(v, "sP")
                v.tensor_reduce(sb["tmpA"][0:1, 0:1], stats1[0:1, :], axis=AXX, op=ALU.max)
                v.drain()
                if tg == 0:
                    v.tensor_copy(nrm8[0:1, 3:4], sb["tmpA"][0:1, 0:1])
                else:
                    v.tensor_tensor(nrm8[0:1, 3:4], nrm8[0:1, 3:4], sb["tmpA"][0:1, 0:1],
                                    op=ALU.max)
                p.mile(v, "sV")
                p.guard_set("psD", "sV")
                p.wait(t, "sV")
                for oc in range(4):
                    p.guard_wait(t, pqn[oc])
                    for kc in range(2):
                        mi = t.matmul(pq[oc][:],
                                      wku[:, kc * 512 + oc * 128:kc * 512 + (oc + 1) * 128],
                                      lat[:, kc * 512:(kc + 1) * 512],
                                      start=(kc == 0), stop=(kc == 1))
                    p.inc(mi, "sP", 1)
                p.wait(v, "sP")
                p.guard_wait(v, "stg")
                for oc in range(4):
                    v.tensor_copy(stg[:, oc * 512:(oc + 1) * 512], pq[oc][:])
                p.mile(v, "sV")
                for oc in range(4):
                    p.guard_set(pqn[oc], "sV")
                emit_norm_update(p, sb, ps, stg, nrm8, 2, tg, 4, "selH")
                p.wait(s, "sV")
                di = s.dma_start(out=io["k_dr"].rearrange("(oc p2) t -> p2 oc t", p2=128)
                                 [:, :, tg * 512:(tg + 1) * 512],
                                 in_=stg[:].rearrange("p (oc t2) -> p oc t2", oc=4))
                p.inc(di, "dS", 16)
                p.guard_set("stg", "dS")
                for ttl in range(4):
                    tt = tg * 4 + ttl
                    bank = pq[ttl % 2]
                    p.guard_wait(t, pqn[ttl % 2])
                    for kc in range(2):
                        mi = t.matmul(bank[:],
                                      lat[:, kc * 512 + ttl * 128:kc * 512 + (ttl + 1) * 128],
                                      wvu[:, kc * 512:(kc + 1) * 512],
                                      start=(kc == 0), stop=(kc == 1))
                    p.inc(mi, "sP", 1)
                    p.wait(v, "sP")
                    vv = V_sb[:].rearrange("p (tt hb d) -> p tt hb d", tt=32, hb=8)
                    v.tensor_copy(vv[:, tt, :, 0:64],
                                  bank[:].rearrange("p (hb d) -> p hb d", hb=8))
                    p.mile(v, "sV")
                    p.guard_set(pqn[ttl % 2], "sV")
                p.guard_set("lat", "sP")
                p.guard_set("nstream", "sP")

        v.tensor_tensor(nrm8[0:8, 4:5], nrm8[0:8, 0:1], nrm8[0:8, 1:2], op=ALU.add)
        v.drain()
        p.mile(v, "sV")
        p.wait(t, "sV")
        p.guard_wait(t, "psD")
        ti = t.matmul(ps["D"][0:8, 0:1], sb["ones_row"][0:1, 0:8], nrm8[0:1, 3:4],
                      start=True, stop=True)
        p.inc(ti, "sP", 1)
        p.wait(v, "sP")
        v.tensor_tensor(nrm8[0:8, 5:6], nrm8[0:8, 2:3], ps["D"][0:8, 0:1], op=ALU.add)
        v.drain()
        v.tensor_tensor(nrm8[0:8, 6:7], nrm8[0:8, 4:5], nrm8[0:8, 5:6], op=ALU.mult)
        p.mile(v, "sV")
        p.guard_set("psD", "sV")
        p.wait(a, "sV")
        a.activation(nrm8[0:8, 7:8], nrm8[0:8, 6:7], AF.Ln)
        a.drain()
        a.activation(nrm8[0:8, 6:7], nrm8[0:8, 7:8], AF.Exp, scale=0.5)
        p.mile(a, "sA")
        p.wait(t, "sA")
        p.guard_wait(t, "psD")
        ti = t.transpose(ps["D"][0:1, 0:8], nrm8[0:8, 6:7], sb["eye_f"][0:8, 0:8])
        p.inc(ti, "sP", 1)
        p.wait(v, "sP")
        v.tensor_copy(sb["tmpA"][0:1, 0:8], ps["D"][0:1, 0:8])
        p.mile(v, "sV")
        p.wait(t, "sV")
        ti = t.matmul(ps["D"][:, 0:8], sb["ones_row"][:], sb["tmpA"][0:1, 0:8],
                      start=True, stop=True)
        p.inc(ti, "sP", 1)
        p.wait(a, "sP")
        a.activation(negU[:], ps["D"][:, 0:8], AF.Copy, scale=-SCALE1)
        p.mile(a, "sA")
        p.guard_set("psD", "sA")
        p.wait(a, "sA")
        a.activation(nrm8[0:8, 7:8], nrm8[0:8, 6:7], AF.Exp, scale=-SCALE1)
        p.mile(a, "sA")
        p.wait(t, "sA")
        p.guard_wait(t, "psE")
        ti = t.transpose(ps["E"][0:8, 0:1], crow[0:1, 0:8], sb["eye_f"][0:1, 0:1])
        p.inc(ti, "sP", 1)
        p.wait(v, "sP")
        v.tensor_tensor(nrm8[0:8, 4:5], ps["E"][0:8, 0:1], nrm8[0:8, 7:8], op=ALU.mult)
        v.drain()
        p.mile(v, "sV")
        p.wait(t, "sV")
        ti = t.transpose(ps["E"][0:1, 0:8], nrm8[0:8, 4:5], sb["eye_f"][0:8, 0:8])
        p.inc(ti, "sP", 1)
        p.wait(v, "sP")
        v.tensor_copy(crow[0:1, 0:8], ps["E"][0:1, 0:8])
        p.mile(v, "sV")
        p.guard_set("psE", "sV")

        with ExitStack() as st2:
            bufs = {
                "kst": [st2.enter_context(nc.sbuf_tensor(f"a1_kst{i}", [64, T], BF16))
                        for i in range(2)],
                "qst": [st2.enter_context(nc.sbuf_tensor(f"a1_qst{i}", [64, 512], BF16))
                        for i in range(2)],
                "qrst": [st2.enter_context(nc.sbuf_tensor(f"a1_qrst{i}", [32, 512], BF16))
                         for i in range(2)],
                "Pt": [st2.enter_context(nc.sbuf_tensor(f"a1_pt{i}", [128, 512], BF16))
                       for i in range(2)],
                "arow": [st2.enter_context(nc.sbuf_tensor(f"a1_arow{i}", [64, 512], BF16))
                         for i in range(2)],
            }
            sb["V_cur"] = V_sb
            sb["krs_cur"] = krs
            units = [(h, qg) for h in range(NH) for qg in range(NT)]
            p.fence(g)
            for ui, (h, qg) in enumerate(units):
                emit_flash(p, sb, ps, io, bufs, h, qg, ui, 1, negU, crow=crow)

        with ExitStack() as st2:
            wo = st2.enter_context(nc.sbuf_tensor("a1_wo", [128, 4 * 1024], BF16))
            astream = st2.enter_context(nc.sbuf_tensor("a1_ast", [128, 4 * 512], BF16))
            stg2 = st2.enter_context(nc.sbuf_tensor("a1_stg2", [128, 2 * 512], BF16))
            p.fence(g)
            p.inc(g.dma_start(out=wo[:].rearrange("p (c o) -> p c o", c=4),
                              in_=io["l1_wo"].rearrange("(c p) o -> p c o", p=128)), "dG", 16)
            ash = io["attn_dr"].rearrange("(c p) t -> p c t", p=128)
            p.wait(a, "dS")
            pr = [ps["A"], ps["B"]]
            prn = ["psA", "psB"]
            psh = io["partial_loc"].rearrange("(c p) t -> p c t", p=128)
            for tg in range(NT):
                p.guard_wait(g, "astream")
                di = g.dma_start(out=astream[:].rearrange("p (c t2) -> p c t2", c=4),
                                 in_=ash[:, :, tg * 512:(tg + 1) * 512])
                nst_ready = p.inc(di, "dG", 16)
                p.wait(t, "dG", nst_ready)
                for oc in range(NCH):
                    bank = pr[oc % 2]
                    p.guard_wait(t, prn[oc % 2])
                    for kc in range(4):
                        mi = t.matmul(bank[:],
                                      wo[:, kc * 1024 + oc * 128:kc * 1024 + (oc + 1) * 128],
                                      astream[:, kc * 512:(kc + 1) * 512],
                                      start=(kc == 0), stop=(kc == 3))
                    p.inc(mi, "sP", 1)
                    p.wait(a, "sP")
                    p.guard_wait(a, ("pstg", oc % 2))
                    a.copy(stg2[:, (oc % 2) * 512:(oc % 2 + 1) * 512], bank[:])
                    p.mile(a, "sA")
                    p.guard_set(prn[oc % 2], "sA")
                    p.wait(s, "sA")
                    di = s.dma_start(out=psh[:, oc:oc + 1, tg * 512:(tg + 1) * 512],
                                     in_=stg2[:, (oc % 2) * 512:(oc % 2 + 1) * 512]
                                     .rearrange("p (one t) -> p one t", one=1))
                    p.inc(di, "dS", 16)
                    p.guard_set(("pstg", oc % 2), "dS")
                p.guard_set("astream", "sP")
        p.wait(s, "dS")
        shp = io["sh_partial1"].rearrange("(sl r) t -> sl r t", sl=2)
        di = s.dma_start(out=shp[SL, :, :],
                         in_=io["partial_loc"].rearrange("(one r) t -> one r t", one=1))
        p.inc(di, "dS", 16)
        emit_exchange_flags(p, sb, io, 3, SL)
        with ExitStack() as st2:
            rd = st2.enter_context(nc.sbuf_tensor("a1_rd", [128, NCH * 2048], BF16))
            psh = io["sh_partial1"].rearrange("(sl c p) (hf t2) -> sl p c hf t2",
                                              sl=2, p=128, hf=2)
            p.fence(s)
            for slot in range(2):
                p.wait(s, "sG")
                p.guard_wait(s, "rd")
                di = s.dma_start(out=rd[:].rearrange("p (c one t) -> p c one t",
                                                     c=NCH, one=1),
                                 in_=psh[slot][:, :, SL, :])
                rd_ready = p.inc(di, "dS", 16)
                p.wait(v, "dS", rd_ready)
                for c in range(NCH):
                    for tg in range(4):
                        xs = x[:, c * TOK + tg * 512:c * TOK + (tg + 1) * 512]
                        rds = rd[:, c * 2048 + tg * 512:c * 2048 + (tg + 1) * 512]
                        v.tensor_tensor(xs, xs, rds, op=ALU.add)
                p.mile(v, "sV")
                p.guard_set("rd", "sV")


def emit_norm_update(p, sb, ps, stg, nrm8, col, tg, nchunks, selname):
    nc = p.nc
    t, v, a = nc.tensor, nc.vector, nc.scalar
    stats1 = ps["D"]
    p.wait(a, "sV")
    p.guard_wait(t, "psD")
    for oc in range(nchunks):
        p.guard_wait(a, ("sqB", 0))
        a.activation(sb["tmpB"][:, 0:512], stg[:, oc * 512:(oc + 1) * 512], AF.Square)
        p.mile(a, "sA")
        p.wait(t, "sA")
        mi = t.matmul(stats1[0:8, :], sb[selname][:, oc * 8:(oc + 1) * 8],
                      sb["tmpB"][:, 0:512], start=(oc == 0), stop=(oc == nchunks - 1))
        p.inc(mi, "sP", 1)
        p.guard_set(("sqB", 0), "sP")
    p.wait(v, "sP")
    v.tensor_reduce(sb["tmpA"][0:8, 0:1], stats1[0:8, :], axis=AXX, op=ALU.max)
    v.drain()
    if tg == 0:
        v.tensor_copy(nrm8[0:8, col:col + 1], sb["tmpA"][0:8, 0:1])
    else:
        v.tensor_tensor(nrm8[0:8, col:col + 1], nrm8[0:8, col:col + 1],
                        sb["tmpA"][0:8, 0:1], op=ALU.max)
    p.mile(v, "sV")
    p.guard_set("psD", "sV")


def emit_postamble(p, io, sb):
    nc = p.nc
    s = nc.sync
    p.wait(s, "sV")
    p.wait(s, "sA")
    p.wait(s, "sP")
    di = s.dma_start(out=io["x_out"].rearrange("(c p) t -> p c t", p=128),
                     in_=sb["x"][:].rearrange("p (c t) -> p c t", c=NCH))
    p.inc(di, "dS", 16)
    if "dbg_rows" in io:
        for i, nm in enumerate(("row_m", "row_r", "row_mr", "row_t")):
            p.inc(s.dma_start(out=io["dbg_rows"][i:i + 1, :], in_=sb[nm][0:1, :]), "dS", 16)
        p.inc(s.dma_start(out=io["dbg_gates"], in_=sb["gates"][0:8, :]), "dS", 16)
        p.inc(s.dma_start(out=io["dbg_normed"].rearrange("(c p) t -> p c t", p=128),
                          in_=sb["normed"][:].rearrange("p (c t) -> p c t", c=NCH)), "dS", 16)
        if "dbg_scr" in sb:
            p.inc(s.dma_start(out=io["dbg_scr"], in_=sb["dbg_scr"][:]), "dS", 16)
    p.wait(s, "dS")



def build_program(plan):
    nc = bass.Bass("TRN2", target_bir_lowering=False)
    io = {}
    io["x_in"] = nc.dram_tensor("x_in", [D, TOK], F32, kind="ExternalInput").ap()
    io["x_out"] = nc.dram_tensor("x_out", [D, TOK], F32, kind="ExternalOutput").ap()
    io["par"] = nc.dram_tensor("par", [1, 1], I32, kind="ExternalInput").ap()
    io["ones1"] = nc.dram_tensor("ones1", [128, 1], F32, kind="ExternalInput").ap()
    io["ones_row"] = nc.dram_tensor("ones_row", [1, 128], F32, kind="ExternalInput").ap()
    io["eye_f"] = nc.dram_tensor("eye_f", [128, 128], F32, kind="ExternalInput").ap()
    io["iotaB"] = nc.dram_tensor("iotaB", [128, 8], F32, kind="ExternalInput").ap()
    io["sel8"] = nc.dram_tensor("sel8", [8, 8 * 128], F32, kind="ExternalInput").ap()
    io["selH"] = nc.dram_tensor("selH", [128, 32], F32, kind="ExternalInput").ap()
    io["selT"] = nc.dram_tensor("selT", [8, 4 * 128], F32, kind="ExternalInput").ap()
    io["selHr"] = nc.dram_tensor("selHr", [128, 16], F32, kind="ExternalInput").ap()
    io["trimask"] = nc.dram_tensor("trimask", [128, 128], F32, kind="ExternalInput").ap()
    io["nonces"] = nc.dram_tensor("nonces", [8, 1], I32, kind="ExternalInput").ap()
    if os.environ.get("MK_DEBUG"):
        io["dbg_rows"] = nc.dram_tensor("dbg_rows", [4, TOK], F32, kind="ExternalOutput").ap()
        io["dbg_gates"] = nc.dram_tensor("dbg_gates", [8, TOK], F32, kind="ExternalOutput").ap()
        io["dbg_normed"] = nc.dram_tensor("dbg_normed", [D, TOK], BF16, kind="ExternalOutput").ap()
        io["dbg_scr"] = nc.dram_tensor("dbg_scr", [128, 320], F32, kind="ExternalOutput").ap()
    has_moe = any(sl in plan for sl in ("moe0", "moe1"))
    if "attn0" in plan:
        io["l0_wq"] = nc.dram_tensor("l0_wq", [D, 512], F32, kind="ExternalInput").ap()
        io["l0_wk"] = nc.dram_tensor("l0_wk", [D, 512], F32, kind="ExternalInput").ap()
        io["l0_wv"] = nc.dram_tensor("l0_wv", [D, 512], F32, kind="ExternalInput").ap()
        io["l0_wo"] = nc.dram_tensor("l0_wo", [512, D], F32, kind="ExternalInput").ap()
        io["qln_g"] = nc.dram_tensor("qln_g", [128, 1], F32, kind="ExternalInput").ap()
        io["qln_b"] = nc.dram_tensor("qln_b", [128, 1], F32, kind="ExternalInput").ap()
        io["kln_g"] = nc.dram_tensor("kln_g", [128, 1], F32, kind="ExternalInput").ap()
        io["kln_b"] = nc.dram_tensor("kln_b", [128, 1], F32, kind="ExternalInput").ap()
        io["l0_bo_cols"] = nc.dram_tensor("l0_bo_cols", [128, 8], F32, kind="ExternalInput").ap()
        io["sh_normed0"] = nc.dram_tensor("sh_normed0", [D, T], BF16,
                                          kind="Internal", addr_space="Shared").ap()
        io["sh_partial0"] = nc.dram_tensor("sh_partial0", [2 * D, T], BF16,
                                           kind="Internal", addr_space="Shared").ap()
        io["q_dr"] = nc.dram_tensor("q_dr", [512, T], BF16, kind="Internal").ap()
        io["normed_loc"] = nc.dram_tensor("normed_loc", [D, TOK], BF16, kind="Internal").ap()
        io["partial_loc"] = nc.dram_tensor("partial_loc", [D, T], BF16, kind="Internal").ap()
        io["k_dr"] = nc.dram_tensor("k_dr", [512, T], BF16, kind="Internal").ap()
        io["attn_dr"] = nc.dram_tensor("attn_dr", [512, T], BF16, kind="Internal").ap()
    if any(sl in plan for sl in ("ffn1a", "ffn1b")):
        io["l1fw"] = nc.dram_tensor("l1fw", [D, D], F32, kind="ExternalInput").ap()
        io["l1fb_cols"] = nc.dram_tensor("l1fb_cols", [128, 8], F32, kind="ExternalInput").ap()
    if "attn1" in plan:
        io["l1_wq"] = nc.dram_tensor("l1_wq", [D, 512], F32, kind="ExternalInput").ap()
        io["l1_wqr_cat"] = nc.dram_tensor("l1_wqr_cat", [D, 512], F32, kind="ExternalInput").ap()
        io["l1_wkvd"] = nc.dram_tensor("l1_wkvd", [D, 256], F32, kind="ExternalInput").ap()
        io["l1_wkr_cat"] = nc.dram_tensor("l1_wkr_cat", [D, 64], F32, kind="ExternalInput").ap()
        io["l1_wku"] = nc.dram_tensor("l1_wku", [256, 512], F32, kind="ExternalInput").ap()
        io["l1_wvu"] = nc.dram_tensor("l1_wvu", [256, 512], F32, kind="ExternalInput").ap()
        io["l1_wo"] = nc.dram_tensor("l1_wo", [512, D], F32, kind="ExternalInput").ap()
        io["lobo"] = nc.dram_tensor("lobo", [1, 8], F32, kind="ExternalInput").ap()
        io["ropecos"] = nc.dram_tensor("ropecos", [128, T], F32, kind="ExternalInput").ap()
        io["ropesin"] = nc.dram_tensor("ropesin", [128, T], F32, kind="ExternalInput").ap()
        io["sh_normed1"] = nc.dram_tensor("sh_normed1", [D, T], BF16,
                                          kind="Internal", addr_space="Shared").ap()
        io["sh_partial1"] = nc.dram_tensor("sh_partial1", [2 * D, T], BF16,
                                           kind="Internal", addr_space="Shared").ap()
        if "attn0" not in plan:
            io["q_dr"] = nc.dram_tensor("q_dr", [512, T], BF16, kind="Internal").ap()
            io["k_dr"] = nc.dram_tensor("k_dr", [512, T], BF16, kind="Internal").ap()
            io["attn_dr"] = nc.dram_tensor("attn_dr", [512, T], BF16, kind="Internal").ap()
            io["normed_loc"] = nc.dram_tensor("normed_loc", [D, TOK], BF16, kind="Internal").ap()
            io["partial_loc"] = nc.dram_tensor("partial_loc", [D, T], BF16, kind="Internal").ap()
        io["qr_dr"] = nc.dram_tensor("qr_dr", [256, T], BF16, kind="Internal").ap()
    if any(sl.startswith("attn") for sl in plan):
        io["sh_flags"] = nc.dram_tensor("sh_flags", [16, 1], I32,
                                        kind="Internal", addr_space="Shared").ap()
    if has_moe:
        io["gate_wT"] = nc.dram_tensor("gate_wT", [D, 8], F32, kind="ExternalInput").ap()
        io["w1"] = nc.dram_tensor("w1", [NEXP, D, HID], F32, kind="ExternalInput").ap()
        io["w2"] = nc.dram_tensor("w2", [NEXP, HID, D], F32, kind="ExternalInput").ap()
        io["w3"] = nc.dram_tensor("w3", [NEXP, D, HID], F32, kind="ExternalInput").ap()

    with ExitStack() as ctx:
        p = Prog(nc, ctx)
        for name in ("dS", "dG", "sP", "sV", "sA", "sG"):
            p.sem(name)
        sb = {}

        def alloc(name, shape, dtype):
            sb[name] = ctx.enter_context(nc.sbuf_tensor("sb_" + name, shape, dtype))
            return sb[name]

        alloc("x", [128, NCH * TOK], F32)
        alloc("ones1", [128, 1], F32)
        alloc("ones_row", [1, 128], F32)
        alloc("eye_f", [128, 128], F32)
        alloc("eye_b", [128, 128], BF16)
        alloc("iotaB", [128, 8], F32)
        alloc("sel8", [8, 8 * 128], F32)
        has_attn = any(sl.startswith("attn") for sl in plan)
        alloc("par", [1, 1], I32)
        alloc("nonces", [8, 1], I32)
        alloc("row_m", [1, TOK], F32)
        alloc("row_r", [1, TOK], F32)
        alloc("row_mr", [1, TOK], F32)
        alloc("row_t", [1, TOK], F32)
        alloc("mr_cols", [128, 32], F32)
        alloc("tmpA", [128, 1024], F32)
        alloc("tmpB", [128, 512], F32)
        if os.environ.get("MK_DEBUG"):
            alloc("dbg_scr", [128, 320], F32)

        ps = {}
        for nm in ("A", "B", "C", "D", "E", "F"):
            ps[nm] = ctx.enter_context(nc.psum_tensor(f"ps{nm}", [128, 512], F32))
        ps["H"] = ctx.enter_context(nc.psum_tensor("psH", [128, 1024], BF16))

        with nc.Block() as block:
            emit_preamble(p, io, sb)
            for name in plan:
                for eng in (nc.tensor, nc.vector, nc.scalar, nc.sync, nc.gpsimd):
                    p.fence(eng)
                if name in ("moe0", "moe1"):
                    emit_moe_sublayer(p, sb, ps, io, name)
                elif name == "attn0":
                    emit_attn0(p, sb, ps, io, ctx)
                elif name in ("ffn1a", "ffn1b"):
                    emit_ffn1(p, sb, ps, io, name)
                elif name == "attn1":
                    emit_attn1(p, sb, ps, io, ctx)
                else:
                    raise NotImplementedError(name)
            emit_postamble(p, io, sb)
    return nc, io



def _host_constants(core):
    iota = np.arange(8, dtype=np.float32)
    sel8 = np.zeros((8, 8 * 128), np.float32)
    for e in range(8):
        sel8[e, e * 128:(e + 1) * 128] = 1.0
    selH = np.zeros((128, 32), np.float32)
    selT = np.zeros((8, 4 * 128), np.float32)
    for c in range(4):
        for pp in range(128):
            h = 2 * c + pp // 64
            selH[pp, c * 8 + h] = 1.0
            selT[h, c * 128 + pp] = 1.0
    selHr = np.zeros((128, 16), np.float32)
    for c in range(2):
        for pp in range(128):
            h = 4 * c + pp // 32
            selHr[pp, c * 8 + h] = 1.0
    qp, kp = np.meshgrid(np.arange(128), np.arange(128), indexing="xy")
    trimask = np.where(kp <= qp, 0.0, NEGINF).astype(np.float32)
    return {
        "par": np.array([[core % 2]], np.int32),
        "ones1": np.ones((128, 1), np.float32),
        "ones_row": np.ones((1, 128), np.float32),
        "eye_f": np.eye(128, dtype=np.float32),
        "iotaB": np.broadcast_to((iota - BIG)[None, :], (128, 8)).copy(),
        "sel8": sel8,
        "selH": selH,
        "selT": selT,
        "selHr": selHr,
        "trimask": trimask,
    }


def kernel(**inputs):
    plan_env = os.environ.get("MK_PLAN", "fused")
    if plan_env == "unfused":
        plans = [[sl] for sl in SUBLAYERS]
    elif plan_env == "fused":
        plans = [SUBLAYERS]
    elif plan_env.startswith("pfx:"):
        k = int(plan_env.split(":")[1])
        plans = [[sl] for sl in SUBLAYERS[:k]]
    else:
        plans = [[sl] for sl in plan_env.split(",")]

    x = np.asarray(inputs["L0_attn_x"])
    B = x.shape[0]
    ncores = 8
    xs = [np.ascontiguousarray(x[c // 2, (c % 2) * TOK:(c % 2 + 1) * TOK, :].T)
          for c in range(ncores)]
    consts = [_host_constants(c) for c in range(ncores)]
    moe_w = {
        "gate_wT": np.ascontiguousarray(np.asarray(inputs["L0_ffn_gate_w"]).T),
        "w1": np.ascontiguousarray(np.asarray(inputs["L0_ffn_w1"])),
        "w2": np.ascontiguousarray(np.asarray(inputs["L0_ffn_w2"])),
        "w3": np.ascontiguousarray(np.asarray(inputs["L0_ffn_w3"])),
    }
    attn0_w = []
    for c in range(ncores):
        h0 = (c % 2) * 8
        rs = slice(h0 * 64, (h0 + 8) * 64)
        attn0_w.append({
            "l0_wq": np.ascontiguousarray(np.asarray(inputs["L0_attn_Wq"])[rs, :].T),
            "l0_wk": np.ascontiguousarray(np.asarray(inputs["L0_attn_Wk"])[rs, :].T),
            "l0_wv": np.ascontiguousarray(np.asarray(inputs["L0_attn_Wv"])[rs, :].T),
            "l0_wo": np.ascontiguousarray(np.asarray(inputs["L0_attn_Wo"])[:, rs].T),
            "qln_g": np.ascontiguousarray(np.tile(np.asarray(inputs["L0_attn_qln_g"]), 2)[:, None]),
            "qln_b": np.ascontiguousarray(np.tile(np.asarray(inputs["L0_attn_qln_b"]), 2)[:, None]),
            "kln_g": np.ascontiguousarray(np.tile(np.asarray(inputs["L0_attn_kln_g"]), 2)[:, None]),
            "kln_b": np.ascontiguousarray(np.tile(np.asarray(inputs["L0_attn_kln_b"]), 2)[:, None]),
            "l0_bo_cols": np.ascontiguousarray(np.asarray(inputs["L0_attn_bo"]).reshape(8, 128).T),
        })

    def _rope_tables():
        half = 16
        inv = (1.0 / (np.float32(10000.0) ** (np.arange(half, dtype=np.float32)
                                              / np.float32(half)))).astype(np.float32)
        pos = np.arange(T, dtype=np.float32)
        ang = (pos[:, None] * inv[None, :]).astype(np.float32)
        cos = np.cos(ang).astype(np.float32)
        sin = np.sin(ang).astype(np.float32)
        cos128 = np.empty((128, T), np.float32)
        sin128 = np.empty((128, T), np.float32)
        for j in range(128):
            b = j % 32
            if b < half:
                cos128[j] = cos[:, b]
                sin128[j] = -sin[:, b]
            else:
                cos128[j] = cos[:, b - half]
                sin128[j] = sin[:, b - half]
        return cos128, sin128

    ropecos, ropesin = _rope_tables()

    def _swap_qr(w, dh):
        w = w.reshape(w.shape[0], -1, dh)
        return np.concatenate([w[:, :, dh // 2:], w[:, :, :dh // 2]], axis=2).reshape(
            w.shape[0], -1)

    attn1_w = []
    for c in range(ncores):
        h0 = (c % 2) * 8
        wq = np.asarray(inputs["L1_attn_Wq"])[:, h0 * 64:(h0 + 8) * 64]
        wqr_all = np.asarray(inputs["L1_attn_Wqr"])
        wqr = wqr_all[:, h0 * 32:(h0 + 8) * 32]
        wqr_cat = np.concatenate([wqr, _swap_qr(wqr, 32)], axis=1)
        wkr = np.asarray(inputs["L1_attn_Wkr"])
        wkr_cat = np.concatenate([wkr, _swap_qr(wkr, 32)], axis=1)
        attn1_w.append({
            "l1_wq": np.ascontiguousarray(wq),
            "l1_wqr_cat": np.ascontiguousarray(wqr_cat),
            "l1_wkvd": np.ascontiguousarray(np.asarray(inputs["L1_attn_Wkvd"])),
            "l1_wkr_cat": np.ascontiguousarray(wkr_cat),
            "l1_wku": np.ascontiguousarray(np.asarray(inputs["L1_attn_Wku"])[:, h0 * 64:(h0 + 8) * 64]),
            "l1_wvu": np.ascontiguousarray(np.asarray(inputs["L1_attn_Wvu"])[:, h0 * 64:(h0 + 8) * 64]),
            "l1_wo": np.ascontiguousarray(np.asarray(inputs["L1_attn_Wo"])[h0 * 64:(h0 + 8) * 64, :]),
            "lobo": np.ascontiguousarray(np.asarray(inputs["L1_attn_lobo_log"])[h0:h0 + 8][None, :]),
            "ropecos": ropecos,
            "ropesin": ropesin,
        })

    ffn1_w = {
        "l1fw": np.ascontiguousarray(np.asarray(inputs["L1_ffn_weight"])),
        "l1fb_cols": np.ascontiguousarray(np.asarray(inputs["L1_ffn_bias"]).reshape(8, 128).T),
    }
    rng = np.random.RandomState()
    for plan in plans:
        nc, _io = build_program(plan)
        nonces = rng.randint(1, 2 ** 30, size=(8, 1)).astype(np.int32)
        in_maps = []
        for c in range(ncores):
            m = {"x_in": xs[c], "nonces": nonces}
            m.update(consts[c])
            if any(sl in plan for sl in ("moe0", "moe1")):
                m.update(moe_w)
            if "attn0" in plan:
                m.update(attn0_w[c])
            if "ffn1a" in plan or "ffn1b" in plan:
                m.update(ffn1_w)
            if "attn1" in plan:
                m.update(attn1_w[c])
            in_maps.append(m)
        res = run_bass_kernel_spmd(nc, in_maps, core_ids=list(range(ncores)))
        xs = [np.asarray(res.results[c]["x_out"]) for c in range(ncores)]

    out = np.empty((B, T, D), np.float32)
    for c in range(ncores):
        out[c // 2, (c % 2) * TOK:(c % 2 + 1) * TOK, :] = xs[c].T
    return out
